# version 06
# baseline (speedup 1.0000x reference)
import contextlib
NS_ITERS = 6
import numpy as np
import ml_dtypes
import concourse.bass as bass
import concourse.mybir as mybir
from concourse.bass_utils import run_bass_kernel_spmd

F32 = mybir.dt.float32
F16 = mybir.dt.float16
BF16 = mybir.dt.bfloat16
U16 = mybir.dt.uint16
AF = mybir.ActivationFunctionType
ALU = mybir.AluOpType
AX = mybir.AxisListType
bfl = ml_dtypes.bfloat16

NCORES = 8
TPC = 2048
NT = 16
D = 1024
IF = 2816
NH, HD = 16, 64
ROUNDS = [(0, 512), (512, 1024), (1024, 1536), (1536, 2048), (2048, 2560), (2560, 2816)]
WSEG_G = 128 * IF
WSEG_U = 128 * IF
WSEG_D = 352 * D
WSEG = WSEG_G + WSEG_U + WSEG_D
PSEG = 128 * 520

MIDS = [0.25, 0.75, 1.25, 1.75, 2.5, 3.5, 5.0]
DELT = [0.5, 0.5, 0.5, 0.5, 1.0, 1.0, 2.0]
C_ABS16, C_MAX16, C_ADD16, C_MSK16 = 0x7FFF, 15360.0, 255.0, 0xFE00
C_T1, C_W1 = 13312.0, 14336.0
C_T2, C_W2 = 14848.0, 1024.0
C_SUB = 15360.0


class Sem:
    def __init__(self, es, nc, name):
        self.h = es.enter_context(nc.semaphore(name))
        self.v = 0


def build(stage="full"):
    nc = bass.Bass("TRN2", target_bir_lowering=False)
    ap = lambda name, shp, dt, kind: nc.dram_tensor(name, shp, dt, kind=kind).ap()
    X = ap("x", [TPC, D], F32, "ExternalInput")
    WQKV = ap("wqkv", [D, 3 * D], F32, "ExternalInput")
    WPROJ = ap("wproj", [D, D], F32, "ExternalInput")
    WGSL = ap("wg_sl", [IF, 128], F32, "ExternalInput")
    WUSL = ap("wu_sl", [IF, 128], F32, "ExternalInput")
    WDSL = ap("wd_sl", [D, 352], F32, "ExternalInput")
    EYE = ap("eye", [128, 128], BF16, "ExternalInput")
    EYEB = ap("eyeb", [128, 512], F32, "ExternalInput")
    ONES1 = ap("ones1", [128, 1], BF16, "ExternalInput")
    ONES2 = ap("ones2", [128, 2], BF16, "ExternalInput")
    E2T = ap("e2t", [2, 128], BF16, "ExternalInput")
    C80 = ap("c80", [128, 1], U16, "ExternalInput")
    ONESR = ap("onesr", [1, 128], BF16, "ExternalInput")
    ETBL = ap("etbl", [128, 256], BF16, "ExternalInput")
    OUT = ap("out", [TPC, D], F32, "ExternalOutput")
    if stage == "wq":
        DBG = ap("dbg", [NCORES * WSEG], BF16, "ExternalOutput")
    CIN_W = ap("cin_w", [WSEG], BF16, "Internal")
    COUT_W = ap("cout_w", [NCORES, WSEG], BF16, "Internal")
    CIN_Y = ap("cin_y", [16 * D], BF16, "Internal")
    COUT_Y = ap("cout_y", [4, 16 * D], BF16, "Internal")
    CIN_P = ap("cin_p", [PSEG], F32, "Internal")
    COUT_P = ap("cout_p", [4, PSEG], F32, "Internal")
    GROUPS4 = [[0, 1, 2, 3], [4, 5, 6, 7]]
    CG_V = CIN_W[0:WSEG_G].rearrange("(p c) -> p c", p=128)
    CU_V = CIN_W[WSEG_G:WSEG_G + WSEG_U].rearrange("(p c) -> p c", p=128)
    CD_V = CIN_W[WSEG_G + WSEG_U:WSEG].rearrange("(r h) -> r h", h=D)

    with contextlib.ExitStack() as es:
        E = es.enter_context
        sb = lambda name, shp, dt: E(nc.sbuf_tensor(name, shp, dt))
        STREAM = sb("stream", [128, NT * D], BF16)
        FMA = sb("fma", [128, 8 * TPC], BF16)
        WA = sb("wa", [128, 24576], BF16)
        C80T = sb("c80t", [128, 1], U16)
        EYET = sb("eyet", [128, 128], BF16)
        ONESRT = sb("onesrt", [1, 128], BF16)
        ETBLT = sb("etblt", [128, 256], BF16)
        O2T = sb("o2t", [128, 2], BF16)
        E2TTT = sb("e2ttt", [2, 128], BF16)
        EYBT = sb("eybt", [128, 512], F32)
        ONES1T = sb("ones1t", [128, 1], BF16)
        AR32 = sb("ar32", [128, 9600], F32)
        AR16 = sb("ar16", [128, 27200], U16)
        _o32 = [0]
        _o16 = [0]
        def av32(pcount, cols, off=None):
            if off is None:
                off = _o32[0]; _o32[0] = off + cols
            return AR32.ap()[0:pcount, off:off + cols]
        def av16(pcount, cols, dt, off=None):
            if off is None:
                off = _o16[0]; _o16[0] = off + cols
            u = AR16.ap()[0:pcount, off:off + cols]
            return u.bitcast(dt) if dt != U16 else u
        BN6 = av32(128, 12 * NT)
        BN2 = av32(128, 2 * NT)
        MNEG = av32(128, NT)
        RSTD = av32(128, NT)
        TMPV = av32(128, NT)
        SIL = [av32(128, 512), av32(128, 512)]
        H16 = [av32(128, 512), av32(128, 512)]
        XOUT_OFF32 = _o32[0]
        XOUT = av32(128, 1024)
        D16 = av32(128, 1024, off=XOUT_OFF32)
        QMX = av32(128, 64)
        QRC = av32(128, 64)
        QRS = av32(128, 64)
        QSC = av32(128, 64)
        OVL32 = _o32[0]
        WP = av32(128, 1408)
        QX = av32(128, 704)
        QC = av32(128, 704)
        QJ = av32(128, 704)
        QS = av32(128, 704)
        WMX = av32(128, IF // 16)
        RCP = av32(128, IF // 16)
        RS_ = av32(128, IF // 16)
        SC_ = av32(128, IF // 16)
        _o32[0] = OVL32
        E2SB_OFF32 = _o32[0]
        E2SB = av32(128, 512)
        SM8 = av32(128, 8)
        WSB_OFF32 = _o32[0]
        WSB = av32(128, 512)
        RDEN = av32(1, 16)
        RD128 = av32(128, 16)
        RDE = av32(128, 8)
        RDO = av32(128, 8)
        MRKV = av32(128, 1)
        MRKS = av32(128, 2, off=None)
        P_ACC = av32(128, 1024)
        SLOC4 = av32(128, 32)
        SLOC = av32(128, 8)
        RCPS = av32(128, 8)
        PCMP = av32(128, 520)
        PRB = av32(128, 520)
        DSB = av32(2, 512, off=E2SB_OFF32)
        DSR = av32(2, 512, off=WSB_OFF32)
        XN16 = av16(128, 1024, F16)
        TU1 = av16(128, 1024, U16)
        TU2 = av16(128, 1024, U16)
        TU3 = av16(128, 1024, U16)
        XQTM = av16(128, 1024, BF16)
        QH = [av16(128, 512, BF16), av16(128, 512, BF16)]
        QHT = [av16(128, 512, BF16), av16(128, 512, BF16)]
        OVL16 = _o16[0]
        WQT = av16(128, 3072, BF16)
        WCT = av16(128, 3072, BF16)
        _o16[0] = OVL16
        LNT_OFF = _o16[0]
        LNT = av16(128, 1024, BF16)
        LNF_OFF = _o16[0]
        LNF = av16(128, 8 * 512, BF16)
        YLM_OFF = _o16[0]
        YLM = av16(16, 1024, BF16)
        YLMF = av16(64, 1024, BF16)
        YLMT = av16(128, 512, BF16)
        QLMF = av16(128, 512, BF16)
        KLMF = av16(128, 512, BF16)
        K2N = av16(128, 512, BF16)
        K2ND = av16(128, 512, BF16)
        KTP = av16(64, 1024, BF16)
        VSB = av16(64, 1024, BF16)
        RDENB = av16(1, 16, BF16)
        KFM = av16(128, 8 * 512, BF16)
        SVF = KFM
        VTMS = av16(128, 4 * 1024, BF16)
        E3P = av16(128, 512, BF16, off=YLM_OFF)
        E3TS = av16(128, 512, BF16, off=YLM_OFF + 512)
        K3V = av16(128, 512, BF16, off=LNT_OFF)
        TSB = av16(128, 512, BF16, off=YLM_OFF + 1024)
        VTS = av16(128, 512, BF16, off=LNF_OFF)
        R1SB = av16(128, 512, BF16, off=LNF_OFF + 512)
        R1TP = av16(128, 512, BF16, off=LNF_OFF + 1024)
        R2SB = av16(128, 512, BF16, off=LNF_OFF + 1536)
        R3SB = av16(128, 512, BF16, off=LNF_OFF + 2048)
        E1P = av16(128, 512, BF16, off=LNF_OFF + 2560)
        RCDB = av16(2, 512, BF16, off=LNF_OFF + 3072)
        RSB = av16(128, 512, BF16, off=LNF_OFF + 3584)
        PGA = E(nc.psum_tensor("pga", [128, 1024], F32))
        PGB = E(nc.psum_tensor("pgb", [128, 1024], F32))
        PDN = E(nc.psum_tensor("pdn", [128, 1024], F32))
        PT = E(nc.psum_tensor("pt", [128, 512], BF16))
        PS = E(nc.psum_tensor("ps", [128, 512], F32))

        sd = Sem(es, nc, "sd")
        sv = Sem(es, nc, "sv")
        ss = Sem(es, nc, "ss")
        st = Sem(es, nc, "st")
        sg = Sem(es, nc, "sg")
        sfr = Sem(es, nc, "sfr")
        so = Sem(es, nc, "so")
        sx = Sem(es, nc, "sx")
        fs = Sem(es, nc, "fs")
        fg = Sem(es, nc, "fg")
        fu = Sem(es, nc, "fu")
        fd = Sem(es, nc, "fd")
        ft = Sem(es, nc, "ft")

        blk = E(nc.Block())

        @blk.sync
        def _(sync):
            sync.dma_start(out=C80T[:], in_=C80).then_inc(sd.h, 16)
            sync.dma_start(out=EYET[:], in_=EYE).then_inc(sd.h, 16)
            sync.dma_start(out=ONESRT[:], in_=ONESR).then_inc(sd.h, 16)
            sync.dma_start(out=ETBLT[:], in_=ETBL).then_inc(sd.h, 16)
            sync.dma_start(out=O2T[:], in_=ONES2).then_inc(sd.h, 16)
            sync.dma_start(out=E2TTT[:], in_=E2T).then_inc(sd.h, 16)
            sync.dma_start(out=EYBT[:], in_=EYEB).then_inc(sd.h, 16)
            sync.dma_start(out=ONES1T[:], in_=ONES1).then_inc(sd.h, 16)
        sd.v += 128
        x_ld = sx.v + 16
        @blk.gpsimd
        def _(gp):
            gp.dma_start(
                out=STREAM[:].rearrange("p (t f) -> p t f", f=D),
                in_=X.rearrange("(t p) f -> p t f", p=128)).then_inc(sx.h, 16)
        sx.v = x_ld

        def wprep(src3d, inner_w, nchunks, tr_specs, stage_fn, wp_free_sv):
            wqt_free_st = st.v
            prev_xn = wp_free_sv
            nh0 = nchunks // 2
            for half in (0, 1):
                nch = (nh0, nchunks - nh0)[half]
                ncols = nch * inner_w
                cb = nh0 * inner_w if half else 0
                ch0 = nh0 if half else 0
                ld = sd.v + 16
                @blk.sync
                def _(sync, ld=ld, ncols=ncols, ch0=ch0, nch=nch,
                      prev_xn=prev_xn):
                    sync.wait_ge(sv.h, prev_xn)
                    sync.dma_start(
                        out=WP[:, 0:ncols].rearrange("p (c f) -> p c f",
                                                     f=inner_w),
                        in_=src3d[:, ch0:ch0 + nch, :]).then_inc(sd.h, 16)
                sd.v = ld
                xn_t = sv.v + 1
                last = (half == 1)
                @blk.vector
                def _(v, ld=ld, ncols=ncols, cb=cb, last=last,
                      wqt_free_st=wqt_free_st, first=(half == 0)):
                    v.wait_ge(sd.h, ld)
                    if first:
                        v.wait_ge(st.h, wqt_free_st)
                    for h0 in range(0, ncols, 704):
                        hw = min(704, ncols - h0)
                        hb = hw // 16
                        b0 = (cb + h0) // 16
                        s3 = WP[:, h0:h0 + hw].rearrange("p (nb w) -> p nb w", w=16)
                        v.tensor_reduce(out=WMX[:, b0:b0 + hb], in_=s3, axis=AX.X,
                                        op=ALU.max, apply_absolute_value=True)
                        v.drain()
                        v.reciprocal(out=RCP[:, b0:b0 + hb], in_=WMX[:, b0:b0 + hb])
                        v.drain()
                        v.tensor_scalar(out=RS_[:, b0:b0 + hb], in0=RCP[:, b0:b0 + hb],
                                        scalar1=6.0, scalar2=None, op0=ALU.mult)
                        v.tensor_scalar(out=SC_[:, b0:b0 + hb], in0=WMX[:, b0:b0 + hb],
                                        scalar1=1.0 / 6.0, scalar2=None, op0=ALU.mult)
                        i = v.tensor_tensor(
                            out=QX[:, 0:hw].rearrange("p (nb w) -> p nb w", w=16),
                            in0=s3,
                            in1=RS_[:, b0:b0 + hb].unsqueeze(2).broadcast_to([128, hb, 16]),
                            op=ALU.mult)
                        if h0 + hw >= ncols:
                            i.then_inc(sv.h, 1)
                        xn = QX[:, 0:hw]
                        v.tensor_scalar(out=QS[:, 0:hw].bitcast(mybir.dt.int32),
                                        in0=xn.bitcast(mybir.dt.int32),
                                        scalar1=0x7FFFFFFF, scalar2=None,
                                        op0=ALU.bitwise_and)
                        axn = QS[:, 0:hw]
                        v.tensor_scalar(out=QC[:, 0:hw], in0=axn, scalar1=MIDS[0],
                                        scalar2=DELT[0], op0=ALU.is_gt, op1=ALU.mult)
                        for j in range(1, 7):
                            v.tensor_scalar(out=QJ[:, 0:hw], in0=axn, scalar1=MIDS[j],
                                            scalar2=DELT[j], op0=ALU.is_gt, op1=ALU.mult)
                            v.tensor_tensor(out=QC[:, 0:hw], in0=QC[:, 0:hw],
                                            in1=QJ[:, 0:hw], op=ALU.add)
                        v.scalar_tensor_tensor(out=QJ[:, 0:hw], in0=xn, scalar=0.0,
                                               in1=QC[:, 0:hw], op0=ALU.is_gt,
                                               op1=ALU.mult)
                        v.scalar_tensor_tensor(out=QC[:, 0:hw], in0=QJ[:, 0:hw],
                                               scalar=2.0, in1=QC[:, 0:hw],
                                               op0=ALU.mult, op1=ALU.subtract)
                        v.tensor_tensor(
                            out=WQT[:, cb + h0:cb + h0 + hw]
                            .rearrange("p (nb w) -> p nb w", w=16),
                            in0=QC[:, 0:hw].rearrange("p (nb w) -> p nb w", w=16),
                            in1=SC_[:, b0:b0 + hb].unsqueeze(2).broadcast_to([128, hb, 16]),
                            op=ALU.mult)
                    if last:
                        v.tensor_scalar(out=MRKV[:], in0=MRKV[:], scalar1=1.0, scalar2=None, op0=ALU.mult).then_inc(sv.h, 1)
                sv.v = xn_t + (1 if last else 0)
                prev_xn = xn_t
            q_t = sv.v
            wct_free_so = so.v
            for g0 in range(0, len(tr_specs), 4):
                grp = tr_specs[g0:g0 + 4]
                t_t = st.v + 1
                f_base = sfr.v
                @blk.tensor
                def _(t, grp=grp, q_t=q_t, f_base=f_base):
                    t.wait_ge(sv.h, q_t)
                    t.wait_ge(sfr.h, f_base)
                    i = None
                    for j, (c0, wdt, _d) in enumerate(grp):
                        i = t.transpose(out=PT[0:wdt, 128 * j:128 * j + 128],
                                        in_=WQT[:, c0:c0 + wdt],
                                        identity=EYET[:])
                    i.then_inc(st.h, 1)
                st.v = t_t
                e_t = sfr.v + 1
                @blk.scalar
                def _(s, grp=grp, t_t=t_t, wct_free_so=wct_free_so):
                    s.wait_ge(st.h, t_t)
                    s.wait_ge(so.h, wct_free_so)
                    i = None
                    for j, (c0, wdt, d0) in enumerate(grp):
                        i = s.copy(out=WCT[0:wdt, d0:d0 + 128],
                                   in_=PT[0:wdt, 128 * j:128 * j + 128])
                    i.then_inc(sfr.h, 1)
                sfr.v = e_t
            fr_done = sfr.v
            stage_fn(fr_done)
            return prev_xn

        gate_specs = [(128 * c, 128, 128 * c) for c in range(22)]
        def stage_gate(fr):
            t = so.v + 16
            @blk.gpsimd
            def _(gp, fr=fr):
                gp.wait_ge(sfr.h, fr)
                gp.dma_start(out=CG_V, in_=WCT[:, 0:IF]).then_inc(so.h, 16)
            so.v = t
        g_xn = wprep(WGSL.rearrange("(c p) f -> p c f", p=128), 128, 22,
                     gate_specs, stage_gate, 0)

        def stage_up(fr):
            t = so.v + 16
            @blk.gpsimd
            def _(gp, fr=fr):
                gp.wait_ge(sfr.h, fr)
                gp.dma_start(out=CU_V, in_=WCT[:, 0:IF]).then_inc(so.h, 16)
            so.v = t
        u_xn = wprep(WUSL.rearrange("(c p) f -> p c f", p=128), 128, 22,
                     up_specs := [(128 * c, 128, 128 * c) for c in range(22)],
                     stage_up, g_xn)

        dn_specs = []
        dn_pieces = []
        pc = 0
        for hch in range(8):
            for ic in range(3):
                wdt = 128 if ic < 2 else 96
                dn_specs.append((352 * hch + 128 * ic, wdt, 128 * pc))
                dn_pieces.append((128 * pc, wdt, 128 * ic, 128 * hch))
                pc += 1
        def stage_dn(fr):
            t = so.v + 16 * len(dn_pieces)
            @blk.gpsimd
            def _(gp, fr=fr):
                gp.wait_ge(sfr.h, fr)
                for (wc0, rows, r0, h0) in dn_pieces:
                    gp.dma_start(out=CD_V[r0:r0 + rows, h0:h0 + 128],
                                 in_=WCT[0:rows, wc0:wc0 + 128]).then_inc(so.h, 16)
            so.v = t
        d_xn = wprep(WDSL.rearrange("(c p) f -> p c f", p=128), 352, 8,
                     dn_specs, stage_dn, u_xn)

        ag_t = sg.v + 1
        agw_stage_done = so.v
        @blk.gpsimd
        def _(gp, agw=agw_stage_done):
            gp.wait_ge(so.h, agw)
            gp.collective_compute(
                "AllGather", ALU.bypass, ins=[CIN_W], outs=[COUT_W],
                replica_groups=[[0, 1, 2, 3, 4, 5, 6, 7]],
            ).then_inc(sg.h, 1)
        sg.v = ag_t

        if stage == "wq":
            @blk.sync
            def _(sync, ag_t=ag_t):
                sync.wait_ge(sg.h, ag_t)
                sync.dma_start(out=DBG, in_=COUT_W.flatten()).then_inc(so.h, 16)
                sync.wait_ge(so.h, so.v + 16)
            so.v += 16
            return nc

        def GV(r):
            return COUT_W[r, 0:WSEG_G].rearrange("(p c) -> p c", p=128)
        def UV(r):
            return COUT_W[r, WSEG_G:WSEG_G + WSEG_U].rearrange("(p c) -> p c", p=128)

        def q16(v, src_ap, dst_ap, ncols, sc2, rstd_ap=None):
            nb = ncols // 16
            s3 = src_ap.rearrange("p (nb w) -> p nb w", w=16)
            v.tensor_reduce(out=QMX[:, 0:nb], in_=s3, axis=AX.X, op=ALU.max,
                            apply_absolute_value=True)
            v.drain()
            v.reciprocal(out=QRC[:, 0:nb], in_=QMX[:, 0:nb])
            v.drain()
            v.tensor_scalar(out=QRS[:, 0:nb], in0=QRC[:, 0:nb], scalar1=6.0,
                            scalar2=None, op0=ALU.mult)
            if rstd_ap is not None:
                v.tensor_scalar(out=QSC[:, 0:nb], in0=QMX[:, 0:nb],
                                scalar1=rstd_ap, scalar2=sc2,
                                op0=ALU.mult, op1=ALU.mult)
            else:
                v.tensor_scalar(out=QSC[:, 0:nb], in0=QMX[:, 0:nb], scalar1=sc2,
                                scalar2=None, op0=ALU.mult)
            v.tensor_tensor(
                out=XN16[:, 0:ncols].rearrange("p (nb w) -> p nb w", w=16),
                in0=s3,
                in1=QRS[:, 0:nb].unsqueeze(2).broadcast_to([128, nb, 16]),
                op=ALU.mult)
            xb = XN16[:, 0:ncols].bitcast(U16)
            t1, t2, t3 = TU1[:, 0:ncols], TU2[:, 0:ncols], TU3[:, 0:ncols]
            v.tensor_scalar(out=t1, in0=xb, scalar1=C_ABS16, scalar2=None,
                            op0=ALU.bitwise_and)
            v.tensor_scalar(out=t2, in0=t1, scalar1=C_MAX16, scalar2=C_ADD16,
                            op0=ALU.max, op1=ALU.add)
            v.tensor_scalar(out=t2, in0=t2, scalar1=C_MSK16, scalar2=None,
                            op0=ALU.bitwise_and)
            v.tensor_scalar(out=t3, in0=t1, scalar1=C_T2, scalar2=C_W2,
                            op0=ALU.is_gt, op1=ALU.mult)
            v.tensor_scalar(out=t1, in0=t1, scalar1=C_T1, scalar2=C_W1,
                            op0=ALU.is_gt, op1=ALU.mult)
            v.tensor_tensor(out=t1, in0=t1, in1=t3, op=ALU.add)
            v.scalar_tensor_tensor(out=t1, in0=t2, scalar=C_SUB, in1=t1,
                                   op0=ALU.subtract, op1=ALU.add)
            v.scalar_tensor_tensor(out=t1, in0=xb, scalar=C80T[:], in1=t1,
                                   op0=ALU.bitwise_and, op1=ALU.bitwise_or)
            return v.tensor_tensor(
                out=dst_ap.rearrange("p (nb w) -> p nb w", w=16),
                in0=t1.bitcast(F16).rearrange("p (nb w) -> p nb w", w=16),
                in1=QSC[:, 0:nb].unsqueeze(2).broadcast_to([128, nb, 16]),
                op=ALU.mult)

        def make_ln_group(ready_sems):
            def emit_group(g, extra=()):
                bn_t = sv.v + 1
                @blk.vector
                def _(v, g=g, rd=list(ready_sems) + list(extra)):
                    for s_, val in rd:
                        v.wait_ge(s_.h, val)
                    for t in range(4 * g, 4 * g + 4):
                        v.bn_stats(out=BN6[:, 12 * t:12 * t + 6],
                                   in_=STREAM[:, 1024 * t:1024 * t + 512])
                        v.bn_stats(out=BN6[:, 12 * t + 6:12 * t + 12],
                                   in_=STREAM[:, 1024 * t + 512:1024 * t + 1024])
                    v.drain()
                    for t in range(4 * g, 4 * g + 4):
                        v.bn_aggr(out=BN2[:, 2 * t:2 * t + 2],
                                  in_=BN6[:, 12 * t:12 * t + 12])
                    v.drain()
                    b2 = BN2[:].rearrange("p (t two) -> p t two", two=2)
                    v.tensor_scalar(out=MNEG[:, 4 * g:4 * g + 4],
                                    in0=b2[:, 4 * g:4 * g + 4, 0], scalar1=-1.0,
                                    scalar2=None, op0=ALU.mult)
                    v.tensor_scalar(out=TMPV[:, 4 * g:4 * g + 4],
                                    in0=b2[:, 4 * g:4 * g + 4, 1], scalar1=1e-5,
                                    scalar2=None, op0=ALU.add)
                    v.drain()
                    v.reciprocal(out=TMPV[:, 4 * g:4 * g + 4],
                                 in_=TMPV[:, 4 * g:4 * g + 4])
                    v.drain()
                    v.tensor_scalar(out=TMPV[:, 4 * g:4 * g + 4],
                                    in0=TMPV[:, 4 * g:4 * g + 4], scalar1=1.0,
                                    scalar2=None, op0=ALU.mult).then_inc(sv.h, 1)
                sv.v = bn_t
                sq_t = ss.v + 1
                @blk.scalar
                def _(s, g=g, bn_t=bn_t):
                    s.wait_ge(sv.h, bn_t)
                    s.activation(out=RSTD[:, 4 * g:4 * g + 4],
                                 in_=TMPV[:, 4 * g:4 * g + 4],
                                 func=AF.Sqrt).then_inc(ss.h, 1)
                ss.v = sq_t
                for t in range(4 * g, 4 * g + 4):
                    q_t = sv.v + 1
                    prev_tr = st.v
                    @blk.vector
                    def _(v, t=t, sq_t=sq_t, prev_tr=prev_tr):
                        v.wait_ge(ss.h, sq_t)
                        if t > 0:
                            v.wait_ge(st.h, prev_tr)
                        v.tensor_scalar(out=D16[:], in0=STREAM[:, 1024 * t:1024 * (t + 1)],
                                        scalar1=MNEG[:, t:t + 1], scalar2=None,
                                        op0=ALU.add)
                        q16(v, D16[:], XQTM[:], 1024, 1.0 / 6.0,
                            rstd_ap=RSTD[:, t:t + 1]).then_inc(sv.h, 1)
                    sv.v = q_t
                    for gg in range(2):
                        t_t = st.v + 1
                        ftw = ft.v
                        @blk.tensor
                        def _(tt, t=t, g=gg, q_t=q_t, ftw=ftw):
                            tt.wait_ge(sv.h, q_t)
                            tt.wait_ge(ft.h, ftw)
                            i = None
                            for j in range(4):
                                c = 4 * g + j
                                i = tt.transpose(out=PT[:, 128 * j:128 * (j + 1)],
                                                 in_=XQTM[:, 128 * c:128 * (c + 1)],
                                                 identity=EYET[:])
                            i.then_inc(st.h, 1)
                        st.v = t_t
                        e_t = ft.v + 1
                        @blk.vector
                        def _(v, t=t, g=gg, t_t=t_t):
                            v.wait_ge(st.h, t_t)
                            dst = FMA[:].rearrange("p (k r) -> p k r", r=2048) \
                                [:, 4 * g:4 * g + 4, 128 * t:128 * (t + 1)]
                            v.tensor_copy(dst, PT[:].rearrange(
                                "p (k r) -> p k r", r=128)).then_inc(ft.h, 1)
                        ft.v = e_t
            return emit_group

        def emit_ln_xq(ready_sems):
            eg = make_ln_group(ready_sems)
            for g in range(4):
                eg(g)
            return ft.v

        def emit_ffn(xq_ft, wa_free_st, write_out):
            round_done = [wa_free_st, wa_free_st]
            par_free = {0: fg, 1: fu}
            par_psum = {0: PGA, 1: PGB}
            for r, (o0, o1) in enumerate(ROUNDS):
                Wr = o1 - o0
                ch = Wr // 128
                wb = 12288 * (r % 2)
                @blk.sync
                def _(sync, o0=o0, o1=o1, Wr=Wr, ch=ch, wb=wb,
                      wa_free=round_done[r % 2], ag_t=ag_t):
                    sync.wait_ge(sg.h, ag_t)
                    sync.wait_ge(st.h, wa_free)
                    for k in range(8):
                        sync.dma_start(out=WA[:, wb + 512 * k:wb + 512 * k + Wr],
                                       in_=GV(k)[:, o0:o1]).then_inc(sd.h, 16)
                        sync.dma_start(out=WA[:, wb + 4096 + 512 * k:wb + 4096 + 512 * k + Wr],
                                       in_=UV(k)[:, o0:o1]).then_inc(sd.h, 16)
                    for j in range(ch):
                        g0 = o0 + 128 * j
                        left = 128
                        while left > 0:
                            rk = g0 // 352
                            loc = g0 - 352 * rk
                            n = min(left, 352 - loc)
                            srcw = COUT_W[rk, WSEG_G + WSEG_U + loc * D:
                                          WSEG_G + WSEG_U + (loc + n) * D] \
                                .rearrange("(q h) -> q h", h=D)
                            p0 = 128 - left
                            sync.dma_start(
                                out=WA[p0:p0 + n, wb + 8192 + 1024 * j:wb + 8192 + 1024 * (j + 1)],
                                in_=srcw).then_inc(sd.h, 16)
                            g0 += n
                            left -= n
                ndma = 16
                for j in range(ch):
                    g0 = o0 + 128 * j
                    left = 128
                    while left > 0:
                        rk = g0 // 352
                        n = min(left, 352 - (g0 - 352 * rk))
                        g0 += n
                        left -= n
                        ndma += 1
                sd.v += 16 * ndma
                wload = sd.v
                def emit_gateup(t):
                    par = t % 2
                    PP = par_psum[par]
                    fsem = par_free[par]
                    mm_t = st.v + 1
                    fw = fsem.v
                    @blk.tensor
                    def _(tb, t=t, PP=PP, fsem=fsem, fw=fw, Wr=Wr, wb=wb,
                          wload=wload):
                        tb.wait_ge(sd.h, wload)
                        tb.wait_ge(ft.h, xq_ft)
                        tb.wait_ge(fsem.h, fw)
                        i = None
                        for k in range(8):
                            lhs = FMA[:, 2048 * k + 128 * t:2048 * k + 128 * (t + 1)]
                            i = tb.matmul(out=PP[:, 0:Wr], lhsT=lhs,
                                          rhs=WA[:, wb + 512 * k:wb + 512 * k + Wr],
                                          start=(k == 0), stop=(k == 7))
                            i = tb.matmul(out=PP[:, 512:512 + Wr], lhsT=lhs,
                                          rhs=WA[:, wb + 4096 + 512 * k:wb + 4096 + 512 * k + Wr],
                                          start=(k == 0), stop=(k == 7))
                        i.then_inc(st.h, 1)
                    st.v = mm_t
                    return mm_t

                def emit_silu_h(t, mm_t):
                    par = t % 2
                    PP = par_psum[par]
                    fsem = par_free[par]
                    sil_t = ss.v + 1
                    @blk.scalar
                    def _(s, PP=PP, par=par, mm_t=mm_t, Wr=Wr):
                        s.wait_ge(st.h, mm_t)
                        s.activation(out=SIL[par][:, 0:Wr], in_=PP[:, 0:Wr],
                                     func=AF.Silu).then_inc(ss.h, 1)
                    ss.v = sil_t
                    h_t = sv.v + 1
                    @blk.vector
                    def _(v, PP=PP, par=par, fsem=fsem, sil_t=sil_t, Wr=Wr):
                        v.wait_ge(ss.h, sil_t)
                        i = v.tensor_tensor(out=H16[par][:, 0:Wr],
                                            in0=SIL[par][:, 0:Wr],
                                            in1=PP[:, 512:512 + Wr], op=ALU.mult)
                        fsem.v += 1
                        i.then_inc(fsem.h, 1)
                        q16(v, H16[par][:, 0:Wr], QH[par][:, 0:Wr], Wr, 1.0 / 12.0) \
                            .then_inc(sv.h, 1)
                    sv.v = h_t
                    return h_t

                def emit_tail(t, h_t):
                    par = t % 2
                    t_t = st.v + 1
                    ftw = ft.v
                    @blk.tensor
                    def _(tb, par=par, h_t=h_t, ftw=ftw, ch=ch):
                        tb.wait_ge(sv.h, h_t)
                        tb.wait_ge(ft.h, ftw)
                        i = None
                        for j in range(ch):
                            i = tb.transpose(out=PT[:, 128 * j:128 * (j + 1)],
                                             in_=QH[par][:, 128 * j:128 * (j + 1)],
                                             identity=EYET[:])
                        i.then_inc(st.h, 1)
                    st.v = t_t
                    e_t = ft.v + 1
                    @blk.vector
                    def _(v, par=par, t_t=t_t, ch=ch):
                        v.wait_ge(st.h, t_t)
                        v.tensor_copy(QHT[par][:, 0:128 * ch],
                                      PT[:, 0:128 * ch]).then_inc(ft.h, 1)
                    ft.v = e_t
                    dn_t = st.v + 1
                    fdw = fd.v
                    @blk.tensor
                    def _(tb, par=par, e_t=e_t, fdw=fdw, ch=ch, wb=wb):
                        tb.wait_ge(ft.h, e_t)
                        tb.wait_ge(fd.h, fdw)
                        i = None
                        for kc in range(ch):
                            for (a, bnd) in ((0, 512), (512, 1024)):
                                i = tb.matmul(
                                    out=PDN[:, a:bnd],
                                    lhsT=QHT[par][:, 128 * kc:128 * (kc + 1)],
                                    rhs=WA[:, wb + 8192 + 1024 * kc + a:wb + 8192 + 1024 * kc + bnd],
                                    start=(kc == 0), stop=(kc == ch - 1))
                        i.then_inc(st.h, 1)
                    st.v = dn_t
                    add_t = sv.v + 1
                    final = write_out is not None and r == len(ROUNDS) - 1
                    @blk.vector
                    def _(v, t=t, dn_t=dn_t, final=final):
                        v.wait_ge(st.h, dn_t)
                        strm = STREAM[:, 1024 * t:1024 * (t + 1)]
                        if final:
                            i = v.tensor_tensor(out=XOUT[:], in0=PDN[:, 0:1024],
                                                in1=strm, op=ALU.add)
                        else:
                            i = v.tensor_tensor(out=strm, in0=PDN[:, 0:1024],
                                                in1=strm, op=ALU.add)
                        fd.v += 1
                        i.then_inc(fd.h, 1)
                        v.tensor_scalar(out=MRKV[:], in0=MRKV[:], scalar1=1.0,
                                        scalar2=None, op0=ALU.mult).then_inc(sv.h, 1)
                    sv.v = add_t
                    if final:
                        o_t = so.v + 16
                        @blk.sync
                        def _(sync, t=t, add_t=add_t, write_out=write_out):
                            sync.wait_ge(sv.h, add_t)
                            sync.dma_start(out=write_out[128 * t:128 * (t + 1), :],
                                           in_=XOUT[:]).then_inc(so.h, 16)
                        so.v = o_t
                        xg = so.v
                        @blk.vector
                        def _(v, xg=xg):
                            v.wait_ge(so.h, xg)

                prev = None
                for t in range(NT + 1):
                    if t < NT:
                        mm_t = emit_gateup(t)
                    if prev is not None:
                        emit_tail(prev[0], prev[1])
                    if t < NT:
                        h_t = emit_silu_h(t, mm_t)
                        prev = (t, h_t)
                round_done[r % 2] = st.v
            return st.v

        def emit_attn(stream_sv, upto="all", ln_cb=None):
            ps_u = [0]
            def ps_wait(eng):
                eng.wait_ge(fs.h, ps_u[0])
            wqkv_ld = sg.v + 16
            wa_free = st.v
            @blk.gpsimd
            def _(gp, wa_free=wa_free):
                gp.wait_ge(st.h, wa_free)
                gp.dma_start(
                    out=WA[:].rearrange("p (k c) -> p k c", c=3 * D),
                    in_=WQKV.rearrange("(k p) c -> p k c", p=128)).then_inc(sg.h, 16)
            sg.v = wqkv_ld
            bn_t = sv.v + 1
            @blk.vector
            def _(v, stream_sv=stream_sv):
                v.wait_ge(sv.h, stream_sv)
                for t in range(NT):
                    v.bn_stats(out=BN6[:, 12 * t:12 * t + 6],
                               in_=STREAM[:, 1024 * t:1024 * t + 512])
                    v.bn_stats(out=BN6[:, 12 * t + 6:12 * t + 12],
                               in_=STREAM[:, 1024 * t + 512:1024 * t + 1024])
                v.drain()
                for t in range(NT):
                    v.bn_aggr(out=BN2[:, 2 * t:2 * t + 2],
                              in_=BN6[:, 12 * t:12 * t + 12])
                v.drain()
                b2 = BN2[:].rearrange("p (t two) -> p t two", two=2)
                v.tensor_scalar(out=MNEG[:], in0=b2[:, :, 0], scalar1=-1.0,
                                scalar2=None, op0=ALU.mult)
                v.tensor_scalar(out=TMPV[:], in0=b2[:, :, 1], scalar1=1e-5,
                                scalar2=None, op0=ALU.add)
                v.drain()
                v.reciprocal(out=TMPV[:], in_=TMPV[:])
                v.drain()
                v.tensor_scalar(out=TMPV[:], in0=TMPV[:], scalar1=1.0,
                                scalar2=None, op0=ALU.mult).then_inc(sv.h, 1)
            sv.v = bn_t
            sq_t = ss.v + 1
            @blk.scalar
            def _(s, bn_t=bn_t):
                s.wait_ge(sv.h, bn_t)
                s.activation(out=RSTD[:], in_=TMPV[:],
                             func=AF.Sqrt).then_inc(ss.h, 1)
            ss.v = sq_t
            for t in range(NT):
                ln_t = sv.v + 1
                @blk.vector
                def _(v, t=t, sq_t=sq_t):
                    if t == 0:
                        v.wait_ge(ss.h, sq_t)
                    v.tensor_scalar(out=LNT[:], in0=STREAM[:, 1024 * t:1024 * (t + 1)],
                                    scalar1=MNEG[:, t:t + 1], scalar2=RSTD[:, t:t + 1],
                                    op0=ALU.add, op1=ALU.mult).then_inc(sv.h, 1)
                sv.v = ln_t
                ym_t = st.v + 1
                fgw = fg.v
                @blk.tensor
                def _(tt, t=t, ln_t=ln_t, fgw=fgw):
                    tt.wait_ge(sv.h, ln_t)
                    if t == 0:
                        tt.wait_ge(fg.h, fgw)
                    i = tt.matmul(out=PGA[0:16, 0:512], lhsT=ETBLT[:, 16 * t:16 * (t + 1)],
                                  rhs=LNT[:, 0:512], start=(t == 0), stop=(t == NT - 1))
                    i = tt.matmul(out=PGA[0:16, 512:1024], lhsT=ETBLT[:, 16 * t:16 * (t + 1)],
                                  rhs=LNT[:, 512:1024], start=(t == 0), stop=(t == NT - 1))
                    i.then_inc(st.h, 1)
                st.v = ym_t
                @blk.vector
                def _(v, ym_t=ym_t):
                    v.wait_ge(st.h, ym_t)
            ylm_t = sv.v + 1
            @blk.vector
            def _(v):
                i = v.tensor_scalar(out=YLM[:], in0=PGA[0:16, 0:1024],
                                    scalar1=1.0 / 128.0, scalar2=None, op0=ALU.mult)
                fg.v += 1
                i.then_inc(fg.h, 1)
                v.tensor_scalar(out=MRKV[:], in0=MRKV[:], scalar1=1.0, scalar2=None, op0=ALU.mult).then_inc(sv.h, 1)
            sv.v = ylm_t
            if upto == "p1":
                return sv.v
            agy_t = sg.v + 1
            @blk.gpsimd
            def _(gp, ylm_t=ylm_t):
                gp.wait_ge(sv.h, ylm_t)
                gp.dma_start(out=CIN_Y.rearrange("(p c) -> p c", p=16),
                             in_=YLM[:]).then_inc(sg.h, 16)
                gp.wait_ge(sg.h, sg.v + 16)
                gp.collective_compute(
                    "AllGather", ALU.bypass, ins=[CIN_Y], outs=[COUT_Y],
                    replica_groups=GROUPS4).then_inc(sg.h, 1)
            sg.v += 17
            agy_done = sg.v
            ylmf_ld = sd.v + 16
            @blk.sync
            def _(sync, agy_done=agy_done):
                sync.wait_ge(sg.h, agy_done)
                sync.dma_start(out=YLMF[:],
                               in_=COUT_Y.rearrange("r (p c) -> (r p) c", c=D)
                               ).then_inc(sd.h, 16)
            sd.v = ylmf_ld
            ftw = ft.v
            yt_t = st.v + 1
            @blk.tensor
            def _(tt, ylmf_ld=ylmf_ld, ftw=ftw):
                tt.wait_ge(sd.h, ylmf_ld)
                tt.wait_ge(ft.h, ftw)
                i = None
                for c in range(8):
                    i = tt.transpose(out=PT[:, 64 * c:64 * (c + 1)],
                                     in_=YLMF[0:64, 128 * c:128 * (c + 1)],
                                     identity=EYET[0:64, 0:64])
                i.then_inc(st.h, 1)
            st.v = yt_t
            ye_t = ft.v + 1
            @blk.scalar
            def _(s, yt_t=yt_t):
                s.wait_ge(st.h, yt_t)
                s.copy(out=YLMT[:], in_=PT[:]).then_inc(ft.h, 1)
            ft.v = ye_t
            if upto == "p2":
                @blk.vector
                def _(v, ye_t=ye_t):
                    v.wait_ge(ft.h, ye_t)
                    v.tensor_scalar(out=MRKV[:], in0=MRKV[:], scalar1=1.0, scalar2=None, op0=ALU.mult).then_inc(sv.h, 1)
                sv.v += 1
                return sv.v
            for dstT, cof in ((QLMF, 0), (KLMF, 1024)):
                mm_t = st.v + 1
                @blk.tensor
                def _(tt, dstT=dstT, cof=cof, ye_t=ye_t, wqkv_ld=wqkv_ld,
                      psu=ps_u[0]):
                    tt.wait_ge(ft.h, ye_t)
                    tt.wait_ge(sg.h, wqkv_ld)
                    tt.wait_ge(fs.h, psu)
                    i = None
                    for och in range(8):
                        for k in range(8):
                            i = tt.matmul(
                                out=PS[:, 64 * och:64 * (och + 1)],
                                lhsT=WA[:, 3072 * k + cof + 128 * och:
                                        3072 * k + cof + 128 * (och + 1)],
                                rhs=YLMT[:, 64 * k:64 * (k + 1)],
                                start=(k == 0), stop=(k == 7))
                    i.then_inc(st.h, 1)
                st.v = mm_t
                ev_t = ss.v + 1
                @blk.scalar
                def _(s, dstT=dstT, mm_t=mm_t):
                    s.wait_ge(st.h, mm_t)
                    i = s.copy(out=dstT[:], in_=PS[:])
                    fs.v += 1
                    i.then_inc(fs.h, 1)
                    s.copy(out=MRKS[:, 0:1], in_=MRKS[:, 1:2]).then_inc(ss.h, 1)
                ss.v = ev_t
                ps_u[0] = fs.v
            lm_ss = ss.v
            if upto == "p3":
                @blk.vector
                def _(v, lm_ss=lm_ss):
                    v.wait_ge(ss.h, lm_ss)
                    v.tensor_scalar(out=MRKV[:], in0=MRKV[:], scalar1=1.0, scalar2=None, op0=ALU.mult).then_inc(sv.h, 1)
                sv.v += 1
                return sv.v
            z2_t = st.v + 1
            @blk.tensor
            def _(tt, lm_ss=lm_ss, psu=ps_u[0]):
                tt.wait_ge(ss.h, lm_ss)
                tt.wait_ge(fs.h, psu)
                i = None
                for h in range(16):
                    p, hh = h // 2, 64 * (h % 2)
                    i = tt.matmul(
                        out=PS[hh:hh + 64, 64 * p:64 * (p + 1)],
                        lhsT=QLMF[hh:hh + 64, 64 * (h // 2):64 * (h // 2) + 64],
                        rhs=KLMF[hh:hh + 64, 64 * (h // 2):64 * (h // 2) + 64],
                        start=True, stop=True)
                i.then_inc(st.h, 1)
            st.v = z2_t
            e2_t = ss.v + 1
            @blk.scalar
            def _(s, z2_t=z2_t):
                s.wait_ge(st.h, z2_t)
                s.activation(out=E2SB[:], in_=PS[:], func=AF.Exp,
                             scale=0.125).then_inc(ss.h, 1)
            ss.v = e2_t
            k2_t = sv.v + 1
            @blk.vector
            def _(v, e2_t=e2_t):
                v.wait_ge(ss.h, e2_t)
                v.tensor_scalar(out=MRKV[:], in0=MRKV[:], scalar1=1.0, scalar2=None, op0=ALU.mult).then_inc(fs.h, 1)
                v.tensor_reduce(out=SM8[:],
                                in_=E2SB[:].rearrange("p (pr c) -> p pr c", c=64),
                                axis=AX.X, op=ALU.add)
                v.drain()
                v.reciprocal(out=SM8[:], in_=SM8[:])
                v.drain()
                v.tensor_tensor(out=K2N[:].rearrange("p (pr c) -> p pr c", c=64),
                                in0=E2SB[:].rearrange("p (pr c) -> p pr c", c=64),
                                in1=SM8[:].unsqueeze(2).broadcast_to([128, 8, 64]),
                                op=ALU.mult).then_inc(sv.h, 1)
            fs.v += 1
            ps_u[0] = fs.v
            sv.v = k2_t
            if upto == "p3a":
                @blk.vector
                def _(v, base=sv.v):
                    v.wait_ge(sv.h, base)
                    v.tensor_scalar(out=MRKV[:], in0=MRKV[:], scalar1=1.0, scalar2=None, op0=ALU.mult).then_inc(sv.h, 1)
                sv.v += 1
                return sv.v
            fdw = fd.v
            dn_t = st.v + 1
            @blk.tensor
            def _(tt, k2_t=k2_t, fdw=fdw):
                tt.wait_ge(sv.h, k2_t)
                tt.wait_ge(fd.h, fdw)
                i = None
                for h in range(16):
                    p, c = h // 2, h % 2
                    i = tt.matmul(out=PDN[0:1, 64 * h:64 * (h + 1)],
                                  lhsT=O2T[:, c:c + 1],
                                  rhs=K2N[:, 64 * p:64 * (p + 1)],
                                  start=True, stop=True)
                i.then_inc(st.h, 1)
            st.v = dn_t
            rd_t = sv.v + 1
            @blk.vector
            def _(v, dn_t=dn_t):
                v.wait_ge(st.h, dn_t)
                v.tensor_reduce(out=RDEN[:],
                                in_=PDN[0:1, :].rearrange("p (h c) -> p h c", c=64),
                                axis=AX.X, op=ALU.max)
                fd.v += 1
                v.tensor_scalar(out=MRKV[:], in0=MRKV[:], scalar1=1.0, scalar2=None, op0=ALU.mult).then_inc(fd.h, 1)
                v.drain()
                v.reciprocal(out=RDEN[:], in_=RDEN[:])
                v.drain()
                v.tensor_scalar(out=RDENB[:], in0=RDEN[:], scalar1=1.0,
                                scalar2=None, op0=ALU.mult).then_inc(sv.h, 1)
            sv.v = rd_t
            if upto == "p3b":
                @blk.vector
                def _(v, base=sv.v):
                    v.wait_ge(sv.h, base)
                    v.tensor_scalar(out=MRKV[:], in0=MRKV[:], scalar1=1.0, scalar2=None, op0=ALU.mult).then_inc(sv.h, 1)
                sv.v += 1
                return sv.v
            rdm_t = st.v + 1
            @blk.tensor
            def _(tt, rd_t=rd_t, psu=ps_u[0]):
                tt.wait_ge(sv.h, rd_t)
                tt.wait_ge(fs.h, psu)
                tt.matmul(out=PS[:, 0:16], lhsT=ONESRT[:], rhs=RDENB[:],
                          start=True, stop=True).then_inc(st.h, 1)
            st.v = rdm_t
            v0_t = sv.v + 1
            @blk.vector
            def _(v, rdm_t=rdm_t):
                v.wait_ge(st.h, rdm_t)
                i = v.tensor_scalar(out=RD128[:], in0=PS[:, 0:16], scalar1=1.0,
                                    scalar2=None, op0=ALU.mult)
                fs.v += 1
                i.then_inc(fs.h, 1)
                v.drain()
                v.tensor_scalar(out=RDE[:], in0=RD128[:].rearrange(
                    "p (pr two) -> p pr two", two=2)[:, :, 0],
                    scalar1=1.0, scalar2=None, op0=ALU.mult)
                v.tensor_scalar(out=RDO[:], in0=RD128[:].rearrange(
                    "p (pr two) -> p pr two", two=2)[:, :, 1],
                    scalar1=1.0, scalar2=None, op0=ALU.mult)
                v.drain()
                v.tensor_tensor(
                    out=K2ND[0:64, :].rearrange("p (pr c) -> p pr c", c=64),
                    in0=K2N[0:64, :].rearrange("p (pr c) -> p pr c", c=64),
                    in1=RDE[0:64, :].unsqueeze(2).broadcast_to([64, 8, 64]),
                    op=ALU.mult)
                v.tensor_tensor(
                    out=K2ND[64:128, :].rearrange("p (pr c) -> p pr c", c=64),
                    in0=K2N[64:128, :].rearrange("p (pr c) -> p pr c", c=64),
                    in1=RDO[64:128, :].unsqueeze(2).broadcast_to([64, 8, 64]),
                    op=ALU.mult).then_inc(sv.h, 1)
            ps_u[0] = fs.v
            sv.v = v0_t
            if upto == "p4":
                @blk.vector
                def _(v, v0_t=v0_t):
                    v.wait_ge(sv.h, v0_t)
                    v.tensor_scalar(out=MRKV[:], in0=MRKV[:], scalar1=1.0, scalar2=None, op0=ALU.mult).then_inc(sv.h, 1)
                sv.v += 1
                return sv.v
            for srcT, dstT in ((K2N, KTP), (K2ND, VSB)):
                for g0 in (0, 4):
                    t_t = st.v + 1
                    ftw = ft.v
                    @blk.tensor
                    def _(tt, srcT=srcT, g0=g0, v0_t=v0_t, ftw=ftw):
                        tt.wait_ge(sv.h, v0_t)
                        tt.wait_ge(ft.h, ftw)
                        i = None
                        for j in range(4):
                            i = tt.transpose(out=PT[0:64, 128 * j:128 * (j + 1)],
                                             in_=srcT[:, 64 * (g0 + j):64 * (g0 + j + 1)],
                                             identity=EYET[:])
                        i.then_inc(st.h, 1)
                    st.v = t_t
                    e_t = ft.v + 1
                    @blk.scalar
                    def _(s, dstT=dstT, g0=g0, t_t=t_t):
                        s.wait_ge(st.h, t_t)
                        s.copy(out=dstT[0:64, 128 * g0:128 * (g0 + 4)],
                               in_=PT[0:64, 0:512]).then_inc(ft.h, 1)
                    ft.v = e_t
            prep_done_ft = ft.v
            if upto == "prep":
                return sv.v

            p_first = [True]
            rd_sv = [0, 0]
            ps_free_sv = [0]
            for g in range(4):
                for tt_i in range(4):
                    t = 4 * g + tt_i
                    ln_t = sv.v + 1
                    prev_tr = st.v
                    @blk.vector
                    def _(v, t=t, prev_tr=prev_tr):
                        v.wait_ge(st.h, prev_tr)
                        v.tensor_scalar(out=LNT[:], in0=STREAM[:, 1024 * t:1024 * (t + 1)],
                                        scalar1=MNEG[:, t:t + 1], scalar2=RSTD[:, t:t + 1],
                                        op0=ALU.add, op1=ALU.mult).then_inc(sv.h, 1)
                    sv.v = ln_t
                    for gg in range(2):
                        t_t = st.v + 1
                        ftw = ft.v
                        @blk.tensor
                        def _(tb, gg=gg, ln_t=ln_t, ftw=ftw):
                            tb.wait_ge(sv.h, ln_t)
                            tb.wait_ge(ft.h, ftw)
                            i = None
                            for j in range(4):
                                c = 4 * gg + j
                                i = tb.transpose(out=PT[:, 128 * j:128 * (j + 1)],
                                                 in_=LNT[:, 128 * c:128 * (c + 1)],
                                                 identity=EYET[:])
                            i.then_inc(st.h, 1)
                        st.v = t_t
                        e_t = ft.v + 1
                        @blk.vector
                        def _(v, gg=gg, tt_i=tt_i, t_t=t_t):
                            v.wait_ge(st.h, t_t)
                            dst = LNF[:].rearrange("p (k r) -> p k r", r=512) \
                                [:, 4 * gg:4 * gg + 4, 128 * tt_i:128 * (tt_i + 1)]
                            v.tensor_copy(dst, PT[:].rearrange(
                                "p (k r) -> p k r", r=128)).then_inc(ft.h, 1)
                        ft.v = e_t
                lnf_ft = ft.v
                for dst_kind in ("q", "k"):
                    cof = 0 if dst_kind == "q" else 1024
                    for och in range(8):
                        mm_t = st.v + 1
                        @blk.tensor
                        def _(tb, och=och, cof=cof, lnf_ft=lnf_ft, psu=ps_u[0]):
                            tb.wait_ge(ft.h, lnf_ft)
                            tb.wait_ge(fs.h, psu)
                            i = None
                            for k in range(8):
                                i = tb.matmul(
                                    out=PS[:, 0:512],
                                    lhsT=WA[:, 3072 * k + cof + 128 * och:
                                            3072 * k + cof + 128 * (och + 1)],
                                    rhs=LNF[:, 512 * k:512 * (k + 1)],
                                    start=(k == 0), stop=(k == 7))
                            i.then_inc(st.h, 1)
                        st.v = mm_t
                        e_t = sv.v + 1
                        @blk.vector
                        def _(v, och=och, g=g, dst_kind=dst_kind, mm_t=mm_t):
                            v.wait_ge(st.h, mm_t)
                            if dst_kind == "q":
                                dst = FMA[:, 2048 * och + 512 * g:2048 * och + 512 * (g + 1)]
                            else:
                                dst = KFM[:, 512 * och:512 * (och + 1)]
                            i = v.tensor_copy(dst, PS[:, 0:512])
                            fs.v += 1
                            i.then_inc(fs.h, 1)
                            v.tensor_scalar(out=MRKV[:], in0=MRKV[:], scalar1=1.0,
                                            scalar2=None, op0=ALU.mult).then_inc(sv.h, 1)
                        sv.v = e_t
                        ps_u[0] = fs.v
                qk_sv = sv.v
                for tt_i in range(4):
                    mm_t = st.v + 1
                    fgw = fg.v
                    @blk.tensor
                    def _(tb, tt_i=tt_i, lnf_ft=lnf_ft, fgw=fgw):
                        tb.wait_ge(ft.h, lnf_ft)
                        tb.wait_ge(fg.h, fgw)
                        i = None
                        for k in range(8):
                            lhs = LNF[:, 512 * k + 128 * tt_i:512 * k + 128 * (tt_i + 1)]
                            for (a, bnd) in ((0, 512), (512, 1024)):
                                i = tb.matmul(out=PGA[:, a:bnd], lhsT=lhs,
                                              rhs=WA[:, 3072 * k + 2048 + a:3072 * k + 2048 + bnd],
                                              start=(k == 0), stop=(k == 7))
                        i.then_inc(st.h, 1)
                    st.v = mm_t
                    ve_t = sv.v + 1
                    @blk.vector
                    def _(v, tt_i=tt_i, mm_t=mm_t):
                        v.wait_ge(st.h, mm_t)
                        i = v.tensor_scalar(out=VTMS[:, 1024 * tt_i:1024 * (tt_i + 1)],
                                            in0=PGA[:, 0:1024], scalar1=1.0,
                                            scalar2=None, op0=ALU.mult)
                        fg.v += 1
                        i.then_inc(fg.h, 1)
                        v.tensor_scalar(out=MRKV[:], in0=MRKV[:], scalar1=1.0, scalar2=None, op0=ALU.mult).then_inc(sv.h, 1)
                    sv.v = ve_t
                vtm_sv = sv.v
                E3PB = [E3P, QH[0]]

                def b1_z3(p):
                    z_t = st.v + 1
                    @blk.tensor
                    def _(tb, p=p, qk_sv=qk_sv, psu=ps_free_sv[0]):
                        tb.wait_ge(sv.h, max(qk_sv, psu))
                        i = None
                        for hh in (0, 64):
                            i = tb.matmul(
                                out=PS[hh:hh + 64, 0:512],
                                lhsT=QLMF[hh:hh + 64, 64 * p:64 * (p + 1)],
                                rhs=KFM[hh:hh + 64, 512 * p:512 * (p + 1)],
                                start=True, stop=True)
                        i.then_inc(st.h, 1)
                    st.v = z_t
                    return z_t

                def b1_exp(p, z_t):
                    par = p % 2
                    e_t = ss.v + 1
                    rdw = rd_sv[par]
                    @blk.scalar
                    def _(s, par=par, z_t=z_t, rdw=rdw):
                        s.wait_ge(st.h, z_t)
                        s.wait_ge(sv.h, rdw)
                        s.activation(out=E3PB[par][:], in_=PS[:, 0:512],
                                     func=AF.Exp, scale=0.125).then_inc(ss.h, 1)
                    ss.v = e_t
                    return e_t

                def b1_reduce(p, e_t):
                    par = p % 2
                    sl_t = sv.v + 1
                    @blk.vector
                    def _(v, p=p, g=g, par=par, e_t=e_t):
                        v.wait_ge(ss.h, e_t)
                        v.tensor_reduce(out=SLOC4[:, 4 * p + g:4 * p + g + 1],
                                        in_=E3PB[par][:], axis=AX.X,
                                        op=ALU.add).then_inc(sv.h, 1)
                    sv.v = sl_t
                    rd_sv[par] = sl_t
                    ps_free_sv[0] = sl_t

                def b1_tail(p, e_t):
                    par = p % 2
                    t_t = st.v + 1
                    ftw = ft.v
                    @blk.tensor
                    def _(tb, par=par, e_t=e_t, ftw=ftw):
                        tb.wait_ge(ss.h, e_t)
                        tb.wait_ge(ft.h, ftw)
                        i = None
                        for j in range(4):
                            i = tb.transpose(out=PT[:, 128 * j:128 * (j + 1)],
                                             in_=E3PB[par][:, 128 * j:128 * (j + 1)],
                                             identity=EYET[:])
                        i.then_inc(st.h, 1)
                    st.v = t_t
                    e2t_ = ft.v + 1
                    @blk.vector
                    def _(v, t_t=t_t):
                        v.wait_ge(st.h, t_t)
                        v.tensor_copy(E3TS[:], PT[:, 0:512]).then_inc(ft.h, 1)
                    ft.v = e2t_
                    pm_t = st.v + 1
                    fuw = fu.v
                    @blk.tensor
                    def _(tb, p=p, e2t_=e2t_, vtm_sv=vtm_sv, fuw=fuw):
                        tb.wait_ge(ft.h, e2t_)
                        tb.wait_ge(sv.h, vtm_sv)
                        tb.wait_ge(fu.h, fuw)
                        i = None
                        for tc in range(4):
                            i = tb.matmul(
                                out=PGB[:, 0:128],
                                lhsT=E3TS[:, 128 * tc:128 * (tc + 1)],
                                rhs=VTMS[:, 1024 * tc + 128 * p:1024 * tc + 128 * (p + 1)],
                                start=(tc == 0), stop=(tc == 3))
                        i.then_inc(st.h, 1)
                    st.v = pm_t
                    pe_t = sv.v + 1
                    @blk.vector
                    def _(v, p=p, g=g, pm_t=pm_t):
                        v.wait_ge(st.h, pm_t)
                        if g == 0:
                            i = v.tensor_scalar(out=P_ACC[:, 128 * p:128 * (p + 1)],
                                                in0=PGB[:, 0:128], scalar1=1.0,
                                                scalar2=None, op0=ALU.mult)
                        else:
                            i = v.tensor_tensor(out=P_ACC[:, 128 * p:128 * (p + 1)],
                                                in0=PGB[:, 0:128],
                                                in1=P_ACC[:, 128 * p:128 * (p + 1)],
                                                op=ALU.add)
                        fu.v += 1
                        i.then_inc(fu.h, 1)
                        v.tensor_scalar(out=MRKV[:], in0=MRKV[:], scalar1=1.0,
                                        scalar2=None, op0=ALU.mult).then_inc(sv.h, 1)
                    sv.v = pe_t

                prevp = None
                for p in range(9):
                    if p < 8:
                        z_t = b1_z3(p)
                    if prevp is not None:
                        b1_tail(prevp[0], prevp[1])
                    if p < 8:
                        e_t = b1_exp(p, z_t)
                        b1_reduce(p, e_t)
                        prevp = (p, e_t)
                    else:
                        prevp = None
                p_first[0] = False
            b1_st = st.v
            if upto == "b1":
                @blk.vector
                def _(v, b1_st=b1_st):
                    v.wait_ge(st.h, b1_st)
                    v.tensor_scalar(out=MRKV[:], in0=MRKV[:], scalar1=1.0, scalar2=None, op0=ALU.mult).then_inc(sv.h, 1)
                sv.v += 1
                return sv.v

            slc_t = sv.v + 1
            @blk.vector
            def _(v, b1_st=b1_st):
                v.wait_ge(st.h, b1_st)
                v.tensor_reduce(out=SLOC[:],
                                in_=SLOC4[:].rearrange("p (pr c) -> p pr c", c=4),
                                axis=AX.X, op=ALU.add)
                v.tensor_scalar(
                    out=PCMP[0:64, 0:512].rearrange("p (pr c) -> p pr c", c=64),
                    in0=P_ACC[0:64, :].rearrange("p (pr c) -> p pr c", c=128)[:, :, 0:64],
                    scalar1=1.0, scalar2=None, op0=ALU.mult)
                v.tensor_scalar(
                    out=PCMP[64:128, 0:512].rearrange("p (pr c) -> p pr c", c=64),
                    in0=P_ACC[64:128, :].rearrange("p (pr c) -> p pr c", c=128)[:, :, 64:128],
                    scalar1=1.0, scalar2=None, op0=ALU.mult)
                v.tensor_scalar(out=PCMP[:, 512:520], in0=SLOC[:], scalar1=1.0,
                                scalar2=None, op0=ALU.mult).then_inc(sv.h, 1)
            sv.v = slc_t
            agp_t = sg.v + 17
            @blk.gpsimd
            def _(gp, slc_t=slc_t):
                gp.wait_ge(sv.h, slc_t)
                gp.dma_start(out=CIN_P.rearrange("(p c) -> p c", p=128),
                             in_=PCMP[:]).then_inc(sg.h, 16)
                gp.wait_ge(sg.h, sg.v + 16)
                gp.collective_compute(
                    "AllGather", ALU.bypass, ins=[CIN_P], outs=[COUT_P],
                    replica_groups=GROUPS4).then_inc(sg.h, 1)
            sg.v = agp_t
            rb_marks = []
            for r in range(4):
                prb_ld = sd.v + 16
                prev_acc = sv.v
                @blk.sync
                def _(sync, r=r, agp_t=agp_t, prev_acc=prev_acc):
                    if r == 0:
                        sync.wait_ge(sg.h, agp_t)
                    else:
                        sync.wait_ge(sv.h, prev_acc)
                    sync.dma_start(
                        out=PRB[:],
                        in_=COUT_P[r, :].rearrange("(p c) -> p c", p=128)
                    ).then_inc(sd.h, 16)
                sd.v = prb_ld
                acc_t = sv.v + 1
                @blk.vector
                def _(v, r=r, prb_ld=prb_ld):
                    v.wait_ge(sd.h, prb_ld)
                    if r == 0:
                        v.tensor_scalar(out=PCMP[:], in0=PRB[:], scalar1=1.0,
                                        scalar2=None, op0=ALU.mult).then_inc(sv.h, 1)
                    else:
                        v.tensor_tensor(out=PCMP[:], in0=PCMP[:], in1=PRB[:],
                                        op=ALU.add).then_inc(sv.h, 1)
                sv.v = acc_t
            prb_ld = sv.v

            vts0_t = sv.v + 1
            @blk.vector
            def _(v, pdt=prep_done_ft):
                v.wait_ge(ft.h, pdt)
                v.tensor_scalar(out=VTS[:], in0=K2ND[:], scalar1=1.0,
                                scalar2=None, op0=ALU.mult).then_inc(sv.h, 1)
            sv.v = vts0_t
            for it in range(NS_ITERS):
                kv_t = st.v + 1
                fdw = fd.v
                @blk.tensor
                def _(tb, psu=ps_u[0], fdw=fdw, first=(it == 0),
                      vts0_t=vts0_t):
                    if first:
                        tb.wait_ge(sv.h, vts0_t)
                    tb.wait_ge(fs.h, psu)
                    tb.wait_ge(fd.h, fdw)
                    i = None
                    for h in range(16):
                        p, hh = h // 2, 64 * (h % 2)
                        i = tb.matmul(
                            out=PS[hh:hh + 64, 64 * p:64 * (p + 1)],
                            lhsT=KTP[0:64, 128 * p + hh:128 * p + hh + 64],
                            rhs=VSB[0:64, 128 * p + hh:128 * p + hh + 64],
                            start=True, stop=True)
                    for h in range(16):
                        p, hh = h // 2, 64 * (h % 2)
                        i = tb.matmul(
                            out=PDN[hh:hh + 64, 64 * p:64 * (p + 1)],
                            lhsT=VSB[0:64, 128 * p + hh:128 * p + hh + 64],
                            rhs=KTP[0:64, 128 * p + hh:128 * p + hh + 64],
                            start=True, stop=True)
                    i.then_inc(st.h, 1)
                st.v = kv_t
                r1_t = sv.v + 1
                @blk.vector
                def _(v, kv_t=kv_t):
                    v.wait_ge(st.h, kv_t)
                    i = v.scalar_tensor_tensor(out=R1SB[:], in0=EYBT[:],
                                               scalar=7.0, in1=PS[:, 0:512],
                                               op0=ALU.mult, op1=ALU.subtract)
                    fs.v += 1
                    i.then_inc(fs.h, 1)
                    i = v.scalar_tensor_tensor(out=R1TP[:], in0=EYBT[:],
                                               scalar=7.0, in1=PDN[:, 0:512],
                                               op0=ALU.mult, op1=ALU.subtract)
                    fd.v += 1
                    i.then_inc(fd.h, 1)
                    v.tensor_scalar(out=MRKV[:], in0=MRKV[:], scalar1=1.0, scalar2=None, op0=ALU.mult).then_inc(sv.h, 1)
                ps_u[0] = fs.v
                sv.v = r1_t
                t2_t = st.v + 1
                @blk.tensor
                def _(tb, r1_t=r1_t):
                    tb.wait_ge(sv.h, r1_t)
                    i = None
                    for h in range(16):
                        p, hh = h // 2, 64 * (h % 2)
                        i = tb.matmul(
                            out=PDN[hh:hh + 64, 512 + 64 * p:512 + 64 * (p + 1)],
                            lhsT=R1TP[hh:hh + 64, 64 * p:64 * (p + 1)],
                            rhs=R1SB[hh:hh + 64, 64 * p:64 * (p + 1)],
                            start=True, stop=True)
                    i.then_inc(st.h, 1)
                st.v = t2_t
                r2_t = sv.v + 1
                @blk.vector
                def _(v, t2_t=t2_t):
                    v.wait_ge(st.h, t2_t)
                    v.scalar_tensor_tensor(out=WSB[:], in0=EYBT[:], scalar=15.0,
                                           in1=PDN[:, 512:1024], op0=ALU.mult,
                                           op1=ALU.add)
                    v.scalar_tensor_tensor(out=R2SB[:], in0=R1SB[:], scalar=-7.0,
                                           in1=WSB[:], op0=ALU.mult, op1=ALU.add)
                    v.tensor_scalar(out=MRKV[:], in0=MRKV[:], scalar1=1.0, scalar2=None, op0=ALU.mult).then_inc(sv.h, 1)
                sv.v = r2_t
                t3_t = st.v + 1
                @blk.tensor
                def _(tb, r2_t=r2_t, psu=ps_u[0]):
                    tb.wait_ge(sv.h, r2_t)
                    tb.wait_ge(fs.h, psu)
                    i = None
                    for h in range(16):
                        p, hh = h // 2, 64 * (h % 2)
                        i = tb.matmul(
                            out=PS[hh:hh + 64, 64 * p:64 * (p + 1)],
                            lhsT=R1TP[hh:hh + 64, 64 * p:64 * (p + 1)],
                            rhs=R2SB[hh:hh + 64, 64 * p:64 * (p + 1)],
                            start=True, stop=True)
                    i.then_inc(st.h, 1)
                st.v = t3_t
                r3_t = sv.v + 1
                @blk.vector
                def _(v, t3_t=t3_t):
                    v.wait_ge(st.h, t3_t)
                    v.scalar_tensor_tensor(out=WSB[:], in0=EYBT[:], scalar=13.0,
                                           in1=PS[:, 0:512], op0=ALU.mult,
                                           op1=ALU.add)
                    i = v.scalar_tensor_tensor(out=R3SB[:], in0=R2SB[:],
                                               scalar=-7.0, in1=WSB[:],
                                               op0=ALU.mult, op1=ALU.add)
                    fs.v += 1
                    i.then_inc(fs.h, 1)
                    v.tensor_scalar(out=MRKV[:], in0=MRKV[:], scalar1=1.0, scalar2=None, op0=ALU.mult).then_inc(sv.h, 1)
                ps_u[0] = fs.v
                sv.v = r3_t
                vp_t = st.v + 1
                @blk.tensor
                def _(tb, r3_t=r3_t, psu=ps_u[0]):
                    tb.wait_ge(sv.h, r3_t)
                    tb.wait_ge(fs.h, psu)
                    i = None
                    for h in range(16):
                        p, hh = h // 2, 64 * (h % 2)
                        i = tb.matmul(
                            out=PS[hh:hh + 64, 64 * p:64 * (p + 1)],
                            lhsT=VTS[hh:hh + 64, 64 * p:64 * (p + 1)],
                            rhs=R3SB[hh:hh + 64, 64 * p:64 * (p + 1)],
                            start=True, stop=True)
                    for h in range(16):
                        p, hh = h // 2, 64 * (h % 2)
                        i = tb.matmul(
                            out=PDN[hh:hh + 64, 64 * p:64 * (p + 1)],
                            lhsT=R3SB[hh:hh + 64, 64 * p:64 * (p + 1)],
                            rhs=VTS[hh:hh + 64, 64 * p:64 * (p + 1)],
                            start=True, stop=True)
                    i.then_inc(st.h, 1)
                st.v = vp_t
                ve_t = sv.v + 1
                @blk.vector
                def _(v, vp_t=vp_t):
                    v.wait_ge(st.h, vp_t)
                    v.tensor_scalar(
                        out=VSB[0:64, :].rearrange("p (pr c) -> p pr c", c=128)[:, :, 0:64],
                        in0=PS[0:64, :].rearrange("p (pr c) -> p pr c", c=64),
                        scalar1=0.25, scalar2=None, op0=ALU.mult)
                    i = v.tensor_scalar(
                        out=VSB[0:64, :].rearrange("p (pr c) -> p pr c", c=128)[:, :, 64:128],
                        in0=PS[64:128, :].rearrange("p (pr c) -> p pr c", c=64),
                        scalar1=0.25, scalar2=None, op0=ALU.mult)
                    fs.v += 1
                    i.then_inc(fs.h, 1)
                    i = v.tensor_scalar(out=VTS[:], in0=PDN[:, 0:512], scalar1=0.25,
                                        scalar2=None, op0=ALU.mult)
                    fd.v += 1
                    i.then_inc(fd.h, 1)
                    v.tensor_scalar(out=MRKV[:], in0=MRKV[:], scalar1=1.0, scalar2=None, op0=ALU.mult).then_inc(sv.h, 1)
                ps_u[0] = fs.v
                sv.v = ve_t
            vts_ft = ft.v
            k3_t = sv.v + 1
            @blk.vector
            def _(v):
                v.drain()
                v.reciprocal(out=RCPS[:], in_=PCMP[:, 512:520])
                v.drain()
                v.tensor_tensor(out=K3V[:].rearrange("p (pr c) -> p pr c", c=64),
                                in0=PCMP[:, 0:512].rearrange("p (pr c) -> p pr c", c=64),
                                in1=RCPS[:].unsqueeze(2).broadcast_to([128, 8, 64]),
                                op=ALU.mult).then_inc(sv.h, 1)
            sv.v = k3_t
            tm_t = st.v + 1
            @blk.tensor
            def _(tb, k3_t=k3_t, vts_ft=vts_ft, psu=ps_free_sv[0]):
                tb.wait_ge(sv.h, max(k3_t, psu))
                tb.wait_ge(ft.h, vts_ft)
                i = None
                for h in range(16):
                    p, hh = h // 2, 64 * (h % 2)
                    i = tb.matmul(
                        out=PS[hh:hh + 64, 64 * p:64 * (p + 1)],
                        lhsT=VTS[hh:hh + 64, 64 * p:64 * (p + 1)],
                        rhs=K3V[hh:hh + 64, 64 * p:64 * (p + 1)],
                        start=True, stop=True)
                i.then_inc(st.h, 1)
            st.v = tm_t
            te_t = sv.v + 1
            @blk.vector
            def _(v, tm_t=tm_t):
                v.wait_ge(st.h, tm_t)
                i = v.tensor_copy(TSB[:], PS[:, 0:512])
                fs.v += 1
                i.then_inc(fs.h, 1)
                v.tensor_scalar(out=MRKV[:], in0=MRKV[:], scalar1=1.0,
                                scalar2=None, op0=ALU.mult).then_inc(sv.h, 1)
            sv.v = te_t
            ps_u[0] = fs.v
            if upto == "ns":
                @blk.vector
                def _(v, te_t=te_t):
                    v.wait_ge(sv.h, te_t)
                    v.tensor_scalar(out=MRKV[:], in0=MRKV[:], scalar1=1.0, scalar2=None, op0=ALU.mult).then_inc(sv.h, 1)
                sv.v += 1
                return sv.v
            wproj_ld = sg.v + 16
            @blk.gpsimd
            def _(gp, b1_st=b1_st):
                gp.wait_ge(st.h, b1_st)
                gp.dma_start(
                    out=WA[:, 0:8192].rearrange("p (k c) -> p k c", c=D),
                    in_=WPROJ.rearrange("(k p) c -> p k c", p=128)).then_inc(sg.h, 16)
            sg.v = wproj_ld

            last_add = sv.v
            pt_u = [ft.v]
            e1_done = [0]
            grp_add = [0, 0, 0, 0]
            for g in range(4):
                if ln_cb is not None and g >= 1:
                    ln_cb(g - 1, [(sv, grp_add[g - 1])])
                def b2_k1(p):
                    k1_t = st.v + 1
                    @blk.tensor
                    def _(tb, p=p, g=g, te_t=te_t, psu=e1_done[0]):
                        tb.wait_ge(sv.h, te_t)
                        tb.wait_ge(ss.h, psu)
                        i = None
                        for hh in (0, 64):
                            i = tb.matmul(
                                out=PS[hh:hh + 64, 0:512],
                                lhsT=KLMF[hh:hh + 64, 64 * p:64 * (p + 1)],
                                rhs=FMA[hh:hh + 64, 2048 * p + 512 * g:2048 * p + 512 * (g + 1)],
                                start=True, stop=True)
                        i.then_inc(st.h, 1)
                    st.v = k1_t
                    return k1_t

                def b2_e1_sv(p, rdmt):
                    e_t = ss.v + 1
                    @blk.scalar
                    def _(s, rdmt=rdmt):
                        s.wait_ge(st.h, rdmt)
                        s.activation(out=E1P[:], in_=PS[:, 0:512],
                                     func=AF.Exp, scale=0.125).then_inc(ss.h, 1)
                    ss.v = e_t
                    e1_done[0] = e_t
                    sv_t = st.v + 1
                    ptw = pt_u[0]
                    @blk.tensor
                    def _(tb, p=p, e_t=e_t, ptw=ptw):
                        tb.wait_ge(ss.h, e_t)
                        tb.wait_ge(ft.h, ptw)
                        i = None
                        for hh in (0, 64):
                            i = tb.matmul(
                                out=PDN[hh:hh + 64, 512:1024],
                                lhsT=TSB[hh:hh + 64, 64 * p:64 * (p + 1)],
                                rhs=E1P[hh:hh + 64, :],
                                start=True, stop=True)
                        i.then_inc(st.h, 1)
                    st.v = sv_t
                    return e_t, sv_t

                def b2_tail(p, e_t, sv_t):
                    dm_t = st.v + 1
                    fdw = fd.v
                    @blk.tensor
                    def _(tb, e_t=e_t, fdw=fdw):
                        tb.wait_ge(ss.h, e_t)
                        tb.wait_ge(fd.h, fdw)
                        tb.matmul(out=PDN[0:2, 0:512],
                                  lhsT=O2T[:], rhs=E1P[:],
                                  start=True, stop=True).then_inc(st.h, 1)
                    st.v = dm_t
                    rc_t = ss.v + 1
                    @blk.scalar
                    def _(s, dm_t=dm_t):
                        s.wait_ge(st.h, dm_t)
                        i = s.activation(out=DSB[:], in_=PDN[0:2, 0:512],
                                         func=AF.Ln)
                        fd.v += 1
                        i.then_inc(fd.h, 1)
                        s.activation(out=RCDB[:], in_=DSB[:], func=AF.Exp,
                                     scale=-1.0).then_inc(ss.h, 1)
                    ss.v = rc_t
                    rm_t = st.v + 1
                    fdw = fd.v
                    @blk.tensor
                    def _(tb, rc_t=rc_t, fdw=fdw):
                        tb.wait_ge(ss.h, rc_t)
                        tb.wait_ge(fd.h, fdw)
                        tb.matmul(out=PDN[:, 0:512], lhsT=E2TTT[:],
                                  rhs=RCDB[:],
                                  start=True, stop=True).then_inc(st.h, 1)
                    st.v = rm_t
                    re_t = sv.v + 1
                    @blk.vector
                    def _(v, rm_t=rm_t):
                        v.wait_ge(st.h, rm_t)
                        i = v.tensor_copy(RSB[:], PDN[:, 0:512])
                        fd.v += 1
                        i.then_inc(fd.h, 1)
                        v.tensor_scalar(out=MRKV[:], in0=MRKV[:], scalar1=1.0,
                                        scalar2=None, op0=ALU.mult).then_inc(sv.h, 1)
                    sv.v = re_t
                    se_t = sv.v + 1
                    @blk.vector
                    def _(v, p=p, sv_t=sv_t):
                        v.wait_ge(st.h, sv_t)
                        i = v.tensor_tensor(out=SVF[:, 512 * p:512 * (p + 1)],
                                            in0=PDN[:, 512:1024], in1=RSB[:],
                                            op=ALU.mult)
                        pt_u[0] = ft.v + 1
                        ft.v += 1
                        i.then_inc(ft.h, 1)
                        v.tensor_scalar(out=MRKV[:], in0=MRKV[:], scalar1=1.0,
                                        scalar2=None, op0=ALU.mult).then_inc(sv.h, 1)
                    sv.v = se_t
                    return dm_t

                prevq = None
                for p in range(9):
                    if p < 8:
                        k1_t = b2_k1(p)
                        rdmt = k1_t
                    if prevq is not None:
                        rdmt = b2_tail(prevq[0], prevq[1][0], prevq[1][1])
                    if p < 8:
                        es = b2_e1_sv(p, rdmt)
                        prevq = (p, es)
                    else:
                        prevq = None
                svf_sv = sv.v
                for tt_i in range(4):
                    t = 4 * g + tt_i
                    pj_t = st.v + 1
                    fgw = fg.v
                    @blk.tensor
                    def _(tb, tt_i=tt_i, svf_sv=svf_sv, fgw=fgw,
                          wproj_ld=wproj_ld):
                        tb.wait_ge(sv.h, svf_sv)
                        tb.wait_ge(sg.h, wproj_ld)
                        tb.wait_ge(fg.h, fgw)
                        i = None
                        for c in range(8):
                            lhs = SVF[:, 512 * c + 128 * tt_i:512 * c + 128 * (tt_i + 1)]
                            for (a, bnd) in ((0, 512), (512, 1024)):
                                i = tb.matmul(out=PGA[:, a:bnd], lhsT=lhs,
                                              rhs=WA[:, 1024 * c + a:1024 * c + bnd],
                                              start=(c == 0), stop=(c == 7))
                        i.then_inc(st.h, 1)
                    st.v = pj_t
                    ad_t = sv.v + 1
                    @blk.vector
                    def _(v, t=t, pj_t=pj_t):
                        v.wait_ge(st.h, pj_t)
                        strm = STREAM[:, 1024 * t:1024 * (t + 1)]
                        i = v.tensor_tensor(out=strm, in0=PGA[:, 0:1024], in1=strm,
                                            op=ALU.add)
                        fg.v += 1
                        i.then_inc(fg.h, 1)
                        v.tensor_scalar(out=MRKV[:], in0=MRKV[:], scalar1=1.0, scalar2=None, op0=ALU.mult).then_inc(sv.h, 1)
                    sv.v = ad_t
                    last_add = sv.v
                grp_add[g] = sv.v
            if ln_cb is not None:
                ln_cb(3, [(sv, grp_add[3])])
            return last_add

        xq1_ft = emit_ln_xq([(sx, x_ld), (sd, 128)])
        ffn1_st = emit_ffn(xq1_ft, 0, None)

        def emit_dump():
            for t in range(NT):
                prev_o = so.v
                c_t = sv.v + 1
                @blk.vector
                def _(v, t=t, prev_o=prev_o, base=sv.v):
                    v.wait_ge(sv.h, base)
                    v.wait_ge(so.h, prev_o)
                    v.tensor_copy(XOUT[:], STREAM[:, 1024 * t:1024 * (t + 1)]) \
                        .then_inc(sv.h, 1)
                sv.v = c_t
                o_t = so.v + 16
                @blk.sync
                def _(sync, t=t, c_t=c_t):
                    sync.wait_ge(sv.h, c_t)
                    sync.dma_start(out=OUT[128 * t:128 * (t + 1), :],
                                   in_=XOUT[:]).then_inc(so.h, 16)
                so.v = o_t
            fin_o = so.v
            @blk.sync
            def _(sync, fin_o=fin_o):
                sync.wait_ge(so.h, fin_o)

        if stage == "ffn1":
            emit_dump()
            return nc
        if stage.startswith("attn"):
            upto = {"attn": "all", "attn_dbg": "all", "attn_prep": "prep",
                    "attn_b1": "b1",
                    "attn_ns": "ns", "attn_p1": "p1", "attn_p2": "p2",
                    "attn_p3": "p3", "attn_p4": "p4", "attn_p3a": "p3a",
                    "attn_p3b": "p3b"}[stage]
            fin_sv = emit_attn(sv.v, upto)
            if stage == "attn_dbg":
                slots = [(VTMS, 1024), (QLMF, 512), (KLMF, 512), (K2N, 512),
                         (K2ND, 512), (VSB, 1024), (K3V, 512), (TSB, 512),
                         (SLOC, 8), (PCMP, 520), (RD128, 16), (R1TP, 512),
                         (KTP, 1024), (R1SB, 512),
                         (R2SB, 512), (R3SB, 512)]
                row = 0
                for src_ap, cols in slots:
                    pc = src_ap.shape[0]
                    c_t = sv.v + 1
                    prev_o = so.v
                    @blk.vector
                    def _(v, src_ap=src_ap, cols=cols, pc=pc, prev_o=prev_o,
                          base=sv.v):
                        v.wait_ge(sv.h, base)
                        v.wait_ge(so.h, prev_o)
                        nc_ = min(cols, 1024)
                        v.tensor_scalar(out=XOUT[0:pc, 0:nc_],
                                        in0=src_ap[:, 0:nc_], scalar1=1.0,
                                        scalar2=None, op0=ALU.mult)
                        if cols > 1024:
                            v.tensor_scalar(out=XOUT[0:pc, 0:1024],
                                            in0=src_ap[:, 0:1024], scalar1=1.0,
                                            scalar2=None, op0=ALU.mult)
                        v.tensor_scalar(out=MRKV[:], in0=MRKV[:], scalar1=1.0, scalar2=None, op0=ALU.mult).then_inc(sv.h, 1)
                    sv.v = c_t
                    o_t = so.v + 16
                    @blk.sync
                    def _(sync, row=row, c_t=c_t):
                        sync.wait_ge(sv.h, c_t)
                        sync.dma_start(out=OUT[128 * row:128 * (row + 1), :],
                                       in_=XOUT[:]).then_inc(so.h, 16)
                    so.v = o_t
                    row += 1
                fin_o = so.v
                @blk.sync
                def _(sync, fin_o=fin_o):
                    sync.wait_ge(so.h, fin_o)
                return nc
            emit_dump()
            return nc
        ln2 = make_ln_group([])
        attn_sv = emit_attn(sv.v, ln_cb=ln2)
        xq2_ft = ft.v
        emit_ffn(xq2_ft, st.v, OUT)
        fin_o = so.v
        @blk.sync
        def _(sync, fin_o=fin_o):
            sync.wait_ge(so.h, fin_o)
    return nc


def make_consts():
    eye = np.eye(128).astype(bfl)
    eyeb = np.tile(np.vstack([np.eye(64), np.eye(64)]).astype(np.float32), (1, 8))
    ones1 = np.ones((128, 1), bfl)
    ones2 = np.zeros((128, 2), np.float32)
    ones2[:64, 0] = 1.0
    ones2[64:, 1] = 1.0
    e2t = np.zeros((2, 128), np.float32)
    e2t[0, :64] = 1.0
    e2t[1, 64:] = 1.0
    c80 = np.full((128, 1), 0x8000, np.uint16)
    onesr = np.ones((1, 128), bfl)
    etbl = np.zeros((128, 256), np.float32)
    for t in range(16):
        etbl[:, 16 * t + t] = 1.0
    return dict(eye=eye, eyeb=eyeb, ones1=ones1, ones2=ones2.astype(bfl),
                e2t=e2t.astype(bfl), c80=c80, onesr=onesr,
                etbl=etbl.astype(bfl))


def make_inmaps(L0_attn_x, L0_attn_w_qkv, L0_attn_w_proj, L0_ffn_w_gate,
                L0_ffn_w_up, L0_ffn_w_down):
    x = np.asarray(L0_attn_x, np.float32).reshape(2, 8192, 1024)
    consts = make_consts()
    wqkv = np.ascontiguousarray(np.asarray(L0_attn_w_qkv, np.float32))
    wproj = np.ascontiguousarray(np.asarray(L0_attn_w_proj, np.float32))
    wg = np.asarray(L0_ffn_w_gate, np.float32)
    wu = np.asarray(L0_ffn_w_up, np.float32)
    wd = np.asarray(L0_ffn_w_down, np.float32)
    in_maps = []
    for c in range(NCORES):
        b, t0 = c // 4, 2048 * (c % 4)
        m = dict(consts)
        m["x"] = np.ascontiguousarray(x[b, t0:t0 + 2048])
        m["wqkv"] = wqkv
        m["wproj"] = wproj
        m["wg_sl"] = np.ascontiguousarray(wg[:, 128 * c:128 * (c + 1)])
        m["wu_sl"] = np.ascontiguousarray(wu[:, 128 * c:128 * (c + 1)])
        m["wd_sl"] = np.ascontiguousarray(wd[:, 352 * c:352 * (c + 1)])
        in_maps.append(m)
    return in_maps


def kernel(L0_attn_x, L0_attn_w_qkv, L0_attn_w_proj, L0_ffn_w_gate,
           L0_ffn_w_up, L0_ffn_w_down):
    in_maps = make_inmaps(L0_attn_x, L0_attn_w_qkv, L0_attn_w_proj,
                          L0_ffn_w_gate, L0_ffn_w_up, L0_ffn_w_down)
    nc = build("full")
    res = run_bass_kernel_spmd(nc, in_maps, core_ids=list(range(NCORES)))
    outs = [np.asarray(res.results[c]["out"]) for c in range(NCORES)]
    full = np.stack([np.concatenate(outs[0:4], axis=0),
                     np.concatenate(outs[4:8], axis=0)]).astype(np.float32)
    return full


# version 07
# speedup vs baseline: 1.2125x; 1.2125x over previous
import contextlib
NS_ITERS = 6
import numpy as np
import ml_dtypes
import concourse.bass as bass
import concourse.mybir as mybir
from concourse.bass_utils import run_bass_kernel_spmd

F32 = mybir.dt.float32
F16 = mybir.dt.float16
BF16 = mybir.dt.bfloat16
U16 = mybir.dt.uint16
AF = mybir.ActivationFunctionType
ALU = mybir.AluOpType
AX = mybir.AxisListType
bfl = ml_dtypes.bfloat16

NCORES = 8
TPC = 2048
NT = 16
D = 1024
IF = 2816
NH, HD = 16, 64
ROUNDS = [(0, 512), (512, 1024), (1024, 1536), (1536, 2048), (2048, 2560), (2560, 2816)]
WSEG_G = 128 * IF
WSEG_U = 128 * IF
WSEG_D = 352 * D
WSEG = WSEG_G + WSEG_U + WSEG_D
PSEG = 128 * 520

MIDS = [0.25, 0.75, 1.25, 1.75, 2.5, 3.5, 5.0]
DELT = [0.5, 0.5, 0.5, 0.5, 1.0, 1.0, 2.0]
C_ABS16, C_MAX16, C_ADD16, C_MSK16 = 0x7FFF, 15360.0, 255.0, 0xFE00
C_T1, C_W1 = 13312.0, 14336.0
C_T2, C_W2 = 14848.0, 1024.0
C_SUB = 15360.0


class Sem:
    def __init__(self, es, nc, name):
        self.h = es.enter_context(nc.semaphore(name))
        self.v = 0


def build(stage="full"):
    nc = bass.Bass("TRN2", target_bir_lowering=False)
    ap = lambda name, shp, dt, kind: nc.dram_tensor(name, shp, dt, kind=kind).ap()
    X = ap("x", [TPC, D], F32, "ExternalInput")
    WQKV = ap("wqkv", [D, 3 * D], F32, "ExternalInput")
    WPROJ = ap("wproj", [D, D], F32, "ExternalInput")
    WGSL = ap("wg_sl", [IF, 128], F32, "ExternalInput")
    WUSL = ap("wu_sl", [IF, 128], F32, "ExternalInput")
    WDSL = ap("wd_sl", [D, 352], F32, "ExternalInput")
    EYE = ap("eye", [128, 128], BF16, "ExternalInput")
    EYEB = ap("eyeb", [128, 512], F32, "ExternalInput")
    ONES1 = ap("ones1", [128, 1], BF16, "ExternalInput")
    ONES2 = ap("ones2", [128, 2], BF16, "ExternalInput")
    E2T = ap("e2t", [2, 128], BF16, "ExternalInput")
    C80 = ap("c80", [128, 1], U16, "ExternalInput")
    ONESR = ap("onesr", [1, 128], BF16, "ExternalInput")
    ETBL = ap("etbl", [128, 256], BF16, "ExternalInput")
    OUT = ap("out", [TPC, D], F32, "ExternalOutput")
    if stage == "wq":
        DBG = ap("dbg", [NCORES * WSEG], BF16, "ExternalOutput")
    CIN_W = ap("cin_w", [WSEG], BF16, "Internal")
    COUT_W = ap("cout_w", [NCORES, WSEG], BF16, "Internal")
    CIN_Y = ap("cin_y", [16 * D], BF16, "Internal")
    COUT_Y = ap("cout_y", [4, 16 * D], BF16, "Internal")
    CIN_P = ap("cin_p", [PSEG], F32, "Internal")
    COUT_P = ap("cout_p", [4, PSEG], F32, "Internal")
    GROUPS4 = [[0, 1, 2, 3], [4, 5, 6, 7]]
    CG_V = CIN_W[0:WSEG_G].rearrange("(p c) -> p c", p=128)
    CU_V = CIN_W[WSEG_G:WSEG_G + WSEG_U].rearrange("(p c) -> p c", p=128)
    CD_V = CIN_W[WSEG_G + WSEG_U:WSEG].rearrange("(r h) -> r h", h=D)

    with contextlib.ExitStack() as es:
        E = es.enter_context
        sb = lambda name, shp, dt: E(nc.sbuf_tensor(name, shp, dt))
        STREAM = sb("stream", [128, NT * D], BF16)
        FMA = sb("fma", [128, 8 * TPC], BF16)
        WA = sb("wa", [128, 24576], BF16)
        C80T = sb("c80t", [128, 1], U16)
        EYET = sb("eyet", [128, 128], BF16)
        ONESRT = sb("onesrt", [1, 128], BF16)
        ETBLT = sb("etblt", [128, 256], BF16)
        O2T = sb("o2t", [128, 2], BF16)
        E2TTT = sb("e2ttt", [2, 128], BF16)
        EYBT = sb("eybt", [128, 512], F32)
        ONES1T = sb("ones1t", [128, 1], BF16)
        AR32 = sb("ar32", [128, 9600], F32)
        AR16 = sb("ar16", [128, 27200], U16)
        _o32 = [0]
        _o16 = [0]
        def av32(pcount, cols, off=None):
            if off is None:
                off = _o32[0]; _o32[0] = off + cols
            return AR32.ap()[0:pcount, off:off + cols]
        def av16(pcount, cols, dt, off=None):
            if off is None:
                off = _o16[0]; _o16[0] = off + cols
            u = AR16.ap()[0:pcount, off:off + cols]
            return u.bitcast(dt) if dt != U16 else u
        BN6 = av32(128, 12 * NT)
        BN2 = av32(128, 2 * NT)
        MNEG = av32(128, NT)
        RSTD = av32(128, NT)
        TMPV = av32(128, NT)
        SIL = [av32(128, 512), av32(128, 512)]
        H16 = [av32(128, 512), av32(128, 512)]
        XOUT_OFF32 = _o32[0]
        XOUT = av32(128, 1024)
        D16 = av32(128, 1024, off=XOUT_OFF32)
        QMX = av32(128, 64)
        QRC = av32(128, 64)
        QRS = av32(128, 64)
        QSC = av32(128, 64)
        OVL32 = _o32[0]
        WP = av32(128, 1408)
        QX = av32(128, 704)
        QC = av32(128, 704)
        QJ = av32(128, 704)
        QS = av32(128, 704)
        WMX = av32(128, IF // 16)
        RCP = av32(128, IF // 16)
        RS_ = av32(128, IF // 16)
        SC_ = av32(128, IF // 16)
        _o32[0] = OVL32
        E2SB_OFF32 = _o32[0]
        E2SB = av32(128, 512)
        SM8 = av32(128, 8)
        WSB_OFF32 = _o32[0]
        WSB = av32(128, 512)
        RDEN = av32(1, 16)
        RD128 = av32(128, 16)
        RDE = av32(128, 8)
        RDO = av32(128, 8)
        MRKV = av32(128, 1)
        MRKS = av32(128, 2, off=None)
        P_ACC = av32(128, 1024)
        SLOC4 = av32(128, 32)
        SLOC = av32(128, 8)
        RCPS = av32(128, 8)
        PCMP = av32(128, 520)
        PRB = av32(128, 520)
        DSB = av32(2, 512, off=E2SB_OFF32)
        DSR = av32(2, 512, off=WSB_OFF32)
        XN16 = av16(128, 1024, F16)
        TU1 = av16(128, 1024, U16)
        TU2 = av16(128, 1024, U16)
        TU3 = av16(128, 1024, U16)
        XQTM = av16(128, 1024, BF16)
        QH = [av16(128, 512, BF16), av16(128, 512, BF16)]
        QHT = [av16(128, 512, BF16), av16(128, 512, BF16)]
        OVL16 = _o16[0]
        WQT = av16(128, 3072, BF16)
        WCT = av16(128, 3072, BF16)
        _o16[0] = OVL16
        LNT_OFF = _o16[0]
        LNT = av16(128, 1024, BF16)
        LNF_OFF = _o16[0]
        LNF = av16(128, 8 * 512, BF16)
        YLM_OFF = _o16[0]
        YLM = av16(16, 1024, BF16)
        YLMF = av16(64, 1024, BF16)
        YLMT = av16(128, 512, BF16)
        QLMF = av16(128, 512, BF16)
        KLMF = av16(128, 512, BF16)
        K2N = av16(128, 512, BF16)
        K2ND = av16(128, 512, BF16)
        KTP = av16(64, 1024, BF16)
        VSB = av16(64, 1024, BF16)
        RDENB = av16(1, 16, BF16)
        KFM = av16(128, 8 * 512, BF16)
        SVF = KFM
        VTMS = av16(128, 4 * 1024, BF16)
        E3P = av16(128, 512, BF16, off=YLM_OFF)
        E3TS = av16(128, 512, BF16, off=YLM_OFF + 512)
        K3V = av16(128, 512, BF16, off=LNT_OFF)
        TSB = av16(128, 512, BF16, off=YLM_OFF + 1024)
        VTS = av16(128, 512, BF16, off=LNF_OFF)
        R1SB = av16(128, 512, BF16, off=LNF_OFF + 512)
        R1TP = av16(128, 512, BF16, off=LNF_OFF + 1024)
        R2SB = av16(128, 512, BF16, off=LNF_OFF + 1536)
        R3SB = av16(128, 512, BF16, off=LNF_OFF + 2048)
        E1P = av16(128, 512, BF16, off=LNF_OFF + 2560)
        RCDB = av16(2, 512, BF16, off=LNF_OFF + 3072)
        RSB = av16(128, 512, BF16, off=LNF_OFF + 3584)
        PGA = E(nc.psum_tensor("pga", [128, 1024], F32))
        PGB = E(nc.psum_tensor("pgb", [128, 1024], F32))
        PDN = E(nc.psum_tensor("pdn", [128, 1024], F32))
        PT = E(nc.psum_tensor("pt", [128, 512], BF16))
        PS = E(nc.psum_tensor("ps", [128, 512], F32))

        sd = Sem(es, nc, "sd")
        sv = Sem(es, nc, "sv")
        ss = Sem(es, nc, "ss")
        st = Sem(es, nc, "st")
        sg = Sem(es, nc, "sg")
        sfr = Sem(es, nc, "sfr")
        so = Sem(es, nc, "so")
        sx = Sem(es, nc, "sx")
        fs = Sem(es, nc, "fs")
        fg = Sem(es, nc, "fg")
        fu = Sem(es, nc, "fu")
        fd = Sem(es, nc, "fd")
        ft = Sem(es, nc, "ft")

        blk = E(nc.Block())

        @blk.sync
        def _(sync):
            sync.dma_start(out=C80T[:], in_=C80).then_inc(sd.h, 16)
            sync.dma_start(out=EYET[:], in_=EYE).then_inc(sd.h, 16)
            sync.dma_start(out=ONESRT[:], in_=ONESR).then_inc(sd.h, 16)
            sync.dma_start(out=ETBLT[:], in_=ETBL).then_inc(sd.h, 16)
            sync.dma_start(out=O2T[:], in_=ONES2).then_inc(sd.h, 16)
            sync.dma_start(out=E2TTT[:], in_=E2T).then_inc(sd.h, 16)
            sync.dma_start(out=EYBT[:], in_=EYEB).then_inc(sd.h, 16)
            sync.dma_start(out=ONES1T[:], in_=ONES1).then_inc(sd.h, 16)
        sd.v += 128
        x_ld = sx.v + 16
        @blk.gpsimd
        def _(gp):
            gp.dma_start(
                out=STREAM[:].rearrange("p (t f) -> p t f", f=D),
                in_=X.rearrange("(t p) f -> p t f", p=128)).then_inc(sx.h, 16)
        sx.v = x_ld

        def wprep(src3d, inner_w, nchunks, tr_specs, stage_fn, wp_free_sv):
            wqt_free_st = st.v
            prev_xn = wp_free_sv
            nh0 = nchunks // 2
            for half in (0, 1):
                nch = (nh0, nchunks - nh0)[half]
                ncols = nch * inner_w
                cb = nh0 * inner_w if half else 0
                ch0 = nh0 if half else 0
                ld = sd.v + 16
                @blk.sync
                def _(sync, ld=ld, ncols=ncols, ch0=ch0, nch=nch,
                      prev_xn=prev_xn):
                    sync.wait_ge(sv.h, prev_xn)
                    sync.dma_start(
                        out=WP[:, 0:ncols].rearrange("p (c f) -> p c f",
                                                     f=inner_w),
                        in_=src3d[:, ch0:ch0 + nch, :]).then_inc(sd.h, 16)
                sd.v = ld
                xn_t = sv.v + 1
                last = (half == 1)
                @blk.vector
                def _(v, ld=ld, ncols=ncols, cb=cb, last=last,
                      wqt_free_st=wqt_free_st, first=(half == 0)):
                    v.wait_ge(sd.h, ld)
                    if first:
                        v.wait_ge(st.h, wqt_free_st)
                    for h0 in range(0, ncols, 704):
                        hw = min(704, ncols - h0)
                        hb = hw // 16
                        b0 = (cb + h0) // 16
                        s3 = WP[:, h0:h0 + hw].rearrange("p (nb w) -> p nb w", w=16)
                        v.tensor_reduce(out=WMX[:, b0:b0 + hb], in_=s3, axis=AX.X,
                                        op=ALU.max, apply_absolute_value=True)
                        v.drain()
                        v.reciprocal(out=RCP[:, b0:b0 + hb], in_=WMX[:, b0:b0 + hb])
                        v.drain()
                        v.tensor_scalar(out=RS_[:, b0:b0 + hb], in0=RCP[:, b0:b0 + hb],
                                        scalar1=6.0, scalar2=None, op0=ALU.mult)
                        v.tensor_scalar(out=SC_[:, b0:b0 + hb], in0=WMX[:, b0:b0 + hb],
                                        scalar1=1.0 / 6.0, scalar2=None, op0=ALU.mult)
                        i = v.tensor_tensor(
                            out=QX[:, 0:hw].rearrange("p (nb w) -> p nb w", w=16),
                            in0=s3,
                            in1=RS_[:, b0:b0 + hb].unsqueeze(2).broadcast_to([128, hb, 16]),
                            op=ALU.mult)
                        if h0 + hw >= ncols:
                            i.then_inc(sv.h, 1)
                        xn = QX[:, 0:hw]
                        v.tensor_scalar(out=QS[:, 0:hw].bitcast(mybir.dt.int32),
                                        in0=xn.bitcast(mybir.dt.int32),
                                        scalar1=0x7FFFFFFF, scalar2=None,
                                        op0=ALU.bitwise_and)
                        axn = QS[:, 0:hw]
                        v.tensor_scalar(out=QC[:, 0:hw], in0=axn, scalar1=MIDS[0],
                                        scalar2=DELT[0], op0=ALU.is_gt, op1=ALU.mult)
                        for j in range(1, 7):
                            v.tensor_scalar(out=QJ[:, 0:hw], in0=axn, scalar1=MIDS[j],
                                            scalar2=DELT[j], op0=ALU.is_gt, op1=ALU.mult)
                            v.tensor_tensor(out=QC[:, 0:hw], in0=QC[:, 0:hw],
                                            in1=QJ[:, 0:hw], op=ALU.add)
                        v.scalar_tensor_tensor(out=QJ[:, 0:hw], in0=xn, scalar=0.0,
                                               in1=QC[:, 0:hw], op0=ALU.is_gt,
                                               op1=ALU.mult)
                        v.scalar_tensor_tensor(out=QC[:, 0:hw], in0=QJ[:, 0:hw],
                                               scalar=2.0, in1=QC[:, 0:hw],
                                               op0=ALU.mult, op1=ALU.subtract)
                        v.tensor_tensor(
                            out=WQT[:, cb + h0:cb + h0 + hw]
                            .rearrange("p (nb w) -> p nb w", w=16),
                            in0=QC[:, 0:hw].rearrange("p (nb w) -> p nb w", w=16),
                            in1=SC_[:, b0:b0 + hb].unsqueeze(2).broadcast_to([128, hb, 16]),
                            op=ALU.mult)
                    if last:
                        v.tensor_scalar(out=MRKV[:], in0=MRKV[:], scalar1=1.0, scalar2=None, op0=ALU.mult).then_inc(sv.h, 1)
                sv.v = xn_t + (1 if last else 0)
                prev_xn = xn_t
            q_t = sv.v
            wct_free_so = so.v
            for g0 in range(0, len(tr_specs), 4):
                grp = tr_specs[g0:g0 + 4]
                t_t = st.v + 1
                f_base = sfr.v
                @blk.tensor
                def _(t, grp=grp, q_t=q_t, f_base=f_base):
                    t.wait_ge(sv.h, q_t)
                    t.wait_ge(sfr.h, f_base)
                    i = None
                    for j, (c0, wdt, _d) in enumerate(grp):
                        i = t.transpose(out=PT[0:wdt, 128 * j:128 * j + 128],
                                        in_=WQT[:, c0:c0 + wdt],
                                        identity=EYET[:])
                    i.then_inc(st.h, 1)
                st.v = t_t
                e_t = sfr.v + 1
                @blk.scalar
                def _(s, grp=grp, t_t=t_t, wct_free_so=wct_free_so):
                    s.wait_ge(st.h, t_t)
                    s.wait_ge(so.h, wct_free_so)
                    i = None
                    for j, (c0, wdt, d0) in enumerate(grp):
                        i = s.copy(out=WCT[0:wdt, d0:d0 + 128],
                                   in_=PT[0:wdt, 128 * j:128 * j + 128])
                    i.then_inc(sfr.h, 1)
                sfr.v = e_t
            fr_done = sfr.v
            stage_fn(fr_done)
            return prev_xn

        gate_specs = [(128 * c, 128, 128 * c) for c in range(22)]
        def stage_gate(fr):
            t = so.v + 16
            @blk.gpsimd
            def _(gp, fr=fr):
                gp.wait_ge(sfr.h, fr)
                gp.dma_start(out=CG_V, in_=WCT[:, 0:IF]).then_inc(so.h, 16)
            so.v = t
        g_xn = wprep(WGSL.rearrange("(c p) f -> p c f", p=128), 128, 22,
                     gate_specs, stage_gate, 0)

        def stage_up(fr):
            t = so.v + 16
            @blk.gpsimd
            def _(gp, fr=fr):
                gp.wait_ge(sfr.h, fr)
                gp.dma_start(out=CU_V, in_=WCT[:, 0:IF]).then_inc(so.h, 16)
            so.v = t
        u_xn = wprep(WUSL.rearrange("(c p) f -> p c f", p=128), 128, 22,
                     up_specs := [(128 * c, 128, 128 * c) for c in range(22)],
                     stage_up, g_xn)

        dn_specs = []
        dn_pieces = []
        pc = 0
        for hch in range(8):
            for ic in range(3):
                wdt = 128 if ic < 2 else 96
                dn_specs.append((352 * hch + 128 * ic, wdt, 128 * pc))
                dn_pieces.append((128 * pc, wdt, 128 * ic, 128 * hch))
                pc += 1
        def stage_dn(fr):
            t = so.v + 16 * len(dn_pieces)
            @blk.gpsimd
            def _(gp, fr=fr):
                gp.wait_ge(sfr.h, fr)
                for (wc0, rows, r0, h0) in dn_pieces:
                    gp.dma_start(out=CD_V[r0:r0 + rows, h0:h0 + 128],
                                 in_=WCT[0:rows, wc0:wc0 + 128]).then_inc(so.h, 16)
            so.v = t
        d_xn = wprep(WDSL.rearrange("(c p) f -> p c f", p=128), 352, 8,
                     dn_specs, stage_dn, u_xn)

        ag_t = sg.v + 1
        agw_stage_done = so.v
        @blk.gpsimd
        def _(gp, agw=agw_stage_done):
            gp.wait_ge(so.h, agw)
            gp.collective_compute(
                "AllGather", ALU.bypass, ins=[CIN_W], outs=[COUT_W],
                replica_groups=[[0, 1, 2, 3, 4, 5, 6, 7]],
            ).then_inc(sg.h, 1)
        sg.v = ag_t

        if stage == "wq":
            @blk.sync
            def _(sync, ag_t=ag_t):
                sync.wait_ge(sg.h, ag_t)
                sync.dma_start(out=DBG, in_=COUT_W.flatten()).then_inc(so.h, 16)
                sync.wait_ge(so.h, so.v + 16)
            so.v += 16
            return nc

        def GV(r):
            return COUT_W[r, 0:WSEG_G].rearrange("(p c) -> p c", p=128)
        def UV(r):
            return COUT_W[r, WSEG_G:WSEG_G + WSEG_U].rearrange("(p c) -> p c", p=128)

        def q16(v, src_ap, dst_ap, ncols, sc2, rstd_ap=None):
            nb = ncols // 16
            s3 = src_ap.rearrange("p (nb w) -> p nb w", w=16)
            v.tensor_reduce(out=QMX[:, 0:nb], in_=s3, axis=AX.X, op=ALU.max,
                            apply_absolute_value=True)
            v.drain()
            v.reciprocal(out=QRC[:, 0:nb], in_=QMX[:, 0:nb])
            v.drain()
            v.tensor_scalar(out=QRS[:, 0:nb], in0=QRC[:, 0:nb], scalar1=6.0,
                            scalar2=None, op0=ALU.mult)
            if rstd_ap is not None:
                v.tensor_scalar(out=QSC[:, 0:nb], in0=QMX[:, 0:nb],
                                scalar1=rstd_ap, scalar2=sc2,
                                op0=ALU.mult, op1=ALU.mult)
            else:
                v.tensor_scalar(out=QSC[:, 0:nb], in0=QMX[:, 0:nb], scalar1=sc2,
                                scalar2=None, op0=ALU.mult)
            v.tensor_tensor(
                out=XN16[:, 0:ncols].rearrange("p (nb w) -> p nb w", w=16),
                in0=s3,
                in1=QRS[:, 0:nb].unsqueeze(2).broadcast_to([128, nb, 16]),
                op=ALU.mult)
            xb = XN16[:, 0:ncols].bitcast(U16)
            t1, t2, t3 = TU1[:, 0:ncols], TU2[:, 0:ncols], TU3[:, 0:ncols]
            v.tensor_scalar(out=t1, in0=xb, scalar1=C_ABS16, scalar2=None,
                            op0=ALU.bitwise_and)
            v.tensor_scalar(out=t2, in0=t1, scalar1=C_MAX16, scalar2=C_ADD16,
                            op0=ALU.max, op1=ALU.add)
            v.tensor_scalar(out=t2, in0=t2, scalar1=C_MSK16, scalar2=None,
                            op0=ALU.bitwise_and)
            v.tensor_scalar(out=t3, in0=t1, scalar1=C_T2, scalar2=C_W2,
                            op0=ALU.is_gt, op1=ALU.mult)
            v.tensor_scalar(out=t1, in0=t1, scalar1=C_T1, scalar2=C_W1,
                            op0=ALU.is_gt, op1=ALU.mult)
            v.tensor_tensor(out=t1, in0=t1, in1=t3, op=ALU.add)
            v.scalar_tensor_tensor(out=t1, in0=t2, scalar=C_SUB, in1=t1,
                                   op0=ALU.subtract, op1=ALU.add)
            v.scalar_tensor_tensor(out=t1, in0=xb, scalar=C80T[:], in1=t1,
                                   op0=ALU.bitwise_and, op1=ALU.bitwise_or)
            return v.tensor_tensor(
                out=dst_ap.rearrange("p (nb w) -> p nb w", w=16),
                in0=t1.bitcast(F16).rearrange("p (nb w) -> p nb w", w=16),
                in1=QSC[:, 0:nb].unsqueeze(2).broadcast_to([128, nb, 16]),
                op=ALU.mult)

        def emit_ln_xq(ready_sems):
            bn_t = sv.v + 1
            @blk.vector
            def _(v, rd=list(ready_sems)):
                for s_, val in rd:
                    v.wait_ge(s_.h, val)
                for t in range(NT):
                    v.bn_stats(out=BN6[:, 12 * t:12 * t + 6],
                               in_=STREAM[:, 1024 * t:1024 * t + 512])
                    v.bn_stats(out=BN6[:, 12 * t + 6:12 * t + 12],
                               in_=STREAM[:, 1024 * t + 512:1024 * t + 1024])
                v.drain()
                for t in range(NT):
                    v.bn_aggr(out=BN2[:, 2 * t:2 * t + 2],
                              in_=BN6[:, 12 * t:12 * t + 12])
                v.drain()
                b2 = BN2[:].rearrange("p (t two) -> p t two", two=2)
                v.tensor_scalar(out=MNEG[:], in0=b2[:, :, 0], scalar1=-1.0,
                                scalar2=None, op0=ALU.mult)
                v.tensor_scalar(out=TMPV[:], in0=b2[:, :, 1], scalar1=1e-5,
                                scalar2=None, op0=ALU.add)
                v.drain()
                v.reciprocal(out=TMPV[:], in_=TMPV[:])
                v.drain()
                v.tensor_scalar(out=TMPV[:], in0=TMPV[:], scalar1=1.0,
                                scalar2=None, op0=ALU.mult).then_inc(sv.h, 1)
            sv.v = bn_t
            sq_t = ss.v + 1
            @blk.scalar
            def _(s, bn_t=bn_t):
                s.wait_ge(sv.h, bn_t)
                s.activation(out=RSTD[:], in_=TMPV[:],
                             func=AF.Sqrt).then_inc(ss.h, 1)
            ss.v = sq_t
            for t in range(NT):
                q_t = sv.v + 1
                prev_tr = st.v
                @blk.vector
                def _(v, t=t, sq_t=sq_t, prev_tr=prev_tr):
                    if t == 0:
                        v.wait_ge(ss.h, sq_t)
                    else:
                        v.wait_ge(st.h, prev_tr)
                    v.tensor_scalar(out=D16[:], in0=STREAM[:, 1024 * t:1024 * (t + 1)],
                                    scalar1=MNEG[:, t:t + 1], scalar2=None,
                                    op0=ALU.add)
                    q16(v, D16[:], XQTM[:], 1024, 1.0 / 6.0,
                        rstd_ap=RSTD[:, t:t + 1]).then_inc(sv.h, 1)
                sv.v = q_t
                for g in range(2):
                    t_t = st.v + 1
                    ftw = ft.v
                    @blk.tensor
                    def _(tt, t=t, g=g, q_t=q_t, ftw=ftw):
                        tt.wait_ge(sv.h, q_t)
                        tt.wait_ge(ft.h, ftw)
                        i = None
                        for j in range(4):
                            c = 4 * g + j
                            i = tt.transpose(out=PT[:, 128 * j:128 * (j + 1)],
                                             in_=XQTM[:, 128 * c:128 * (c + 1)],
                                             identity=EYET[:])
                        i.then_inc(st.h, 1)
                    st.v = t_t
                    e_t = ft.v + 1
                    @blk.vector
                    def _(v, t=t, g=g, t_t=t_t):
                        v.wait_ge(st.h, t_t)
                        dst = FMA[:].rearrange("p (k r) -> p k r", r=2048) \
                            [:, 4 * g:4 * g + 4, 128 * t:128 * (t + 1)]
                        v.tensor_copy(dst, PT[:].rearrange(
                            "p (k r) -> p k r", r=128)).then_inc(ft.h, 1)
                    ft.v = e_t
            return ft.v

        def emit_ffn(xq_ft, wa_free_st, write_out):
            round_done = [wa_free_st, wa_free_st]
            par_free = {0: fg, 1: fu}
            par_psum = {0: PGA, 1: PGB}
            for r, (o0, o1) in enumerate(ROUNDS):
                Wr = o1 - o0
                ch = Wr // 128
                wb = 12288 * (r % 2)
                @blk.sync
                def _(sync, o0=o0, o1=o1, Wr=Wr, ch=ch, wb=wb,
                      wa_free=round_done[r % 2], ag_t=ag_t):
                    sync.wait_ge(sg.h, ag_t)
                    sync.wait_ge(st.h, wa_free)
                    for k in range(8):
                        sync.dma_start(out=WA[:, wb + 512 * k:wb + 512 * k + Wr],
                                       in_=GV(k)[:, o0:o1]).then_inc(sd.h, 16)
                        sync.dma_start(out=WA[:, wb + 4096 + 512 * k:wb + 4096 + 512 * k + Wr],
                                       in_=UV(k)[:, o0:o1]).then_inc(sd.h, 16)
                    for j in range(ch):
                        g0 = o0 + 128 * j
                        left = 128
                        while left > 0:
                            rk = g0 // 352
                            loc = g0 - 352 * rk
                            n = min(left, 352 - loc)
                            srcw = COUT_W[rk, WSEG_G + WSEG_U + loc * D:
                                          WSEG_G + WSEG_U + (loc + n) * D] \
                                .rearrange("(q h) -> q h", h=D)
                            p0 = 128 - left
                            sync.dma_start(
                                out=WA[p0:p0 + n, wb + 8192 + 1024 * j:wb + 8192 + 1024 * (j + 1)],
                                in_=srcw).then_inc(sd.h, 16)
                            g0 += n
                            left -= n
                ndma = 16
                for j in range(ch):
                    g0 = o0 + 128 * j
                    left = 128
                    while left > 0:
                        rk = g0 // 352
                        n = min(left, 352 - (g0 - 352 * rk))
                        g0 += n
                        left -= n
                        ndma += 1
                sd.v += 16 * ndma
                wload = sd.v
                def emit_gateup(t):
                    par = t % 2
                    PP = par_psum[par]
                    fsem = par_free[par]
                    mm_t = st.v + 1
                    fw = fsem.v
                    @blk.tensor
                    def _(tb, t=t, PP=PP, fsem=fsem, fw=fw, Wr=Wr, wb=wb,
                          wload=wload):
                        tb.wait_ge(sd.h, wload)
                        tb.wait_ge(ft.h, xq_ft)
                        tb.wait_ge(fsem.h, fw)
                        i = None
                        for k in range(8):
                            lhs = FMA[:, 2048 * k + 128 * t:2048 * k + 128 * (t + 1)]
                            i = tb.matmul(out=PP[:, 0:Wr], lhsT=lhs,
                                          rhs=WA[:, wb + 512 * k:wb + 512 * k + Wr],
                                          start=(k == 0), stop=(k == 7))
                            i = tb.matmul(out=PP[:, 512:512 + Wr], lhsT=lhs,
                                          rhs=WA[:, wb + 4096 + 512 * k:wb + 4096 + 512 * k + Wr],
                                          start=(k == 0), stop=(k == 7))
                        i.then_inc(st.h, 1)
                    st.v = mm_t
                    return mm_t

                def emit_silu_h(t, mm_t):
                    par = t % 2
                    PP = par_psum[par]
                    fsem = par_free[par]
                    sil_t = ss.v + 1
                    @blk.scalar
                    def _(s, PP=PP, par=par, mm_t=mm_t, Wr=Wr):
                        s.wait_ge(st.h, mm_t)
                        s.activation(out=SIL[par][:, 0:Wr], in_=PP[:, 0:Wr],
                                     func=AF.Silu).then_inc(ss.h, 1)
                    ss.v = sil_t
                    h_t = sv.v + 1
                    @blk.vector
                    def _(v, PP=PP, par=par, fsem=fsem, sil_t=sil_t, Wr=Wr):
                        v.wait_ge(ss.h, sil_t)
                        i = v.tensor_tensor(out=H16[par][:, 0:Wr],
                                            in0=SIL[par][:, 0:Wr],
                                            in1=PP[:, 512:512 + Wr], op=ALU.mult)
                        fsem.v += 1
                        i.then_inc(fsem.h, 1)
                        q16(v, H16[par][:, 0:Wr], QH[par][:, 0:Wr], Wr, 1.0 / 12.0) \
                            .then_inc(sv.h, 1)
                    sv.v = h_t
                    return h_t

                def emit_tail(t, h_t):
                    par = t % 2
                    t_t = st.v + 1
                    ftw = ft.v
                    @blk.tensor
                    def _(tb, par=par, h_t=h_t, ftw=ftw, ch=ch):
                        tb.wait_ge(sv.h, h_t)
                        tb.wait_ge(ft.h, ftw)
                        i = None
                        for j in range(ch):
                            i = tb.transpose(out=PT[:, 128 * j:128 * (j + 1)],
                                             in_=QH[par][:, 128 * j:128 * (j + 1)],
                                             identity=EYET[:])
                        i.then_inc(st.h, 1)
                    st.v = t_t
                    e_t = ft.v + 1
                    @blk.vector
                    def _(v, par=par, t_t=t_t, ch=ch):
                        v.wait_ge(st.h, t_t)
                        v.tensor_copy(QHT[par][:, 0:128 * ch],
                                      PT[:, 0:128 * ch]).then_inc(ft.h, 1)
                    ft.v = e_t
                    dn_t = st.v + 1
                    fdw = fd.v
                    @blk.tensor
                    def _(tb, par=par, e_t=e_t, fdw=fdw, ch=ch, wb=wb):
                        tb.wait_ge(ft.h, e_t)
                        tb.wait_ge(fd.h, fdw)
                        i = None
                        for kc in range(ch):
                            for (a, bnd) in ((0, 512), (512, 1024)):
                                i = tb.matmul(
                                    out=PDN[:, a:bnd],
                                    lhsT=QHT[par][:, 128 * kc:128 * (kc + 1)],
                                    rhs=WA[:, wb + 8192 + 1024 * kc + a:wb + 8192 + 1024 * kc + bnd],
                                    start=(kc == 0), stop=(kc == ch - 1))
                        i.then_inc(st.h, 1)
                    st.v = dn_t
                    return dn_t

                def emit_add(t, dn_t):
                        add_t = sv.v + 1
                        final = write_out is not None and r == len(ROUNDS) - 1
                        @blk.vector
                        def _(v, t=t, dn_t=dn_t, final=final):
                            v.wait_ge(st.h, dn_t)
                            strm = STREAM[:, 1024 * t:1024 * (t + 1)]
                            if final:
                                i = v.tensor_tensor(out=XOUT[:], in0=PDN[:, 0:1024],
                                                        in1=strm, op=ALU.add)
                            else:
                                i = v.tensor_tensor(out=strm, in0=PDN[:, 0:1024],
                                                        in1=strm, op=ALU.add)
                            fd.v += 1
                            i.then_inc(fd.h, 1)
                            v.tensor_scalar(out=MRKV[:], in0=MRKV[:], scalar1=1.0,
                                                scalar2=None, op0=ALU.mult).then_inc(sv.h, 1)
                        sv.v = add_t
                        if final:
                            o_t = so.v + 16
                            @blk.sync
                            def _(sync, t=t, add_t=add_t, write_out=write_out):
                                sync.wait_ge(sv.h, add_t)
                                sync.dma_start(out=write_out[128 * t:128 * (t + 1), :],
                                                   in_=XOUT[:]).then_inc(so.h, 16)
                            so.v = o_t
                            xg = so.v
                            @blk.vector
                            def _(v, xg=xg):
                                v.wait_ge(so.h, xg)

                prev = None
                pend_add = None
                for t in range(NT + 1):
                    if t < NT:
                        mm_t = emit_gateup(t)
                    if prev is not None:
                        pend = (prev[0], emit_tail(prev[0], prev[1]))
                    else:
                        pend = None
                    if t < NT:
                        h_t = emit_silu_h(t, mm_t)
                        prev = (t, h_t)
                    if pend is not None:
                        emit_add(pend[0], pend[1])
                round_done[r % 2] = st.v
            return st.v

        def emit_attn(stream_sv, upto="all"):
            ps_u = [0]
            def ps_wait(eng):
                eng.wait_ge(fs.h, ps_u[0])
            wqkv_ld = sg.v + 16
            wa_free = st.v
            @blk.gpsimd
            def _(gp, wa_free=wa_free):
                gp.wait_ge(st.h, wa_free)
                gp.dma_start(
                    out=WA[:].rearrange("p (k c) -> p k c", c=3 * D),
                    in_=WQKV.rearrange("(k p) c -> p k c", p=128)).then_inc(sg.h, 16)
            sg.v = wqkv_ld
            bn_t = sv.v + 1
            @blk.vector
            def _(v, stream_sv=stream_sv):
                v.wait_ge(sv.h, stream_sv)
                for t in range(NT):
                    v.bn_stats(out=BN6[:, 12 * t:12 * t + 6],
                               in_=STREAM[:, 1024 * t:1024 * t + 512])
                    v.bn_stats(out=BN6[:, 12 * t + 6:12 * t + 12],
                               in_=STREAM[:, 1024 * t + 512:1024 * t + 1024])
                v.drain()
                for t in range(NT):
                    v.bn_aggr(out=BN2[:, 2 * t:2 * t + 2],
                              in_=BN6[:, 12 * t:12 * t + 12])
                v.drain()
                b2 = BN2[:].rearrange("p (t two) -> p t two", two=2)
                v.tensor_scalar(out=MNEG[:], in0=b2[:, :, 0], scalar1=-1.0,
                                scalar2=None, op0=ALU.mult)
                v.tensor_scalar(out=TMPV[:], in0=b2[:, :, 1], scalar1=1e-5,
                                scalar2=None, op0=ALU.add)
                v.drain()
                v.reciprocal(out=TMPV[:], in_=TMPV[:])
                v.drain()
                v.tensor_scalar(out=TMPV[:], in0=TMPV[:], scalar1=1.0,
                                scalar2=None, op0=ALU.mult).then_inc(sv.h, 1)
            sv.v = bn_t
            sq_t = ss.v + 1
            @blk.scalar
            def _(s, bn_t=bn_t):
                s.wait_ge(sv.h, bn_t)
                s.activation(out=RSTD[:], in_=TMPV[:],
                             func=AF.Sqrt).then_inc(ss.h, 1)
            ss.v = sq_t
            for t in range(NT):
                ln_t = sv.v + 1
                @blk.vector
                def _(v, t=t, sq_t=sq_t):
                    if t == 0:
                        v.wait_ge(ss.h, sq_t)
                    v.tensor_scalar(out=LNT[:], in0=STREAM[:, 1024 * t:1024 * (t + 1)],
                                    scalar1=MNEG[:, t:t + 1], scalar2=RSTD[:, t:t + 1],
                                    op0=ALU.add, op1=ALU.mult).then_inc(sv.h, 1)
                sv.v = ln_t
                ym_t = st.v + 1
                fgw = fg.v
                @blk.tensor
                def _(tt, t=t, ln_t=ln_t, fgw=fgw):
                    tt.wait_ge(sv.h, ln_t)
                    if t == 0:
                        tt.wait_ge(fg.h, fgw)
                    i = tt.matmul(out=PGA[0:16, 0:512], lhsT=ETBLT[:, 16 * t:16 * (t + 1)],
                                  rhs=LNT[:, 0:512], start=(t == 0), stop=(t == NT - 1))
                    i = tt.matmul(out=PGA[0:16, 512:1024], lhsT=ETBLT[:, 16 * t:16 * (t + 1)],
                                  rhs=LNT[:, 512:1024], start=(t == 0), stop=(t == NT - 1))
                    i.then_inc(st.h, 1)
                st.v = ym_t
                @blk.vector
                def _(v, ym_t=ym_t):
                    v.wait_ge(st.h, ym_t)
            ylm_t = sv.v + 1
            @blk.vector
            def _(v):
                i = v.tensor_scalar(out=YLM[:], in0=PGA[0:16, 0:1024],
                                    scalar1=1.0 / 128.0, scalar2=None, op0=ALU.mult)
                fg.v += 1
                i.then_inc(fg.h, 1)
                v.tensor_scalar(out=MRKV[:], in0=MRKV[:], scalar1=1.0, scalar2=None, op0=ALU.mult).then_inc(sv.h, 1)
            sv.v = ylm_t
            if upto == "p1":
                return sv.v
            agy_t = sg.v + 1
            @blk.gpsimd
            def _(gp, ylm_t=ylm_t):
                gp.wait_ge(sv.h, ylm_t)
                gp.dma_start(out=CIN_Y.rearrange("(p c) -> p c", p=16),
                             in_=YLM[:]).then_inc(sg.h, 16)
                gp.wait_ge(sg.h, sg.v + 16)
                gp.collective_compute(
                    "AllGather", ALU.bypass, ins=[CIN_Y], outs=[COUT_Y],
                    replica_groups=GROUPS4).then_inc(sg.h, 1)
            sg.v += 17
            agy_done = sg.v
            ylmf_ld = sd.v + 16
            @blk.sync
            def _(sync, agy_done=agy_done):
                sync.wait_ge(sg.h, agy_done)
                sync.dma_start(out=YLMF[:],
                               in_=COUT_Y.rearrange("r (p c) -> (r p) c", c=D)
                               ).then_inc(sd.h, 16)
            sd.v = ylmf_ld
            ftw = ft.v
            yt_t = st.v + 1
            @blk.tensor
            def _(tt, ylmf_ld=ylmf_ld, ftw=ftw):
                tt.wait_ge(sd.h, ylmf_ld)
                tt.wait_ge(ft.h, ftw)
                i = None
                for c in range(8):
                    i = tt.transpose(out=PT[:, 64 * c:64 * (c + 1)],
                                     in_=YLMF[0:64, 128 * c:128 * (c + 1)],
                                     identity=EYET[0:64, 0:64])
                i.then_inc(st.h, 1)
            st.v = yt_t
            ye_t = ft.v + 1
            @blk.scalar
            def _(s, yt_t=yt_t):
                s.wait_ge(st.h, yt_t)
                s.copy(out=YLMT[:], in_=PT[:]).then_inc(ft.h, 1)
            ft.v = ye_t
            if upto == "p2":
                @blk.vector
                def _(v, ye_t=ye_t):
                    v.wait_ge(ft.h, ye_t)
                    v.tensor_scalar(out=MRKV[:], in0=MRKV[:], scalar1=1.0, scalar2=None, op0=ALU.mult).then_inc(sv.h, 1)
                sv.v += 1
                return sv.v
            for dstT, cof in ((QLMF, 0), (KLMF, 1024)):
                mm_t = st.v + 1
                @blk.tensor
                def _(tt, dstT=dstT, cof=cof, ye_t=ye_t, wqkv_ld=wqkv_ld,
                      psu=ps_u[0]):
                    tt.wait_ge(ft.h, ye_t)
                    tt.wait_ge(sg.h, wqkv_ld)
                    tt.wait_ge(fs.h, psu)
                    i = None
                    for och in range(8):
                        for k in range(8):
                            i = tt.matmul(
                                out=PS[:, 64 * och:64 * (och + 1)],
                                lhsT=WA[:, 3072 * k + cof + 128 * och:
                                        3072 * k + cof + 128 * (och + 1)],
                                rhs=YLMT[:, 64 * k:64 * (k + 1)],
                                start=(k == 0), stop=(k == 7))
                    i.then_inc(st.h, 1)
                st.v = mm_t
                ev_t = ss.v + 1
                @blk.scalar
                def _(s, dstT=dstT, mm_t=mm_t):
                    s.wait_ge(st.h, mm_t)
                    i = s.copy(out=dstT[:], in_=PS[:])
                    fs.v += 1
                    i.then_inc(fs.h, 1)
                    s.copy(out=MRKS[:, 0:1], in_=MRKS[:, 1:2]).then_inc(ss.h, 1)
                ss.v = ev_t
                ps_u[0] = fs.v
            lm_ss = ss.v
            if upto == "p3":
                @blk.vector
                def _(v, lm_ss=lm_ss):
                    v.wait_ge(ss.h, lm_ss)
                    v.tensor_scalar(out=MRKV[:], in0=MRKV[:], scalar1=1.0, scalar2=None, op0=ALU.mult).then_inc(sv.h, 1)
                sv.v += 1
                return sv.v
            z2_t = st.v + 1
            @blk.tensor
            def _(tt, lm_ss=lm_ss, psu=ps_u[0]):
                tt.wait_ge(ss.h, lm_ss)
                tt.wait_ge(fs.h, psu)
                i = None
                for h in range(16):
                    p, hh = h // 2, 64 * (h % 2)
                    i = tt.matmul(
                        out=PS[hh:hh + 64, 64 * p:64 * (p + 1)],
                        lhsT=QLMF[hh:hh + 64, 64 * (h // 2):64 * (h // 2) + 64],
                        rhs=KLMF[hh:hh + 64, 64 * (h // 2):64 * (h // 2) + 64],
                        start=True, stop=True)
                i.then_inc(st.h, 1)
            st.v = z2_t
            e2_t = ss.v + 1
            @blk.scalar
            def _(s, z2_t=z2_t):
                s.wait_ge(st.h, z2_t)
                s.activation(out=E2SB[:], in_=PS[:], func=AF.Exp,
                             scale=0.125).then_inc(ss.h, 1)
            ss.v = e2_t
            k2_t = sv.v + 1
            @blk.vector
            def _(v, e2_t=e2_t):
                v.wait_ge(ss.h, e2_t)
                v.tensor_scalar(out=MRKV[:], in0=MRKV[:], scalar1=1.0, scalar2=None, op0=ALU.mult).then_inc(fs.h, 1)
                v.tensor_reduce(out=SM8[:],
                                in_=E2SB[:].rearrange("p (pr c) -> p pr c", c=64),
                                axis=AX.X, op=ALU.add)
                v.drain()
                v.reciprocal(out=SM8[:], in_=SM8[:])
                v.drain()
                v.tensor_tensor(out=K2N[:].rearrange("p (pr c) -> p pr c", c=64),
                                in0=E2SB[:].rearrange("p (pr c) -> p pr c", c=64),
                                in1=SM8[:].unsqueeze(2).broadcast_to([128, 8, 64]),
                                op=ALU.mult).then_inc(sv.h, 1)
            fs.v += 1
            ps_u[0] = fs.v
            sv.v = k2_t
            if upto == "p3a":
                @blk.vector
                def _(v, base=sv.v):
                    v.wait_ge(sv.h, base)
                    v.tensor_scalar(out=MRKV[:], in0=MRKV[:], scalar1=1.0, scalar2=None, op0=ALU.mult).then_inc(sv.h, 1)
                sv.v += 1
                return sv.v
            fdw = fd.v
            dn_t = st.v + 1
            @blk.tensor
            def _(tt, k2_t=k2_t, fdw=fdw):
                tt.wait_ge(sv.h, k2_t)
                tt.wait_ge(fd.h, fdw)
                i = None
                for h in range(16):
                    p, c = h // 2, h % 2
                    i = tt.matmul(out=PDN[0:1, 64 * h:64 * (h + 1)],
                                  lhsT=O2T[:, c:c + 1],
                                  rhs=K2N[:, 64 * p:64 * (p + 1)],
                                  start=True, stop=True)
                i.then_inc(st.h, 1)
            st.v = dn_t
            rd_t = sv.v + 1
            @blk.vector
            def _(v, dn_t=dn_t):
                v.wait_ge(st.h, dn_t)
                v.tensor_reduce(out=RDEN[:],
                                in_=PDN[0:1, :].rearrange("p (h c) -> p h c", c=64),
                                axis=AX.X, op=ALU.max)
                fd.v += 1
                v.tensor_scalar(out=MRKV[:], in0=MRKV[:], scalar1=1.0, scalar2=None, op0=ALU.mult).then_inc(fd.h, 1)
                v.drain()
                v.reciprocal(out=RDEN[:], in_=RDEN[:])
                v.drain()
                v.tensor_scalar(out=RDENB[:], in0=RDEN[:], scalar1=1.0,
                                scalar2=None, op0=ALU.mult).then_inc(sv.h, 1)
            sv.v = rd_t
            if upto == "p3b":
                @blk.vector
                def _(v, base=sv.v):
                    v.wait_ge(sv.h, base)
                    v.tensor_scalar(out=MRKV[:], in0=MRKV[:], scalar1=1.0, scalar2=None, op0=ALU.mult).then_inc(sv.h, 1)
                sv.v += 1
                return sv.v
            rdm_t = st.v + 1
            @blk.tensor
            def _(tt, rd_t=rd_t, psu=ps_u[0]):
                tt.wait_ge(sv.h, rd_t)
                tt.wait_ge(fs.h, psu)
                tt.matmul(out=PS[:, 0:16], lhsT=ONESRT[:], rhs=RDENB[:],
                          start=True, stop=True).then_inc(st.h, 1)
            st.v = rdm_t
            v0_t = sv.v + 1
            @blk.vector
            def _(v, rdm_t=rdm_t):
                v.wait_ge(st.h, rdm_t)
                i = v.tensor_scalar(out=RD128[:], in0=PS[:, 0:16], scalar1=1.0,
                                    scalar2=None, op0=ALU.mult)
                fs.v += 1
                i.then_inc(fs.h, 1)
                v.drain()
                v.tensor_scalar(out=RDE[:], in0=RD128[:].rearrange(
                    "p (pr two) -> p pr two", two=2)[:, :, 0],
                    scalar1=1.0, scalar2=None, op0=ALU.mult)
                v.tensor_scalar(out=RDO[:], in0=RD128[:].rearrange(
                    "p (pr two) -> p pr two", two=2)[:, :, 1],
                    scalar1=1.0, scalar2=None, op0=ALU.mult)
                v.drain()
                v.tensor_tensor(
                    out=K2ND[0:64, :].rearrange("p (pr c) -> p pr c", c=64),
                    in0=K2N[0:64, :].rearrange("p (pr c) -> p pr c", c=64),
                    in1=RDE[0:64, :].unsqueeze(2).broadcast_to([64, 8, 64]),
                    op=ALU.mult)
                v.tensor_tensor(
                    out=K2ND[64:128, :].rearrange("p (pr c) -> p pr c", c=64),
                    in0=K2N[64:128, :].rearrange("p (pr c) -> p pr c", c=64),
                    in1=RDO[64:128, :].unsqueeze(2).broadcast_to([64, 8, 64]),
                    op=ALU.mult).then_inc(sv.h, 1)
            ps_u[0] = fs.v
            sv.v = v0_t
            if upto == "p4":
                @blk.vector
                def _(v, v0_t=v0_t):
                    v.wait_ge(sv.h, v0_t)
                    v.tensor_scalar(out=MRKV[:], in0=MRKV[:], scalar1=1.0, scalar2=None, op0=ALU.mult).then_inc(sv.h, 1)
                sv.v += 1
                return sv.v
            for srcT, dstT in ((K2N, KTP), (K2ND, VSB)):
                for g0 in (0, 4):
                    t_t = st.v + 1
                    ftw = ft.v
                    @blk.tensor
                    def _(tt, srcT=srcT, g0=g0, v0_t=v0_t, ftw=ftw):
                        tt.wait_ge(sv.h, v0_t)
                        tt.wait_ge(ft.h, ftw)
                        i = None
                        for j in range(4):
                            i = tt.transpose(out=PT[0:64, 128 * j:128 * (j + 1)],
                                             in_=srcT[:, 64 * (g0 + j):64 * (g0 + j + 1)],
                                             identity=EYET[:])
                        i.then_inc(st.h, 1)
                    st.v = t_t
                    e_t = ft.v + 1
                    @blk.scalar
                    def _(s, dstT=dstT, g0=g0, t_t=t_t):
                        s.wait_ge(st.h, t_t)
                        s.copy(out=dstT[0:64, 128 * g0:128 * (g0 + 4)],
                               in_=PT[0:64, 0:512]).then_inc(ft.h, 1)
                    ft.v = e_t
            prep_done_ft = ft.v
            if upto == "prep":
                return sv.v

            p_first = [True]
            rd_sv = [0, 0]
            for g in range(4):
                for tt_i in range(4):
                    t = 4 * g + tt_i
                    ln_t = sv.v + 1
                    prev_tr = st.v
                    @blk.vector
                    def _(v, t=t, prev_tr=prev_tr):
                        v.wait_ge(st.h, prev_tr)
                        v.tensor_scalar(out=LNT[:], in0=STREAM[:, 1024 * t:1024 * (t + 1)],
                                        scalar1=MNEG[:, t:t + 1], scalar2=RSTD[:, t:t + 1],
                                        op0=ALU.add, op1=ALU.mult).then_inc(sv.h, 1)
                    sv.v = ln_t
                    for gg in range(2):
                        t_t = st.v + 1
                        ftw = ft.v
                        @blk.tensor
                        def _(tb, gg=gg, ln_t=ln_t, ftw=ftw):
                            tb.wait_ge(sv.h, ln_t)
                            tb.wait_ge(ft.h, ftw)
                            i = None
                            for j in range(4):
                                c = 4 * gg + j
                                i = tb.transpose(out=PT[:, 128 * j:128 * (j + 1)],
                                                 in_=LNT[:, 128 * c:128 * (c + 1)],
                                                 identity=EYET[:])
                            i.then_inc(st.h, 1)
                        st.v = t_t
                        e_t = ft.v + 1
                        @blk.vector
                        def _(v, gg=gg, tt_i=tt_i, t_t=t_t):
                            v.wait_ge(st.h, t_t)
                            dst = LNF[:].rearrange("p (k r) -> p k r", r=512) \
                                [:, 4 * gg:4 * gg + 4, 128 * tt_i:128 * (tt_i + 1)]
                            v.tensor_copy(dst, PT[:].rearrange(
                                "p (k r) -> p k r", r=128)).then_inc(ft.h, 1)
                        ft.v = e_t
                lnf_ft = ft.v
                for dst_kind in ("q", "k"):
                    cof = 0 if dst_kind == "q" else 1024
                    for och in range(8):
                        mm_t = st.v + 1
                        @blk.tensor
                        def _(tb, och=och, cof=cof, lnf_ft=lnf_ft, psu=ps_u[0]):
                            tb.wait_ge(ft.h, lnf_ft)
                            tb.wait_ge(fs.h, psu)
                            i = None
                            for k in range(8):
                                i = tb.matmul(
                                    out=PS[:, 0:512],
                                    lhsT=WA[:, 3072 * k + cof + 128 * och:
                                            3072 * k + cof + 128 * (och + 1)],
                                    rhs=LNF[:, 512 * k:512 * (k + 1)],
                                    start=(k == 0), stop=(k == 7))
                            i.then_inc(st.h, 1)
                        st.v = mm_t
                        e_t = sv.v + 1
                        @blk.vector
                        def _(v, och=och, g=g, dst_kind=dst_kind, mm_t=mm_t):
                            v.wait_ge(st.h, mm_t)
                            if dst_kind == "q":
                                dst = FMA[:, 2048 * och + 512 * g:2048 * och + 512 * (g + 1)]
                            else:
                                dst = KFM[:, 512 * och:512 * (och + 1)]
                            i = v.tensor_copy(dst, PS[:, 0:512])
                            fs.v += 1
                            i.then_inc(fs.h, 1)
                            v.tensor_scalar(out=MRKV[:], in0=MRKV[:], scalar1=1.0,
                                            scalar2=None, op0=ALU.mult).then_inc(sv.h, 1)
                        sv.v = e_t
                        ps_u[0] = fs.v
                qk_sv = sv.v
                for tt_i in range(4):
                    mm_t = st.v + 1
                    fgw = fg.v
                    @blk.tensor
                    def _(tb, tt_i=tt_i, lnf_ft=lnf_ft, fgw=fgw):
                        tb.wait_ge(ft.h, lnf_ft)
                        tb.wait_ge(fg.h, fgw)
                        i = None
                        for k in range(8):
                            lhs = LNF[:, 512 * k + 128 * tt_i:512 * k + 128 * (tt_i + 1)]
                            for (a, bnd) in ((0, 512), (512, 1024)):
                                i = tb.matmul(out=PGA[:, a:bnd], lhsT=lhs,
                                              rhs=WA[:, 3072 * k + 2048 + a:3072 * k + 2048 + bnd],
                                              start=(k == 0), stop=(k == 7))
                        i.then_inc(st.h, 1)
                    st.v = mm_t
                    ve_t = sv.v + 1
                    @blk.vector
                    def _(v, tt_i=tt_i, mm_t=mm_t):
                        v.wait_ge(st.h, mm_t)
                        i = v.tensor_scalar(out=VTMS[:, 1024 * tt_i:1024 * (tt_i + 1)],
                                            in0=PGA[:, 0:1024], scalar1=1.0,
                                            scalar2=None, op0=ALU.mult)
                        fg.v += 1
                        i.then_inc(fg.h, 1)
                        v.tensor_scalar(out=MRKV[:], in0=MRKV[:], scalar1=1.0, scalar2=None, op0=ALU.mult).then_inc(sv.h, 1)
                    sv.v = ve_t
                vtm_sv = sv.v
                E3PB = [E3P, QH[0]]

                def b1_z3(p):
                    z_t = st.v + 1
                    @blk.tensor
                    def _(tb, p=p, qk_sv=qk_sv, psu=ps_u[0]):
                        tb.wait_ge(sv.h, qk_sv)
                        tb.wait_ge(fs.h, psu)
                        i = None
                        for hh in (0, 64):
                            i = tb.matmul(
                                out=PS[hh:hh + 64, 0:512],
                                lhsT=QLMF[hh:hh + 64, 64 * p:64 * (p + 1)],
                                rhs=KFM[hh:hh + 64, 512 * p:512 * (p + 1)],
                                start=True, stop=True)
                        i.then_inc(st.h, 1)
                    st.v = z_t
                    return z_t

                def b1_exp(p, z_t):
                    par = p % 2
                    e_t = ss.v + 1
                    rdw = rd_sv[par]
                    @blk.scalar
                    def _(s, par=par, z_t=z_t, rdw=rdw):
                        s.wait_ge(st.h, z_t)
                        s.wait_ge(sv.h, rdw)
                        i = s.activation(out=E3PB[par][:], in_=PS[:, 0:512],
                                         func=AF.Exp, scale=0.125)
                        fs.v += 1
                        i.then_inc(fs.h, 1)
                        s.activation(out=MRKS[:, 0:1], in_=MRKS[:, 1:2],
                                     func=AF.Exp).then_inc(ss.h, 1)
                    ss.v = e_t
                    ps_u[0] = fs.v
                    return e_t

                def b1_reduce(p, e_t):
                    par = p % 2
                    sl_t = sv.v + 1
                    @blk.vector
                    def _(v, p=p, g=g, par=par, e_t=e_t):
                        v.wait_ge(ss.h, e_t)
                        v.tensor_reduce(out=SLOC4[:, 4 * p + g:4 * p + g + 1],
                                        in_=E3PB[par][:], axis=AX.X,
                                        op=ALU.add).then_inc(sv.h, 1)
                    sv.v = sl_t
                    rd_sv[par] = sl_t

                def b1_tail(p, e_t):
                    par = p % 2
                    t_t = st.v + 1
                    ftw = ft.v
                    @blk.tensor
                    def _(tb, par=par, e_t=e_t, ftw=ftw):
                        tb.wait_ge(ss.h, e_t)
                        tb.wait_ge(ft.h, ftw)
                        i = None
                        for j in range(4):
                            i = tb.transpose(out=PT[:, 128 * j:128 * (j + 1)],
                                             in_=E3PB[par][:, 128 * j:128 * (j + 1)],
                                             identity=EYET[:])
                        i.then_inc(st.h, 1)
                    st.v = t_t
                    e2t_ = ft.v + 1
                    @blk.vector
                    def _(v, t_t=t_t):
                        v.wait_ge(st.h, t_t)
                        v.tensor_copy(E3TS[:], PT[:, 0:512]).then_inc(ft.h, 1)
                    ft.v = e2t_
                    pm_t = st.v + 1
                    fuw = fu.v
                    @blk.tensor
                    def _(tb, p=p, e2t_=e2t_, vtm_sv=vtm_sv, fuw=fuw):
                        tb.wait_ge(ft.h, e2t_)
                        tb.wait_ge(sv.h, vtm_sv)
                        tb.wait_ge(fu.h, fuw)
                        i = None
                        for tc in range(4):
                            i = tb.matmul(
                                out=PGB[:, 0:128],
                                lhsT=E3TS[:, 128 * tc:128 * (tc + 1)],
                                rhs=VTMS[:, 1024 * tc + 128 * p:1024 * tc + 128 * (p + 1)],
                                start=(tc == 0), stop=(tc == 3))
                        i.then_inc(st.h, 1)
                    st.v = pm_t
                    pe_t = sv.v + 1
                    @blk.vector
                    def _(v, p=p, g=g, pm_t=pm_t):
                        v.wait_ge(st.h, pm_t)
                        if g == 0:
                            i = v.tensor_scalar(out=P_ACC[:, 128 * p:128 * (p + 1)],
                                                in0=PGB[:, 0:128], scalar1=1.0,
                                                scalar2=None, op0=ALU.mult)
                        else:
                            i = v.tensor_tensor(out=P_ACC[:, 128 * p:128 * (p + 1)],
                                                in0=PGB[:, 0:128],
                                                in1=P_ACC[:, 128 * p:128 * (p + 1)],
                                                op=ALU.add)
                        fu.v += 1
                        i.then_inc(fu.h, 1)
                        v.tensor_scalar(out=MRKV[:], in0=MRKV[:], scalar1=1.0,
                                        scalar2=None, op0=ALU.mult).then_inc(sv.h, 1)
                    sv.v = pe_t

                prevp = None
                for p in range(9):
                    if p < 8:
                        z_t = b1_z3(p)
                    if prevp is not None:
                        b1_tail(prevp[0], prevp[1])
                    if p < 8:
                        e_t = b1_exp(p, z_t)
                        b1_reduce(p, e_t)
                        prevp = (p, e_t)
                    else:
                        prevp = None
                p_first[0] = False
            b1_st = st.v
            if upto == "b1":
                @blk.vector
                def _(v, b1_st=b1_st):
                    v.wait_ge(st.h, b1_st)
                    v.tensor_scalar(out=MRKV[:], in0=MRKV[:], scalar1=1.0, scalar2=None, op0=ALU.mult).then_inc(sv.h, 1)
                sv.v += 1
                return sv.v

            slc_t = sv.v + 1
            @blk.vector
            def _(v, b1_st=b1_st):
                v.wait_ge(st.h, b1_st)
                v.tensor_reduce(out=SLOC[:],
                                in_=SLOC4[:].rearrange("p (pr c) -> p pr c", c=4),
                                axis=AX.X, op=ALU.add)
                v.tensor_scalar(
                    out=PCMP[0:64, 0:512].rearrange("p (pr c) -> p pr c", c=64),
                    in0=P_ACC[0:64, :].rearrange("p (pr c) -> p pr c", c=128)[:, :, 0:64],
                    scalar1=1.0, scalar2=None, op0=ALU.mult)
                v.tensor_scalar(
                    out=PCMP[64:128, 0:512].rearrange("p (pr c) -> p pr c", c=64),
                    in0=P_ACC[64:128, :].rearrange("p (pr c) -> p pr c", c=128)[:, :, 64:128],
                    scalar1=1.0, scalar2=None, op0=ALU.mult)
                v.tensor_scalar(out=PCMP[:, 512:520], in0=SLOC[:], scalar1=1.0,
                                scalar2=None, op0=ALU.mult).then_inc(sv.h, 1)
            sv.v = slc_t
            agp_t = sg.v + 17
            @blk.gpsimd
            def _(gp, slc_t=slc_t):
                gp.wait_ge(sv.h, slc_t)
                gp.dma_start(out=CIN_P.rearrange("(p c) -> p c", p=128),
                             in_=PCMP[:]).then_inc(sg.h, 16)
                gp.wait_ge(sg.h, sg.v + 16)
                gp.collective_compute(
                    "AllGather", ALU.bypass, ins=[CIN_P], outs=[COUT_P],
                    replica_groups=GROUPS4).then_inc(sg.h, 1)
            sg.v = agp_t
            rb_marks = []
            for r in range(4):
                prb_ld = sd.v + 16
                prev_acc = sv.v
                @blk.sync
                def _(sync, r=r, agp_t=agp_t, prev_acc=prev_acc):
                    if r == 0:
                        sync.wait_ge(sg.h, agp_t)
                    else:
                        sync.wait_ge(sv.h, prev_acc)
                    sync.dma_start(
                        out=PRB[:],
                        in_=COUT_P[r, :].rearrange("(p c) -> p c", p=128)
                    ).then_inc(sd.h, 16)
                sd.v = prb_ld
                acc_t = sv.v + 1
                @blk.vector
                def _(v, r=r, prb_ld=prb_ld):
                    v.wait_ge(sd.h, prb_ld)
                    if r == 0:
                        v.tensor_scalar(out=PCMP[:], in0=PRB[:], scalar1=1.0,
                                        scalar2=None, op0=ALU.mult).then_inc(sv.h, 1)
                    else:
                        v.tensor_tensor(out=PCMP[:], in0=PCMP[:], in1=PRB[:],
                                        op=ALU.add).then_inc(sv.h, 1)
                sv.v = acc_t
            prb_ld = sv.v

            vts0_t = sv.v + 1
            @blk.vector
            def _(v, pdt=prep_done_ft):
                v.wait_ge(ft.h, pdt)
                v.tensor_scalar(out=VTS[:], in0=K2ND[:], scalar1=1.0,
                                scalar2=None, op0=ALU.mult).then_inc(sv.h, 1)
            sv.v = vts0_t
            for it in range(NS_ITERS):
                kv_t = st.v + 1
                fdw = fd.v
                @blk.tensor
                def _(tb, psu=ps_u[0], fdw=fdw, first=(it == 0),
                      vts0_t=vts0_t):
                    if first:
                        tb.wait_ge(sv.h, vts0_t)
                    tb.wait_ge(fs.h, psu)
                    tb.wait_ge(fd.h, fdw)
                    i = None
                    for h in range(16):
                        p, hh = h // 2, 64 * (h % 2)
                        i = tb.matmul(
                            out=PS[hh:hh + 64, 64 * p:64 * (p + 1)],
                            lhsT=KTP[0:64, 128 * p + hh:128 * p + hh + 64],
                            rhs=VSB[0:64, 128 * p + hh:128 * p + hh + 64],
                            start=True, stop=True)
                    for h in range(16):
                        p, hh = h // 2, 64 * (h % 2)
                        i = tb.matmul(
                            out=PDN[hh:hh + 64, 64 * p:64 * (p + 1)],
                            lhsT=VSB[0:64, 128 * p + hh:128 * p + hh + 64],
                            rhs=KTP[0:64, 128 * p + hh:128 * p + hh + 64],
                            start=True, stop=True)
                    i.then_inc(st.h, 1)
                st.v = kv_t
                r1_t = sv.v + 1
                @blk.vector
                def _(v, kv_t=kv_t):
                    v.wait_ge(st.h, kv_t)
                    i = v.scalar_tensor_tensor(out=R1SB[:], in0=EYBT[:],
                                               scalar=7.0, in1=PS[:, 0:512],
                                               op0=ALU.mult, op1=ALU.subtract)
                    fs.v += 1
                    i.then_inc(fs.h, 1)
                    i = v.scalar_tensor_tensor(out=R1TP[:], in0=EYBT[:],
                                               scalar=7.0, in1=PDN[:, 0:512],
                                               op0=ALU.mult, op1=ALU.subtract)
                    fd.v += 1
                    i.then_inc(fd.h, 1)
                    v.tensor_scalar(out=MRKV[:], in0=MRKV[:], scalar1=1.0, scalar2=None, op0=ALU.mult).then_inc(sv.h, 1)
                ps_u[0] = fs.v
                sv.v = r1_t
                t2_t = st.v + 1
                @blk.tensor
                def _(tb, r1_t=r1_t):
                    tb.wait_ge(sv.h, r1_t)
                    i = None
                    for h in range(16):
                        p, hh = h // 2, 64 * (h % 2)
                        i = tb.matmul(
                            out=PDN[hh:hh + 64, 512 + 64 * p:512 + 64 * (p + 1)],
                            lhsT=R1TP[hh:hh + 64, 64 * p:64 * (p + 1)],
                            rhs=R1SB[hh:hh + 64, 64 * p:64 * (p + 1)],
                            start=True, stop=True)
                    i.then_inc(st.h, 1)
                st.v = t2_t
                r2_t = sv.v + 1
                @blk.vector
                def _(v, t2_t=t2_t):
                    v.wait_ge(st.h, t2_t)
                    v.scalar_tensor_tensor(out=WSB[:], in0=EYBT[:], scalar=15.0,
                                           in1=PDN[:, 512:1024], op0=ALU.mult,
                                           op1=ALU.add)
                    v.scalar_tensor_tensor(out=R2SB[:], in0=R1SB[:], scalar=-7.0,
                                           in1=WSB[:], op0=ALU.mult, op1=ALU.add)
                    v.tensor_scalar(out=MRKV[:], in0=MRKV[:], scalar1=1.0, scalar2=None, op0=ALU.mult).then_inc(sv.h, 1)
                sv.v = r2_t
                t3_t = st.v + 1
                @blk.tensor
                def _(tb, r2_t=r2_t, psu=ps_u[0]):
                    tb.wait_ge(sv.h, r2_t)
                    tb.wait_ge(fs.h, psu)
                    i = None
                    for h in range(16):
                        p, hh = h // 2, 64 * (h % 2)
                        i = tb.matmul(
                            out=PS[hh:hh + 64, 64 * p:64 * (p + 1)],
                            lhsT=R1TP[hh:hh + 64, 64 * p:64 * (p + 1)],
                            rhs=R2SB[hh:hh + 64, 64 * p:64 * (p + 1)],
                            start=True, stop=True)
                    i.then_inc(st.h, 1)
                st.v = t3_t
                r3_t = sv.v + 1
                @blk.vector
                def _(v, t3_t=t3_t):
                    v.wait_ge(st.h, t3_t)
                    v.scalar_tensor_tensor(out=WSB[:], in0=EYBT[:], scalar=13.0,
                                           in1=PS[:, 0:512], op0=ALU.mult,
                                           op1=ALU.add)
                    i = v.scalar_tensor_tensor(out=R3SB[:], in0=R2SB[:],
                                               scalar=-7.0, in1=WSB[:],
                                               op0=ALU.mult, op1=ALU.add)
                    fs.v += 1
                    i.then_inc(fs.h, 1)
                    v.tensor_scalar(out=MRKV[:], in0=MRKV[:], scalar1=1.0, scalar2=None, op0=ALU.mult).then_inc(sv.h, 1)
                ps_u[0] = fs.v
                sv.v = r3_t
                vp_t = st.v + 1
                @blk.tensor
                def _(tb, r3_t=r3_t, psu=ps_u[0]):
                    tb.wait_ge(sv.h, r3_t)
                    tb.wait_ge(fs.h, psu)
                    i = None
                    for h in range(16):
                        p, hh = h // 2, 64 * (h % 2)
                        i = tb.matmul(
                            out=PS[hh:hh + 64, 64 * p:64 * (p + 1)],
                            lhsT=VTS[hh:hh + 64, 64 * p:64 * (p + 1)],
                            rhs=R3SB[hh:hh + 64, 64 * p:64 * (p + 1)],
                            start=True, stop=True)
                    for h in range(16):
                        p, hh = h // 2, 64 * (h % 2)
                        i = tb.matmul(
                            out=PDN[hh:hh + 64, 64 * p:64 * (p + 1)],
                            lhsT=R3SB[hh:hh + 64, 64 * p:64 * (p + 1)],
                            rhs=VTS[hh:hh + 64, 64 * p:64 * (p + 1)],
                            start=True, stop=True)
                    i.then_inc(st.h, 1)
                st.v = vp_t
                ve_t = sv.v + 1
                @blk.vector
                def _(v, vp_t=vp_t):
                    v.wait_ge(st.h, vp_t)
                    v.tensor_scalar(
                        out=VSB[0:64, :].rearrange("p (pr c) -> p pr c", c=128)[:, :, 0:64],
                        in0=PS[0:64, :].rearrange("p (pr c) -> p pr c", c=64),
                        scalar1=0.25, scalar2=None, op0=ALU.mult)
                    i = v.tensor_scalar(
                        out=VSB[0:64, :].rearrange("p (pr c) -> p pr c", c=128)[:, :, 64:128],
                        in0=PS[64:128, :].rearrange("p (pr c) -> p pr c", c=64),
                        scalar1=0.25, scalar2=None, op0=ALU.mult)
                    fs.v += 1
                    i.then_inc(fs.h, 1)
                    i = v.tensor_scalar(out=VTS[:], in0=PDN[:, 0:512], scalar1=0.25,
                                        scalar2=None, op0=ALU.mult)
                    fd.v += 1
                    i.then_inc(fd.h, 1)
                    v.tensor_scalar(out=MRKV[:], in0=MRKV[:], scalar1=1.0, scalar2=None, op0=ALU.mult).then_inc(sv.h, 1)
                ps_u[0] = fs.v
                sv.v = ve_t
            vts_ft = ft.v
            k3_t = sv.v + 1
            @blk.vector
            def _(v):
                v.drain()
                v.reciprocal(out=RCPS[:], in_=PCMP[:, 512:520])
                v.drain()
                v.tensor_tensor(out=K3V[:].rearrange("p (pr c) -> p pr c", c=64),
                                in0=PCMP[:, 0:512].rearrange("p (pr c) -> p pr c", c=64),
                                in1=RCPS[:].unsqueeze(2).broadcast_to([128, 8, 64]),
                                op=ALU.mult).then_inc(sv.h, 1)
            sv.v = k3_t
            tm_t = st.v + 1
            @blk.tensor
            def _(tb, k3_t=k3_t, vts_ft=vts_ft, psu=ps_u[0]):
                tb.wait_ge(sv.h, k3_t)
                tb.wait_ge(ft.h, vts_ft)
                tb.wait_ge(fs.h, psu)
                i = None
                for h in range(16):
                    p, hh = h // 2, 64 * (h % 2)
                    i = tb.matmul(
                        out=PS[hh:hh + 64, 64 * p:64 * (p + 1)],
                        lhsT=VTS[hh:hh + 64, 64 * p:64 * (p + 1)],
                        rhs=K3V[hh:hh + 64, 64 * p:64 * (p + 1)],
                        start=True, stop=True)
                i.then_inc(st.h, 1)
            st.v = tm_t
            te_t = sv.v + 1
            @blk.vector
            def _(v, tm_t=tm_t):
                v.wait_ge(st.h, tm_t)
                i = v.tensor_copy(TSB[:], PS[:, 0:512])
                fs.v += 1
                i.then_inc(fs.h, 1)
                v.tensor_scalar(out=MRKV[:], in0=MRKV[:], scalar1=1.0,
                                scalar2=None, op0=ALU.mult).then_inc(sv.h, 1)
            sv.v = te_t
            ps_u[0] = fs.v
            if upto == "ns":
                @blk.vector
                def _(v, te_t=te_t):
                    v.wait_ge(sv.h, te_t)
                    v.tensor_scalar(out=MRKV[:], in0=MRKV[:], scalar1=1.0, scalar2=None, op0=ALU.mult).then_inc(sv.h, 1)
                sv.v += 1
                return sv.v
            wproj_ld = sg.v + 16
            @blk.gpsimd
            def _(gp, b1_st=b1_st):
                gp.wait_ge(st.h, b1_st)
                gp.dma_start(
                    out=WA[:, 0:8192].rearrange("p (k c) -> p k c", c=D),
                    in_=WPROJ.rearrange("(k p) c -> p k c", p=128)).then_inc(sg.h, 16)
            sg.v = wproj_ld

            last_add = sv.v
            pt_u = [ft.v]
            for g in range(4):
                def b2_k1(p):
                    k1_t = st.v + 1
                    @blk.tensor
                    def _(tb, p=p, g=g, te_t=te_t, psu=ps_u[0]):
                        tb.wait_ge(sv.h, te_t)
                        tb.wait_ge(fs.h, psu)
                        i = None
                        for hh in (0, 64):
                            i = tb.matmul(
                                out=PS[hh:hh + 64, 0:512],
                                lhsT=KLMF[hh:hh + 64, 64 * p:64 * (p + 1)],
                                rhs=FMA[hh:hh + 64, 2048 * p + 512 * g:2048 * p + 512 * (g + 1)],
                                start=True, stop=True)
                        i.then_inc(st.h, 1)
                    st.v = k1_t
                    return k1_t

                def b2_e1_sv(p, rdmt):
                    e_t = ss.v + 1
                    @blk.scalar
                    def _(s, rdmt=rdmt):
                        s.wait_ge(st.h, rdmt)
                        i = s.activation(out=E1P[:], in_=PS[:, 0:512],
                                         func=AF.Exp, scale=0.125)
                        fs.v += 1
                        i.then_inc(fs.h, 1)
                        s.activation(out=MRKS[:, 0:1], in_=MRKS[:, 1:2],
                                     func=AF.Exp).then_inc(ss.h, 1)
                    ss.v = e_t
                    ps_u[0] = fs.v
                    sv_t = st.v + 1
                    ptw = pt_u[0]
                    @blk.tensor
                    def _(tb, p=p, e_t=e_t, ptw=ptw):
                        tb.wait_ge(ss.h, e_t)
                        tb.wait_ge(ft.h, ptw)
                        i = None
                        for hh in (0, 64):
                            i = tb.matmul(
                                out=PDN[hh:hh + 64, 512:1024],
                                lhsT=TSB[hh:hh + 64, 64 * p:64 * (p + 1)],
                                rhs=E1P[hh:hh + 64, :],
                                start=True, stop=True)
                        i.then_inc(st.h, 1)
                    st.v = sv_t
                    return e_t, sv_t

                def b2_tail(p, e_t, sv_t):
                    dm_t = st.v + 1
                    fdw = fd.v
                    @blk.tensor
                    def _(tb, e_t=e_t, fdw=fdw):
                        tb.wait_ge(ss.h, e_t)
                        tb.wait_ge(fd.h, fdw)
                        tb.matmul(out=PDN[0:2, 0:512],
                                  lhsT=O2T[:], rhs=E1P[:],
                                  start=True, stop=True).then_inc(st.h, 1)
                    st.v = dm_t
                    rc_t = ss.v + 1
                    @blk.scalar
                    def _(s, dm_t=dm_t):
                        s.wait_ge(st.h, dm_t)
                        i = s.activation(out=DSB[:], in_=PDN[0:2, 0:512],
                                         func=AF.Ln)
                        fd.v += 1
                        i.then_inc(fd.h, 1)
                        s.activation(out=RCDB[:], in_=DSB[:], func=AF.Exp,
                                     scale=-1.0).then_inc(ss.h, 1)
                    ss.v = rc_t
                    rm_t = st.v + 1
                    fdw = fd.v
                    @blk.tensor
                    def _(tb, rc_t=rc_t, fdw=fdw):
                        tb.wait_ge(ss.h, rc_t)
                        tb.wait_ge(fd.h, fdw)
                        tb.matmul(out=PDN[:, 0:512], lhsT=E2TTT[:],
                                  rhs=RCDB[:],
                                  start=True, stop=True).then_inc(st.h, 1)
                    st.v = rm_t
                    re_t = sv.v + 1
                    @blk.vector
                    def _(v, rm_t=rm_t):
                        v.wait_ge(st.h, rm_t)
                        i = v.tensor_copy(RSB[:], PDN[:, 0:512])
                        fd.v += 1
                        i.then_inc(fd.h, 1)
                        v.tensor_scalar(out=MRKV[:], in0=MRKV[:], scalar1=1.0,
                                        scalar2=None, op0=ALU.mult).then_inc(sv.h, 1)
                    sv.v = re_t
                    se_t = sv.v + 1
                    @blk.vector
                    def _(v, p=p, sv_t=sv_t):
                        v.wait_ge(st.h, sv_t)
                        i = v.tensor_tensor(out=SVF[:, 512 * p:512 * (p + 1)],
                                            in0=PDN[:, 512:1024], in1=RSB[:],
                                            op=ALU.mult)
                        pt_u[0] = ft.v + 1
                        ft.v += 1
                        i.then_inc(ft.h, 1)
                        v.tensor_scalar(out=MRKV[:], in0=MRKV[:], scalar1=1.0,
                                        scalar2=None, op0=ALU.mult).then_inc(sv.h, 1)
                    sv.v = se_t
                    return dm_t

                prevq = None
                for p in range(9):
                    if p < 8:
                        k1_t = b2_k1(p)
                        rdmt = k1_t
                    if prevq is not None:
                        rdmt = b2_tail(prevq[0], prevq[1][0], prevq[1][1])
                    if p < 8:
                        es = b2_e1_sv(p, rdmt)
                        prevq = (p, es)
                    else:
                        prevq = None
                svf_sv = sv.v
                for tt_i in range(4):
                    t = 4 * g + tt_i
                    pj_t = st.v + 1
                    fgw = fg.v
                    @blk.tensor
                    def _(tb, tt_i=tt_i, svf_sv=svf_sv, fgw=fgw,
                          wproj_ld=wproj_ld):
                        tb.wait_ge(sv.h, svf_sv)
                        tb.wait_ge(sg.h, wproj_ld)
                        tb.wait_ge(fg.h, fgw)
                        i = None
                        for c in range(8):
                            lhs = SVF[:, 512 * c + 128 * tt_i:512 * c + 128 * (tt_i + 1)]
                            for (a, bnd) in ((0, 512), (512, 1024)):
                                i = tb.matmul(out=PGA[:, a:bnd], lhsT=lhs,
                                              rhs=WA[:, 1024 * c + a:1024 * c + bnd],
                                              start=(c == 0), stop=(c == 7))
                        i.then_inc(st.h, 1)
                    st.v = pj_t
                    ad_t = sv.v + 1
                    @blk.vector
                    def _(v, t=t, pj_t=pj_t):
                        v.wait_ge(st.h, pj_t)
                        strm = STREAM[:, 1024 * t:1024 * (t + 1)]
                        i = v.tensor_tensor(out=strm, in0=PGA[:, 0:1024], in1=strm,
                                            op=ALU.add)
                        fg.v += 1
                        i.then_inc(fg.h, 1)
                        v.tensor_scalar(out=MRKV[:], in0=MRKV[:], scalar1=1.0, scalar2=None, op0=ALU.mult).then_inc(sv.h, 1)
                    sv.v = ad_t
                    last_add = sv.v
            return last_add

        xq1_ft = emit_ln_xq([(sx, x_ld), (sd, 128)])
        ffn1_st = emit_ffn(xq1_ft, 0, None)

        def emit_dump():
            for t in range(NT):
                prev_o = so.v
                c_t = sv.v + 1
                @blk.vector
                def _(v, t=t, prev_o=prev_o, base=sv.v):
                    v.wait_ge(sv.h, base)
                    v.wait_ge(so.h, prev_o)
                    v.tensor_copy(XOUT[:], STREAM[:, 1024 * t:1024 * (t + 1)]) \
                        .then_inc(sv.h, 1)
                sv.v = c_t
                o_t = so.v + 16
                @blk.sync
                def _(sync, t=t, c_t=c_t):
                    sync.wait_ge(sv.h, c_t)
                    sync.dma_start(out=OUT[128 * t:128 * (t + 1), :],
                                   in_=XOUT[:]).then_inc(so.h, 16)
                so.v = o_t
            fin_o = so.v
            @blk.sync
            def _(sync, fin_o=fin_o):
                sync.wait_ge(so.h, fin_o)

        if stage == "ffn1":
            emit_dump()
            return nc
        if stage.startswith("attn"):
            upto = {"attn": "all", "attn_dbg": "all", "attn_prep": "prep",
                    "attn_b1": "b1",
                    "attn_ns": "ns", "attn_p1": "p1", "attn_p2": "p2",
                    "attn_p3": "p3", "attn_p4": "p4", "attn_p3a": "p3a",
                    "attn_p3b": "p3b"}[stage]
            fin_sv = emit_attn(sv.v, upto)
            if stage == "attn_dbg":
                slots = [(VTMS, 1024), (QLMF, 512), (KLMF, 512), (K2N, 512),
                         (K2ND, 512), (VSB, 1024), (K3V, 512), (TSB, 512),
                         (SLOC, 8), (PCMP, 520), (RD128, 16), (R1TP, 512),
                         (KTP, 1024), (R1SB, 512),
                         (R2SB, 512), (R3SB, 512)]
                row = 0
                for src_ap, cols in slots:
                    pc = src_ap.shape[0]
                    c_t = sv.v + 1
                    prev_o = so.v
                    @blk.vector
                    def _(v, src_ap=src_ap, cols=cols, pc=pc, prev_o=prev_o,
                          base=sv.v):
                        v.wait_ge(sv.h, base)
                        v.wait_ge(so.h, prev_o)
                        nc_ = min(cols, 1024)
                        v.tensor_scalar(out=XOUT[0:pc, 0:nc_],
                                        in0=src_ap[:, 0:nc_], scalar1=1.0,
                                        scalar2=None, op0=ALU.mult)
                        if cols > 1024:
                            v.tensor_scalar(out=XOUT[0:pc, 0:1024],
                                            in0=src_ap[:, 0:1024], scalar1=1.0,
                                            scalar2=None, op0=ALU.mult)
                        v.tensor_scalar(out=MRKV[:], in0=MRKV[:], scalar1=1.0, scalar2=None, op0=ALU.mult).then_inc(sv.h, 1)
                    sv.v = c_t
                    o_t = so.v + 16
                    @blk.sync
                    def _(sync, row=row, c_t=c_t):
                        sync.wait_ge(sv.h, c_t)
                        sync.dma_start(out=OUT[128 * row:128 * (row + 1), :],
                                       in_=XOUT[:]).then_inc(so.h, 16)
                    so.v = o_t
                    row += 1
                fin_o = so.v
                @blk.sync
                def _(sync, fin_o=fin_o):
                    sync.wait_ge(so.h, fin_o)
                return nc
            emit_dump()
            return nc
        attn_sv = emit_attn(sv.v)
        xq2_ft = emit_ln_xq([(sv, attn_sv)])
        emit_ffn(xq2_ft, st.v, OUT)
        fin_o = so.v
        @blk.sync
        def _(sync, fin_o=fin_o):
            sync.wait_ge(so.h, fin_o)
    return nc


def make_consts():
    eye = np.eye(128).astype(bfl)
    eyeb = np.tile(np.vstack([np.eye(64), np.eye(64)]).astype(np.float32), (1, 8))
    ones1 = np.ones((128, 1), bfl)
    ones2 = np.zeros((128, 2), np.float32)
    ones2[:64, 0] = 1.0
    ones2[64:, 1] = 1.0
    e2t = np.zeros((2, 128), np.float32)
    e2t[0, :64] = 1.0
    e2t[1, 64:] = 1.0
    c80 = np.full((128, 1), 0x8000, np.uint16)
    onesr = np.ones((1, 128), bfl)
    etbl = np.zeros((128, 256), np.float32)
    for t in range(16):
        etbl[:, 16 * t + t] = 1.0
    return dict(eye=eye, eyeb=eyeb, ones1=ones1, ones2=ones2.astype(bfl),
                e2t=e2t.astype(bfl), c80=c80, onesr=onesr,
                etbl=etbl.astype(bfl))


def make_inmaps(L0_attn_x, L0_attn_w_qkv, L0_attn_w_proj, L0_ffn_w_gate,
                L0_ffn_w_up, L0_ffn_w_down):
    x = np.asarray(L0_attn_x, np.float32).reshape(2, 8192, 1024)
    consts = make_consts()
    wqkv = np.ascontiguousarray(np.asarray(L0_attn_w_qkv, np.float32))
    wproj = np.ascontiguousarray(np.asarray(L0_attn_w_proj, np.float32))
    wg = np.asarray(L0_ffn_w_gate, np.float32)
    wu = np.asarray(L0_ffn_w_up, np.float32)
    wd = np.asarray(L0_ffn_w_down, np.float32)
    in_maps = []
    for c in range(NCORES):
        b, t0 = c // 4, 2048 * (c % 4)
        m = dict(consts)
        m["x"] = np.ascontiguousarray(x[b, t0:t0 + 2048])
        m["wqkv"] = wqkv
        m["wproj"] = wproj
        m["wg_sl"] = np.ascontiguousarray(wg[:, 128 * c:128 * (c + 1)])
        m["wu_sl"] = np.ascontiguousarray(wu[:, 128 * c:128 * (c + 1)])
        m["wd_sl"] = np.ascontiguousarray(wd[:, 352 * c:352 * (c + 1)])
        in_maps.append(m)
    return in_maps


def kernel(L0_attn_x, L0_attn_w_qkv, L0_attn_w_proj, L0_ffn_w_gate,
           L0_ffn_w_up, L0_ffn_w_down):
    in_maps = make_inmaps(L0_attn_x, L0_attn_w_qkv, L0_attn_w_proj,
                          L0_ffn_w_gate, L0_ffn_w_up, L0_ffn_w_down)
    nc = build("full")
    res = run_bass_kernel_spmd(nc, in_maps, core_ids=list(range(NCORES)))
    outs = [np.asarray(res.results[c]["out"]) for c in range(NCORES)]
    full = np.stack([np.concatenate(outs[0:4], axis=0),
                     np.concatenate(outs[4:8], axis=0)]).astype(np.float32)
    return full


# version 08 (final)
# speedup vs baseline: 1.2253x; 1.0106x over previous
import contextlib
NS_ITERS = 6
import numpy as np
import ml_dtypes
import concourse.bass as bass
import concourse.mybir as mybir
from concourse.bass_utils import run_bass_kernel_spmd

F32 = mybir.dt.float32
F16 = mybir.dt.float16
BF16 = mybir.dt.bfloat16
U16 = mybir.dt.uint16
AF = mybir.ActivationFunctionType
ALU = mybir.AluOpType
AX = mybir.AxisListType
bfl = ml_dtypes.bfloat16

NCORES = 8
TPC = 2048
NT = 16
D = 1024
IF = 2816
NH, HD = 16, 64
ROUNDS = [(0, 512), (512, 1024), (1024, 1536), (1536, 2048), (2048, 2560), (2560, 2816)]
WSEG_G = 128 * IF
WSEG_U = 128 * IF
WSEG_D = 352 * D
WSEG = WSEG_G + WSEG_U + WSEG_D
PSEG = 128 * 520

MIDS = [0.25, 0.75, 1.25, 1.75, 2.5, 3.5, 5.0]
DELT = [0.5, 0.5, 0.5, 0.5, 1.0, 1.0, 2.0]
C_ABS16, C_MAX16, C_ADD16, C_MSK16 = 0x7FFF, 15360.0, 255.0, 0xFE00
C_T1, C_W1 = 13312.0, 14336.0
C_T2, C_W2 = 14848.0, 1024.0
C_SUB = 15360.0


class Sem:
    def __init__(self, es, nc, name):
        self.h = es.enter_context(nc.semaphore(name))
        self.v = 0


def build(stage="full"):
    nc = bass.Bass("TRN2", target_bir_lowering=False)
    ap = lambda name, shp, dt, kind: nc.dram_tensor(name, shp, dt, kind=kind).ap()
    X = ap("x", [TPC, D], F32, "ExternalInput")
    WQKV = ap("wqkv", [D, 3 * D], F32, "ExternalInput")
    WPROJ = ap("wproj", [D, D], F32, "ExternalInput")
    WGSL = ap("wg_sl", [IF, 128], F32, "ExternalInput")
    WUSL = ap("wu_sl", [IF, 128], F32, "ExternalInput")
    WDSL = ap("wd_sl", [D, 352], F32, "ExternalInput")
    EYE = ap("eye", [128, 128], BF16, "ExternalInput")
    EYEB = ap("eyeb", [128, 512], F32, "ExternalInput")
    ONES1 = ap("ones1", [128, 1], BF16, "ExternalInput")
    ONES2 = ap("ones2", [128, 2], BF16, "ExternalInput")
    E2T = ap("e2t", [2, 128], BF16, "ExternalInput")
    C80 = ap("c80", [128, 1], U16, "ExternalInput")
    ONESR = ap("onesr", [1, 128], BF16, "ExternalInput")
    ETBL = ap("etbl", [128, 256], BF16, "ExternalInput")
    OUT = ap("out", [TPC, D], F32, "ExternalOutput")
    if stage == "wq":
        DBG = ap("dbg", [NCORES * WSEG], BF16, "ExternalOutput")
    CIN_W = ap("cin_w", [WSEG], BF16, "Internal")
    COUT_W = ap("cout_w", [NCORES, WSEG], BF16, "Internal")
    CIN_Y = ap("cin_y", [16 * D], BF16, "Internal")
    COUT_Y = ap("cout_y", [4, 16 * D], BF16, "Internal")
    CIN_P = ap("cin_p", [PSEG], F32, "Internal")
    COUT_P = ap("cout_p", [4, PSEG], F32, "Internal")
    GROUPS4 = [[0, 1, 2, 3], [4, 5, 6, 7]]
    CG_V = CIN_W[0:WSEG_G].rearrange("(p c) -> p c", p=128)
    CU_V = CIN_W[WSEG_G:WSEG_G + WSEG_U].rearrange("(p c) -> p c", p=128)
    CD_V = CIN_W[WSEG_G + WSEG_U:WSEG].rearrange("(r h) -> r h", h=D)

    with contextlib.ExitStack() as es:
        E = es.enter_context
        sb = lambda name, shp, dt: E(nc.sbuf_tensor(name, shp, dt))
        STREAM = sb("stream", [128, NT * D], BF16)
        FMA = sb("fma", [128, 8 * TPC], BF16)
        WA = sb("wa", [128, 24576], BF16)
        C80T = sb("c80t", [128, 1], U16)
        EYET = sb("eyet", [128, 128], BF16)
        ONESRT = sb("onesrt", [1, 128], BF16)
        ETBLT = sb("etblt", [128, 256], BF16)
        O2T = sb("o2t", [128, 2], BF16)
        E2TTT = sb("e2ttt", [2, 128], BF16)
        EYBT = sb("eybt", [128, 512], F32)
        ONES1T = sb("ones1t", [128, 1], BF16)
        AR32 = sb("ar32", [128, 9600], F32)
        AR16 = sb("ar16", [128, 27200], U16)
        _o32 = [0]
        _o16 = [0]
        def av32(pcount, cols, off=None):
            if off is None:
                off = _o32[0]; _o32[0] = off + cols
            return AR32.ap()[0:pcount, off:off + cols]
        def av16(pcount, cols, dt, off=None):
            if off is None:
                off = _o16[0]; _o16[0] = off + cols
            u = AR16.ap()[0:pcount, off:off + cols]
            return u.bitcast(dt) if dt != U16 else u
        BN6 = av32(128, 12 * NT)
        BN2 = av32(128, 2 * NT)
        MNEG = av32(128, NT)
        RSTD = av32(128, NT)
        TMPV = av32(128, NT)
        SIL = [av32(128, 512), av32(128, 512)]
        H16 = [av32(128, 512), av32(128, 512)]
        XOUT_OFF32 = _o32[0]
        XOUT = av32(128, 1024)
        D16 = av32(128, 1024, off=XOUT_OFF32)
        QMX = av32(128, 64)
        QRC = av32(128, 64)
        QRS = av32(128, 64)
        QSC = av32(128, 64)
        OVL32 = _o32[0]
        WP = av32(128, 1408)
        QX = av32(128, 704)
        QC = av32(128, 704)
        QJ = av32(128, 704)
        QS = av32(128, 704)
        WMX = av32(128, IF // 16)
        RCP = av32(128, IF // 16)
        RS_ = av32(128, IF // 16)
        SC_ = av32(128, IF // 16)
        _o32[0] = OVL32
        E2SB_OFF32 = _o32[0]
        E2SB = av32(128, 512)
        SM8 = av32(128, 8)
        WSB_OFF32 = _o32[0]
        WSB = av32(128, 512)
        RDEN = av32(1, 16)
        RD128 = av32(128, 16)
        RDE = av32(128, 8)
        RDO = av32(128, 8)
        MRKV = av32(128, 1)
        MRKS = av32(128, 2, off=None)
        P_ACC = av32(128, 1024)
        SLOC4 = av32(128, 32)
        SLOC = av32(128, 8)
        RCPS = av32(128, 8)
        PCMP = av32(128, 520)
        PRB = av32(128, 520)
        DSB = av32(2, 512, off=E2SB_OFF32)
        DSR = av32(2, 512, off=WSB_OFF32)
        XN16 = av16(128, 1024, F16)
        TU1 = av16(128, 1024, U16)
        TU2 = av16(128, 1024, U16)
        TU3 = av16(128, 1024, U16)
        XQTM = av16(128, 1024, BF16)
        QH = [av16(128, 512, BF16), av16(128, 512, BF16)]
        QHT = [av16(128, 512, BF16), av16(128, 512, BF16)]
        OVL16 = _o16[0]
        WQT = av16(128, 3072, BF16)
        WCT = av16(128, 3072, BF16)
        _o16[0] = OVL16
        LNT_OFF = _o16[0]
        LNT = av16(128, 1024, BF16)
        LNF_OFF = _o16[0]
        LNF = av16(128, 8 * 512, BF16)
        YLM_OFF = _o16[0]
        YLM = av16(16, 1024, BF16)
        YLMF = av16(64, 1024, BF16)
        YLMT = av16(128, 512, BF16)
        QLMF = av16(128, 512, BF16)
        KLMF = av16(128, 512, BF16)
        K2N = av16(128, 512, BF16)
        K2ND = av16(128, 512, BF16)
        KTP = av16(64, 1024, BF16)
        VSB = av16(64, 1024, BF16)
        RDENB = av16(1, 16, BF16)
        KFM = av16(128, 8 * 512, BF16)
        SVF = KFM
        VTMS = av16(128, 4 * 1024, BF16)
        E3P = av16(128, 512, BF16, off=YLM_OFF)
        E3TS = av16(128, 512, BF16, off=YLM_OFF + 512)
        K3V = av16(128, 512, BF16, off=LNT_OFF)
        TSB = av16(128, 512, BF16, off=YLM_OFF + 1024)
        VTS = av16(128, 512, BF16, off=LNF_OFF)
        R1SB = av16(128, 512, BF16, off=LNF_OFF + 512)
        R1TP = av16(128, 512, BF16, off=LNF_OFF + 1024)
        R2SB = av16(128, 512, BF16, off=LNF_OFF + 1536)
        R3SB = av16(128, 512, BF16, off=LNF_OFF + 2048)
        E1P = av16(128, 512, BF16, off=LNF_OFF + 2560)
        RCDB = av16(2, 512, BF16, off=LNF_OFF + 3072)
        RSB = av16(128, 512, BF16, off=LNF_OFF + 3584)
        PGA = E(nc.psum_tensor("pga", [128, 1024], F32))
        PGB = E(nc.psum_tensor("pgb", [128, 1024], F32))
        PDN = E(nc.psum_tensor("pdn", [128, 1024], F32))
        PT = E(nc.psum_tensor("pt", [128, 512], BF16))
        PS = E(nc.psum_tensor("ps", [128, 512], F32))

        sd = Sem(es, nc, "sd")
        sv = Sem(es, nc, "sv")
        ss = Sem(es, nc, "ss")
        st = Sem(es, nc, "st")
        sg = Sem(es, nc, "sg")
        sfr = Sem(es, nc, "sfr")
        so = Sem(es, nc, "so")
        sx = Sem(es, nc, "sx")
        fs = Sem(es, nc, "fs")
        fg = Sem(es, nc, "fg")
        fu = Sem(es, nc, "fu")
        fd = Sem(es, nc, "fd")
        ft = Sem(es, nc, "ft")

        blk = E(nc.Block())

        @blk.sync
        def _(sync):
            sync.dma_start(out=C80T[:], in_=C80).then_inc(sd.h, 16)
            sync.dma_start(out=EYET[:], in_=EYE).then_inc(sd.h, 16)
            sync.dma_start(out=ONESRT[:], in_=ONESR).then_inc(sd.h, 16)
            sync.dma_start(out=ETBLT[:], in_=ETBL).then_inc(sd.h, 16)
            sync.dma_start(out=O2T[:], in_=ONES2).then_inc(sd.h, 16)
            sync.dma_start(out=E2TTT[:], in_=E2T).then_inc(sd.h, 16)
            sync.dma_start(out=EYBT[:], in_=EYEB).then_inc(sd.h, 16)
            sync.dma_start(out=ONES1T[:], in_=ONES1).then_inc(sd.h, 16)
        sd.v += 128
        x_ld = sx.v + 16
        @blk.gpsimd
        def _(gp):
            gp.dma_start(
                out=STREAM[:].rearrange("p (t f) -> p t f", f=D),
                in_=X.rearrange("(t p) f -> p t f", p=128)).then_inc(sx.h, 16)
        sx.v = x_ld

        def wprep(src3d, inner_w, nchunks, tr_specs, stage_fn, wp_free_sv):
            wqt_free_st = st.v
            prev_xn = wp_free_sv
            nh0 = nchunks // 2
            for half in (0, 1):
                nch = (nh0, nchunks - nh0)[half]
                ncols = nch * inner_w
                cb = nh0 * inner_w if half else 0
                ch0 = nh0 if half else 0
                ld = sd.v + 16
                @blk.sync
                def _(sync, ld=ld, ncols=ncols, ch0=ch0, nch=nch,
                      prev_xn=prev_xn):
                    sync.wait_ge(sv.h, prev_xn)
                    sync.dma_start(
                        out=WP[:, 0:ncols].rearrange("p (c f) -> p c f",
                                                     f=inner_w),
                        in_=src3d[:, ch0:ch0 + nch, :]).then_inc(sd.h, 16)
                sd.v = ld
                xn_t = sv.v + 1
                last = (half == 1)
                @blk.vector
                def _(v, ld=ld, ncols=ncols, cb=cb, last=last,
                      wqt_free_st=wqt_free_st, first=(half == 0)):
                    v.wait_ge(sd.h, ld)
                    if first:
                        v.wait_ge(st.h, wqt_free_st)
                    for h0 in range(0, ncols, 704):
                        hw = min(704, ncols - h0)
                        hb = hw // 16
                        b0 = (cb + h0) // 16
                        s3 = WP[:, h0:h0 + hw].rearrange("p (nb w) -> p nb w", w=16)
                        v.tensor_reduce(out=WMX[:, b0:b0 + hb], in_=s3, axis=AX.X,
                                        op=ALU.max, apply_absolute_value=True)
                        v.drain()
                        v.reciprocal(out=RCP[:, b0:b0 + hb], in_=WMX[:, b0:b0 + hb])
                        v.drain()
                        v.tensor_scalar(out=RS_[:, b0:b0 + hb], in0=RCP[:, b0:b0 + hb],
                                        scalar1=6.0, scalar2=None, op0=ALU.mult)
                        v.tensor_scalar(out=SC_[:, b0:b0 + hb], in0=WMX[:, b0:b0 + hb],
                                        scalar1=1.0 / 6.0, scalar2=None, op0=ALU.mult)
                        i = v.tensor_tensor(
                            out=QX[:, 0:hw].rearrange("p (nb w) -> p nb w", w=16),
                            in0=s3,
                            in1=RS_[:, b0:b0 + hb].unsqueeze(2).broadcast_to([128, hb, 16]),
                            op=ALU.mult)
                        if h0 + hw >= ncols:
                            i.then_inc(sv.h, 1)
                        xn = QX[:, 0:hw]
                        v.tensor_scalar(out=QS[:, 0:hw].bitcast(mybir.dt.int32),
                                        in0=xn.bitcast(mybir.dt.int32),
                                        scalar1=0x7FFFFFFF, scalar2=None,
                                        op0=ALU.bitwise_and)
                        axn = QS[:, 0:hw]
                        v.tensor_scalar(out=QC[:, 0:hw], in0=axn, scalar1=MIDS[0],
                                        scalar2=DELT[0], op0=ALU.is_gt, op1=ALU.mult)
                        for j in range(1, 7):
                            v.tensor_scalar(out=QJ[:, 0:hw], in0=axn, scalar1=MIDS[j],
                                            scalar2=DELT[j], op0=ALU.is_gt, op1=ALU.mult)
                            v.tensor_tensor(out=QC[:, 0:hw], in0=QC[:, 0:hw],
                                            in1=QJ[:, 0:hw], op=ALU.add)
                        v.scalar_tensor_tensor(out=QJ[:, 0:hw], in0=xn, scalar=0.0,
                                               in1=QC[:, 0:hw], op0=ALU.is_gt,
                                               op1=ALU.mult)
                        v.scalar_tensor_tensor(out=QC[:, 0:hw], in0=QJ[:, 0:hw],
                                               scalar=2.0, in1=QC[:, 0:hw],
                                               op0=ALU.mult, op1=ALU.subtract)
                        v.tensor_tensor(
                            out=WQT[:, cb + h0:cb + h0 + hw]
                            .rearrange("p (nb w) -> p nb w", w=16),
                            in0=QC[:, 0:hw].rearrange("p (nb w) -> p nb w", w=16),
                            in1=SC_[:, b0:b0 + hb].unsqueeze(2).broadcast_to([128, hb, 16]),
                            op=ALU.mult)
                    if last:
                        v.tensor_scalar(out=MRKV[:], in0=MRKV[:], scalar1=1.0, scalar2=None, op0=ALU.mult).then_inc(sv.h, 1)
                sv.v = xn_t + (1 if last else 0)
                prev_xn = xn_t
            q_t = sv.v
            wct_free_so = so.v
            for g0 in range(0, len(tr_specs), 4):
                grp = tr_specs[g0:g0 + 4]
                t_t = st.v + 1
                f_base = sfr.v
                @blk.tensor
                def _(t, grp=grp, q_t=q_t, f_base=f_base):
                    t.wait_ge(sv.h, q_t)
                    t.wait_ge(sfr.h, f_base)
                    i = None
                    for j, (c0, wdt, _d) in enumerate(grp):
                        i = t.transpose(out=PT[0:wdt, 128 * j:128 * j + 128],
                                        in_=WQT[:, c0:c0 + wdt],
                                        identity=EYET[:])
                    i.then_inc(st.h, 1)
                st.v = t_t
                e_t = sfr.v + 1
                @blk.scalar
                def _(s, grp=grp, t_t=t_t, wct_free_so=wct_free_so):
                    s.wait_ge(st.h, t_t)
                    s.wait_ge(so.h, wct_free_so)
                    i = None
                    for j, (c0, wdt, d0) in enumerate(grp):
                        i = s.copy(out=WCT[0:wdt, d0:d0 + 128],
                                   in_=PT[0:wdt, 128 * j:128 * j + 128])
                    i.then_inc(sfr.h, 1)
                sfr.v = e_t
            fr_done = sfr.v
            stage_fn(fr_done)
            return prev_xn

        gate_specs = [(128 * c, 128, 128 * c) for c in range(22)]
        def stage_gate(fr):
            t = so.v + 16
            @blk.gpsimd
            def _(gp, fr=fr):
                gp.wait_ge(sfr.h, fr)
                gp.dma_start(out=CG_V, in_=WCT[:, 0:IF]).then_inc(so.h, 16)
            so.v = t
        g_xn = wprep(WGSL.rearrange("(c p) f -> p c f", p=128), 128, 22,
                     gate_specs, stage_gate, 0)

        def stage_up(fr):
            t = so.v + 16
            @blk.gpsimd
            def _(gp, fr=fr):
                gp.wait_ge(sfr.h, fr)
                gp.dma_start(out=CU_V, in_=WCT[:, 0:IF]).then_inc(so.h, 16)
            so.v = t
        u_xn = wprep(WUSL.rearrange("(c p) f -> p c f", p=128), 128, 22,
                     up_specs := [(128 * c, 128, 128 * c) for c in range(22)],
                     stage_up, g_xn)

        dn_specs = []
        dn_pieces = []
        pc = 0
        for hch in range(8):
            for ic in range(3):
                wdt = 128 if ic < 2 else 96
                dn_specs.append((352 * hch + 128 * ic, wdt, 128 * pc))
                dn_pieces.append((128 * pc, wdt, 128 * ic, 128 * hch))
                pc += 1
        def stage_dn(fr):
            t = so.v + 16 * len(dn_pieces)
            @blk.gpsimd
            def _(gp, fr=fr):
                gp.wait_ge(sfr.h, fr)
                for (wc0, rows, r0, h0) in dn_pieces:
                    gp.dma_start(out=CD_V[r0:r0 + rows, h0:h0 + 128],
                                 in_=WCT[0:rows, wc0:wc0 + 128]).then_inc(so.h, 16)
            so.v = t
        d_xn = wprep(WDSL.rearrange("(c p) f -> p c f", p=128), 352, 8,
                     dn_specs, stage_dn, u_xn)

        ag_t = sg.v + 1
        agw_stage_done = so.v
        @blk.gpsimd
        def _(gp, agw=agw_stage_done):
            gp.wait_ge(so.h, agw)
            gp.collective_compute(
                "AllGather", ALU.bypass, ins=[CIN_W], outs=[COUT_W],
                replica_groups=[[0, 1, 2, 3, 4, 5, 6, 7]],
            ).then_inc(sg.h, 1)
        sg.v = ag_t

        if stage == "wq":
            @blk.sync
            def _(sync, ag_t=ag_t):
                sync.wait_ge(sg.h, ag_t)
                sync.dma_start(out=DBG, in_=COUT_W.flatten()).then_inc(so.h, 16)
                sync.wait_ge(so.h, so.v + 16)
            so.v += 16
            return nc

        def GV(r):
            return COUT_W[r, 0:WSEG_G].rearrange("(p c) -> p c", p=128)
        def UV(r):
            return COUT_W[r, WSEG_G:WSEG_G + WSEG_U].rearrange("(p c) -> p c", p=128)

        def q16(v, src_ap, dst_ap, ncols, sc2, rstd_ap=None):
            nb = ncols // 16
            s3 = src_ap.rearrange("p (nb w) -> p nb w", w=16)
            v.tensor_reduce(out=QMX[:, 0:nb], in_=s3, axis=AX.X, op=ALU.max,
                            apply_absolute_value=True)
            v.drain()
            v.reciprocal(out=QRC[:, 0:nb], in_=QMX[:, 0:nb])
            v.drain()
            v.tensor_scalar(out=QRS[:, 0:nb], in0=QRC[:, 0:nb], scalar1=6.0,
                            scalar2=None, op0=ALU.mult)
            if rstd_ap is not None:
                v.tensor_scalar(out=QSC[:, 0:nb], in0=QMX[:, 0:nb],
                                scalar1=rstd_ap, scalar2=sc2,
                                op0=ALU.mult, op1=ALU.mult)
            else:
                v.tensor_scalar(out=QSC[:, 0:nb], in0=QMX[:, 0:nb], scalar1=sc2,
                                scalar2=None, op0=ALU.mult)
            v.tensor_tensor(
                out=XN16[:, 0:ncols].rearrange("p (nb w) -> p nb w", w=16),
                in0=s3,
                in1=QRS[:, 0:nb].unsqueeze(2).broadcast_to([128, nb, 16]),
                op=ALU.mult)
            xb = XN16[:, 0:ncols].bitcast(U16)
            t1, t2, t3 = TU1[:, 0:ncols], TU2[:, 0:ncols], TU3[:, 0:ncols]
            v.tensor_scalar(out=t1, in0=xb, scalar1=C_ABS16, scalar2=None,
                            op0=ALU.bitwise_and)
            v.tensor_scalar(out=t2, in0=t1, scalar1=C_MAX16, scalar2=C_ADD16,
                            op0=ALU.max, op1=ALU.add)
            v.tensor_scalar(out=t2, in0=t2, scalar1=C_MSK16, scalar2=None,
                            op0=ALU.bitwise_and)
            v.tensor_scalar(out=t3, in0=t1, scalar1=C_T2, scalar2=C_W2,
                            op0=ALU.is_gt, op1=ALU.mult)
            v.tensor_scalar(out=t1, in0=t1, scalar1=C_T1, scalar2=C_W1,
                            op0=ALU.is_gt, op1=ALU.mult)
            v.tensor_tensor(out=t1, in0=t1, in1=t3, op=ALU.add)
            v.scalar_tensor_tensor(out=t1, in0=t2, scalar=C_SUB, in1=t1,
                                   op0=ALU.subtract, op1=ALU.add)
            v.scalar_tensor_tensor(out=t1, in0=xb, scalar=C80T[:], in1=t1,
                                   op0=ALU.bitwise_and, op1=ALU.bitwise_or)
            return v.tensor_tensor(
                out=dst_ap.rearrange("p (nb w) -> p nb w", w=16),
                in0=t1.bitcast(F16).rearrange("p (nb w) -> p nb w", w=16),
                in1=QSC[:, 0:nb].unsqueeze(2).broadcast_to([128, nb, 16]),
                op=ALU.mult)

        def emit_ln_xq(ready_sems):
            bn_t = sv.v + 1
            @blk.vector
            def _(v, rd=list(ready_sems)):
                for s_, val in rd:
                    v.wait_ge(s_.h, val)
                for t in range(NT):
                    v.bn_stats(out=BN6[:, 12 * t:12 * t + 6],
                               in_=STREAM[:, 1024 * t:1024 * t + 512])
                    v.bn_stats(out=BN6[:, 12 * t + 6:12 * t + 12],
                               in_=STREAM[:, 1024 * t + 512:1024 * t + 1024])
                v.drain()
                for t in range(NT):
                    v.bn_aggr(out=BN2[:, 2 * t:2 * t + 2],
                              in_=BN6[:, 12 * t:12 * t + 12])
                v.drain()
                b2 = BN2[:].rearrange("p (t two) -> p t two", two=2)
                v.tensor_scalar(out=MNEG[:], in0=b2[:, :, 0], scalar1=-1.0,
                                scalar2=None, op0=ALU.mult)
                v.tensor_scalar(out=TMPV[:], in0=b2[:, :, 1], scalar1=1e-5,
                                scalar2=None, op0=ALU.add)
                v.drain()
                v.reciprocal(out=TMPV[:], in_=TMPV[:])
                v.drain()
                v.tensor_scalar(out=TMPV[:], in0=TMPV[:], scalar1=1.0,
                                scalar2=None, op0=ALU.mult).then_inc(sv.h, 1)
            sv.v = bn_t
            sq_t = ss.v + 1
            @blk.scalar
            def _(s, bn_t=bn_t):
                s.wait_ge(sv.h, bn_t)
                s.activation(out=RSTD[:], in_=TMPV[:],
                             func=AF.Sqrt).then_inc(ss.h, 1)
            ss.v = sq_t
            for t in range(NT):
                q_t = sv.v + 1
                prev_tr = st.v
                @blk.vector
                def _(v, t=t, sq_t=sq_t, prev_tr=prev_tr):
                    if t == 0:
                        v.wait_ge(ss.h, sq_t)
                    else:
                        v.wait_ge(st.h, prev_tr)
                    v.tensor_scalar(out=D16[:], in0=STREAM[:, 1024 * t:1024 * (t + 1)],
                                    scalar1=MNEG[:, t:t + 1], scalar2=None,
                                    op0=ALU.add)
                    q16(v, D16[:], XQTM[:], 1024, 1.0 / 6.0,
                        rstd_ap=RSTD[:, t:t + 1]).then_inc(sv.h, 1)
                sv.v = q_t
                for g in range(2):
                    t_t = st.v + 1
                    ftw = ft.v
                    @blk.tensor
                    def _(tt, t=t, g=g, q_t=q_t, ftw=ftw):
                        tt.wait_ge(sv.h, q_t)
                        tt.wait_ge(ft.h, ftw)
                        i = None
                        for j in range(4):
                            c = 4 * g + j
                            i = tt.transpose(out=PT[:, 128 * j:128 * (j + 1)],
                                             in_=XQTM[:, 128 * c:128 * (c + 1)],
                                             identity=EYET[:])
                        i.then_inc(st.h, 1)
                    st.v = t_t
                    e_t = ft.v + 1
                    @blk.vector
                    def _(v, t=t, g=g, t_t=t_t):
                        v.wait_ge(st.h, t_t)
                        dst = FMA[:].rearrange("p (k r) -> p k r", r=2048) \
                            [:, 4 * g:4 * g + 4, 128 * t:128 * (t + 1)]
                        v.tensor_copy(dst, PT[:].rearrange(
                            "p (k r) -> p k r", r=128)).then_inc(ft.h, 1)
                    ft.v = e_t
            return ft.v

        def emit_ffn(xq_ft, wa_free_st, write_out):
            round_done = [wa_free_st, wa_free_st]
            par_free = {0: fg, 1: fu}
            par_psum = {0: PGA, 1: PGB}
            for r, (o0, o1) in enumerate(ROUNDS):
                Wr = o1 - o0
                ch = Wr // 128
                wb = 12288 * (r % 2)
                @blk.sync
                def _(sync, o0=o0, o1=o1, Wr=Wr, ch=ch, wb=wb,
                      wa_free=round_done[r % 2], ag_t=ag_t):
                    sync.wait_ge(sg.h, ag_t)
                    sync.wait_ge(st.h, wa_free)
                    for k in range(8):
                        sync.dma_start(out=WA[:, wb + 512 * k:wb + 512 * k + Wr],
                                       in_=GV(k)[:, o0:o1]).then_inc(sd.h, 16)
                        sync.dma_start(out=WA[:, wb + 4096 + 512 * k:wb + 4096 + 512 * k + Wr],
                                       in_=UV(k)[:, o0:o1]).then_inc(sd.h, 16)
                    for j in range(ch):
                        g0 = o0 + 128 * j
                        left = 128
                        while left > 0:
                            rk = g0 // 352
                            loc = g0 - 352 * rk
                            n = min(left, 352 - loc)
                            srcw = COUT_W[rk, WSEG_G + WSEG_U + loc * D:
                                          WSEG_G + WSEG_U + (loc + n) * D] \
                                .rearrange("(q h) -> q h", h=D)
                            p0 = 128 - left
                            sync.dma_start(
                                out=WA[p0:p0 + n, wb + 8192 + 1024 * j:wb + 8192 + 1024 * (j + 1)],
                                in_=srcw).then_inc(sd.h, 16)
                            g0 += n
                            left -= n
                ndma = 16
                for j in range(ch):
                    g0 = o0 + 128 * j
                    left = 128
                    while left > 0:
                        rk = g0 // 352
                        n = min(left, 352 - (g0 - 352 * rk))
                        g0 += n
                        left -= n
                        ndma += 1
                sd.v += 16 * ndma
                wload = sd.v
                def emit_gateup(t):
                    par = t % 2
                    PP = par_psum[par]
                    fsem = par_free[par]
                    mm_t = st.v + 1
                    fw = fsem.v
                    @blk.tensor
                    def _(tb, t=t, PP=PP, fsem=fsem, fw=fw, Wr=Wr, wb=wb,
                          wload=wload):
                        tb.wait_ge(sd.h, wload)
                        tb.wait_ge(ft.h, xq_ft)
                        tb.wait_ge(fsem.h, fw)
                        i = None
                        for k in range(8):
                            lhs = FMA[:, 2048 * k + 128 * t:2048 * k + 128 * (t + 1)]
                            i = tb.matmul(out=PP[:, 0:Wr], lhsT=lhs,
                                          rhs=WA[:, wb + 512 * k:wb + 512 * k + Wr],
                                          start=(k == 0), stop=(k == 7))
                            i = tb.matmul(out=PP[:, 512:512 + Wr], lhsT=lhs,
                                          rhs=WA[:, wb + 4096 + 512 * k:wb + 4096 + 512 * k + Wr],
                                          start=(k == 0), stop=(k == 7))
                        i.then_inc(st.h, 1)
                    st.v = mm_t
                    return mm_t

                def emit_silu_h(t, mm_t):
                    par = t % 2
                    PP = par_psum[par]
                    fsem = par_free[par]
                    sil_t = ss.v + 1
                    @blk.scalar
                    def _(s, PP=PP, par=par, mm_t=mm_t, Wr=Wr):
                        s.wait_ge(st.h, mm_t)
                        s.activation(out=SIL[par][:, 0:Wr], in_=PP[:, 0:Wr],
                                     func=AF.Silu).then_inc(ss.h, 1)
                    ss.v = sil_t
                    h_t = sv.v + 1
                    @blk.vector
                    def _(v, PP=PP, par=par, fsem=fsem, sil_t=sil_t, Wr=Wr):
                        v.wait_ge(ss.h, sil_t)
                        i = v.tensor_tensor(out=H16[par][:, 0:Wr],
                                            in0=SIL[par][:, 0:Wr],
                                            in1=PP[:, 512:512 + Wr], op=ALU.mult)
                        fsem.v += 1
                        i.then_inc(fsem.h, 1)
                        q16(v, H16[par][:, 0:Wr], QH[par][:, 0:Wr], Wr, 1.0 / 12.0) \
                            .then_inc(sv.h, 1)
                    sv.v = h_t
                    return h_t

                def emit_tail(t, h_t):
                    par = t % 2
                    t_t = st.v + 1
                    ftw = ft.v
                    @blk.tensor
                    def _(tb, par=par, h_t=h_t, ftw=ftw, ch=ch):
                        tb.wait_ge(sv.h, h_t)
                        tb.wait_ge(ft.h, ftw)
                        i = None
                        for j in range(ch):
                            i = tb.transpose(out=PT[:, 128 * j:128 * (j + 1)],
                                             in_=QH[par][:, 128 * j:128 * (j + 1)],
                                             identity=EYET[:])
                        i.then_inc(st.h, 1)
                    st.v = t_t
                    e_t = ft.v + 1
                    @blk.vector
                    def _(v, par=par, t_t=t_t, ch=ch):
                        v.wait_ge(st.h, t_t)
                        v.tensor_copy(QHT[par][:, 0:128 * ch],
                                      PT[:, 0:128 * ch]).then_inc(ft.h, 1)
                    ft.v = e_t
                    dn_t = st.v + 1
                    fdw = fd.v
                    @blk.tensor
                    def _(tb, par=par, e_t=e_t, fdw=fdw, ch=ch, wb=wb):
                        tb.wait_ge(ft.h, e_t)
                        tb.wait_ge(fd.h, fdw)
                        i = None
                        for kc in range(ch):
                            for (a, bnd) in ((0, 512), (512, 1024)):
                                i = tb.matmul(
                                    out=PDN[:, a:bnd],
                                    lhsT=QHT[par][:, 128 * kc:128 * (kc + 1)],
                                    rhs=WA[:, wb + 8192 + 1024 * kc + a:wb + 8192 + 1024 * kc + bnd],
                                    start=(kc == 0), stop=(kc == ch - 1))
                        i.then_inc(st.h, 1)
                    st.v = dn_t
                    return dn_t

                def emit_add(t, dn_t):
                        add_t = sv.v + 1
                        final = write_out is not None and r == len(ROUNDS) - 1
                        @blk.vector
                        def _(v, t=t, dn_t=dn_t, final=final):
                            v.wait_ge(st.h, dn_t)
                            strm = STREAM[:, 1024 * t:1024 * (t + 1)]
                            if final:
                                i = v.tensor_tensor(out=XOUT[:], in0=PDN[:, 0:1024],
                                                        in1=strm, op=ALU.add)
                            else:
                                i = v.tensor_tensor(out=strm, in0=PDN[:, 0:1024],
                                                        in1=strm, op=ALU.add)
                            fd.v += 1
                            i.then_inc(fd.h, 1)
                            v.tensor_scalar(out=MRKV[:], in0=MRKV[:], scalar1=1.0,
                                                scalar2=None, op0=ALU.mult).then_inc(sv.h, 1)
                        sv.v = add_t
                        if final:
                            o_t = so.v + 16
                            @blk.sync
                            def _(sync, t=t, add_t=add_t, write_out=write_out):
                                sync.wait_ge(sv.h, add_t)
                                sync.dma_start(out=write_out[128 * t:128 * (t + 1), :],
                                                   in_=XOUT[:]).then_inc(so.h, 16)
                            so.v = o_t
                            xg = so.v
                            @blk.vector
                            def _(v, xg=xg):
                                v.wait_ge(so.h, xg)

                prev = None
                pend_add = None
                for t in range(NT + 1):
                    if t < NT:
                        mm_t = emit_gateup(t)
                    if prev is not None:
                        pend = (prev[0], emit_tail(prev[0], prev[1]))
                    else:
                        pend = None
                    if t < NT:
                        h_t = emit_silu_h(t, mm_t)
                        prev = (t, h_t)
                    if pend is not None:
                        emit_add(pend[0], pend[1])
                round_done[r % 2] = st.v
            return st.v

        def emit_attn(stream_sv, upto="all"):
            ps_u = [0]
            def ps_wait(eng):
                eng.wait_ge(fs.h, ps_u[0])
            wqkv_ld = sg.v + 16
            wa_free = st.v
            @blk.gpsimd
            def _(gp, wa_free=wa_free):
                gp.wait_ge(st.h, wa_free)
                gp.dma_start(
                    out=WA[:].rearrange("p (k c) -> p k c", c=3 * D),
                    in_=WQKV.rearrange("(k p) c -> p k c", p=128)).then_inc(sg.h, 16)
            sg.v = wqkv_ld
            bn_t = sv.v + 1
            @blk.vector
            def _(v, stream_sv=stream_sv):
                v.wait_ge(sv.h, stream_sv)
                for t in range(NT):
                    v.bn_stats(out=BN6[:, 12 * t:12 * t + 6],
                               in_=STREAM[:, 1024 * t:1024 * t + 512])
                    v.bn_stats(out=BN6[:, 12 * t + 6:12 * t + 12],
                               in_=STREAM[:, 1024 * t + 512:1024 * t + 1024])
                v.drain()
                for t in range(NT):
                    v.bn_aggr(out=BN2[:, 2 * t:2 * t + 2],
                              in_=BN6[:, 12 * t:12 * t + 12])
                v.drain()
                b2 = BN2[:].rearrange("p (t two) -> p t two", two=2)
                v.tensor_scalar(out=MNEG[:], in0=b2[:, :, 0], scalar1=-1.0,
                                scalar2=None, op0=ALU.mult)
                v.tensor_scalar(out=TMPV[:], in0=b2[:, :, 1], scalar1=1e-5,
                                scalar2=None, op0=ALU.add)
                v.drain()
                v.reciprocal(out=TMPV[:], in_=TMPV[:])
                v.drain()
                v.tensor_scalar(out=TMPV[:], in0=TMPV[:], scalar1=1.0,
                                scalar2=None, op0=ALU.mult).then_inc(sv.h, 1)
            sv.v = bn_t
            sq_t = ss.v + 1
            @blk.scalar
            def _(s, bn_t=bn_t):
                s.wait_ge(sv.h, bn_t)
                s.activation(out=RSTD[:], in_=TMPV[:],
                             func=AF.Sqrt).then_inc(ss.h, 1)
            ss.v = sq_t
            for t in range(NT):
                ln_t = sv.v + 1
                @blk.vector
                def _(v, t=t, sq_t=sq_t):
                    if t == 0:
                        v.wait_ge(ss.h, sq_t)
                    v.tensor_scalar(out=LNT[:], in0=STREAM[:, 1024 * t:1024 * (t + 1)],
                                    scalar1=MNEG[:, t:t + 1], scalar2=RSTD[:, t:t + 1],
                                    op0=ALU.add, op1=ALU.mult).then_inc(sv.h, 1)
                sv.v = ln_t
                ym_t = st.v + 1
                fgw = fg.v
                @blk.tensor
                def _(tt, t=t, ln_t=ln_t, fgw=fgw):
                    tt.wait_ge(sv.h, ln_t)
                    if t == 0:
                        tt.wait_ge(fg.h, fgw)
                    i = tt.matmul(out=PGA[0:16, 0:512], lhsT=ETBLT[:, 16 * t:16 * (t + 1)],
                                  rhs=LNT[:, 0:512], start=(t == 0), stop=(t == NT - 1))
                    i = tt.matmul(out=PGA[0:16, 512:1024], lhsT=ETBLT[:, 16 * t:16 * (t + 1)],
                                  rhs=LNT[:, 512:1024], start=(t == 0), stop=(t == NT - 1))
                    i.then_inc(st.h, 1)
                st.v = ym_t
                @blk.vector
                def _(v, ym_t=ym_t):
                    v.wait_ge(st.h, ym_t)
            ylm_t = sv.v + 1
            @blk.vector
            def _(v):
                i = v.tensor_scalar(out=YLM[:], in0=PGA[0:16, 0:1024],
                                    scalar1=1.0 / 128.0, scalar2=None, op0=ALU.mult)
                fg.v += 1
                i.then_inc(fg.h, 1)
                v.tensor_scalar(out=MRKV[:], in0=MRKV[:], scalar1=1.0, scalar2=None, op0=ALU.mult).then_inc(sv.h, 1)
            sv.v = ylm_t
            if upto == "p1":
                return sv.v
            agy_t = sg.v + 1
            @blk.gpsimd
            def _(gp, ylm_t=ylm_t):
                gp.wait_ge(sv.h, ylm_t)
                gp.dma_start(out=CIN_Y.rearrange("(p c) -> p c", p=16),
                             in_=YLM[:]).then_inc(sg.h, 16)
                gp.wait_ge(sg.h, sg.v + 16)
                gp.collective_compute(
                    "AllGather", ALU.bypass, ins=[CIN_Y], outs=[COUT_Y],
                    replica_groups=GROUPS4).then_inc(sg.h, 1)
            sg.v += 17
            agy_done = sg.v
            ylmf_ld = sd.v + 16
            @blk.sync
            def _(sync, agy_done=agy_done):
                sync.wait_ge(sg.h, agy_done)
                sync.dma_start(out=YLMF[:],
                               in_=COUT_Y.rearrange("r (p c) -> (r p) c", c=D)
                               ).then_inc(sd.h, 16)
            sd.v = ylmf_ld
            ftw = ft.v
            yt_t = st.v + 1
            @blk.tensor
            def _(tt, ylmf_ld=ylmf_ld, ftw=ftw):
                tt.wait_ge(sd.h, ylmf_ld)
                tt.wait_ge(ft.h, ftw)
                i = None
                for c in range(8):
                    i = tt.transpose(out=PT[:, 64 * c:64 * (c + 1)],
                                     in_=YLMF[0:64, 128 * c:128 * (c + 1)],
                                     identity=EYET[0:64, 0:64])
                i.then_inc(st.h, 1)
            st.v = yt_t
            ye_t = ft.v + 1
            @blk.scalar
            def _(s, yt_t=yt_t):
                s.wait_ge(st.h, yt_t)
                s.copy(out=YLMT[:], in_=PT[:]).then_inc(ft.h, 1)
            ft.v = ye_t
            if upto == "p2":
                @blk.vector
                def _(v, ye_t=ye_t):
                    v.wait_ge(ft.h, ye_t)
                    v.tensor_scalar(out=MRKV[:], in0=MRKV[:], scalar1=1.0, scalar2=None, op0=ALU.mult).then_inc(sv.h, 1)
                sv.v += 1
                return sv.v
            for dstT, cof in ((QLMF, 0), (KLMF, 1024)):
                mm_t = st.v + 1
                @blk.tensor
                def _(tt, dstT=dstT, cof=cof, ye_t=ye_t, wqkv_ld=wqkv_ld,
                      psu=ps_u[0]):
                    tt.wait_ge(ft.h, ye_t)
                    tt.wait_ge(sg.h, wqkv_ld)
                    tt.wait_ge(fs.h, psu)
                    i = None
                    for och in range(8):
                        for k in range(8):
                            i = tt.matmul(
                                out=PS[:, 64 * och:64 * (och + 1)],
                                lhsT=WA[:, 3072 * k + cof + 128 * och:
                                        3072 * k + cof + 128 * (och + 1)],
                                rhs=YLMT[:, 64 * k:64 * (k + 1)],
                                start=(k == 0), stop=(k == 7))
                    i.then_inc(st.h, 1)
                st.v = mm_t
                ev_t = ss.v + 1
                @blk.scalar
                def _(s, dstT=dstT, mm_t=mm_t):
                    s.wait_ge(st.h, mm_t)
                    i = s.copy(out=dstT[:], in_=PS[:])
                    fs.v += 1
                    i.then_inc(fs.h, 1)
                    s.copy(out=MRKS[:, 0:1], in_=MRKS[:, 1:2]).then_inc(ss.h, 1)
                ss.v = ev_t
                ps_u[0] = fs.v
            lm_ss = ss.v
            if upto == "p3":
                @blk.vector
                def _(v, lm_ss=lm_ss):
                    v.wait_ge(ss.h, lm_ss)
                    v.tensor_scalar(out=MRKV[:], in0=MRKV[:], scalar1=1.0, scalar2=None, op0=ALU.mult).then_inc(sv.h, 1)
                sv.v += 1
                return sv.v
            z2_t = st.v + 1
            @blk.tensor
            def _(tt, lm_ss=lm_ss, psu=ps_u[0]):
                tt.wait_ge(ss.h, lm_ss)
                tt.wait_ge(fs.h, psu)
                i = None
                for h in range(16):
                    p, hh = h // 2, 64 * (h % 2)
                    i = tt.matmul(
                        out=PS[hh:hh + 64, 64 * p:64 * (p + 1)],
                        lhsT=QLMF[hh:hh + 64, 64 * (h // 2):64 * (h // 2) + 64],
                        rhs=KLMF[hh:hh + 64, 64 * (h // 2):64 * (h // 2) + 64],
                        start=True, stop=True)
                i.then_inc(st.h, 1)
            st.v = z2_t
            e2_t = ss.v + 1
            @blk.scalar
            def _(s, z2_t=z2_t):
                s.wait_ge(st.h, z2_t)
                s.activation(out=E2SB[:], in_=PS[:], func=AF.Exp,
                             scale=0.125).then_inc(ss.h, 1)
            ss.v = e2_t
            k2_t = sv.v + 1
            @blk.vector
            def _(v, e2_t=e2_t):
                v.wait_ge(ss.h, e2_t)
                v.tensor_scalar(out=MRKV[:], in0=MRKV[:], scalar1=1.0, scalar2=None, op0=ALU.mult).then_inc(fs.h, 1)
                v.tensor_reduce(out=SM8[:],
                                in_=E2SB[:].rearrange("p (pr c) -> p pr c", c=64),
                                axis=AX.X, op=ALU.add)
                v.drain()
                v.reciprocal(out=SM8[:], in_=SM8[:])
                v.drain()
                v.tensor_tensor(out=K2N[:].rearrange("p (pr c) -> p pr c", c=64),
                                in0=E2SB[:].rearrange("p (pr c) -> p pr c", c=64),
                                in1=SM8[:].unsqueeze(2).broadcast_to([128, 8, 64]),
                                op=ALU.mult).then_inc(sv.h, 1)
            fs.v += 1
            ps_u[0] = fs.v
            sv.v = k2_t
            if upto == "p3a":
                @blk.vector
                def _(v, base=sv.v):
                    v.wait_ge(sv.h, base)
                    v.tensor_scalar(out=MRKV[:], in0=MRKV[:], scalar1=1.0, scalar2=None, op0=ALU.mult).then_inc(sv.h, 1)
                sv.v += 1
                return sv.v
            fdw = fd.v
            dn_t = st.v + 1
            @blk.tensor
            def _(tt, k2_t=k2_t, fdw=fdw):
                tt.wait_ge(sv.h, k2_t)
                tt.wait_ge(fd.h, fdw)
                i = None
                for h in range(16):
                    p, c = h // 2, h % 2
                    i = tt.matmul(out=PDN[0:1, 64 * h:64 * (h + 1)],
                                  lhsT=O2T[:, c:c + 1],
                                  rhs=K2N[:, 64 * p:64 * (p + 1)],
                                  start=True, stop=True)
                i.then_inc(st.h, 1)
            st.v = dn_t
            rd_t = sv.v + 1
            @blk.vector
            def _(v, dn_t=dn_t):
                v.wait_ge(st.h, dn_t)
                v.tensor_reduce(out=RDEN[:],
                                in_=PDN[0:1, :].rearrange("p (h c) -> p h c", c=64),
                                axis=AX.X, op=ALU.max)
                fd.v += 1
                v.tensor_scalar(out=MRKV[:], in0=MRKV[:], scalar1=1.0, scalar2=None, op0=ALU.mult).then_inc(fd.h, 1)
                v.drain()
                v.reciprocal(out=RDEN[:], in_=RDEN[:])
                v.drain()
                v.tensor_scalar(out=RDENB[:], in0=RDEN[:], scalar1=1.0,
                                scalar2=None, op0=ALU.mult).then_inc(sv.h, 1)
            sv.v = rd_t
            if upto == "p3b":
                @blk.vector
                def _(v, base=sv.v):
                    v.wait_ge(sv.h, base)
                    v.tensor_scalar(out=MRKV[:], in0=MRKV[:], scalar1=1.0, scalar2=None, op0=ALU.mult).then_inc(sv.h, 1)
                sv.v += 1
                return sv.v
            rdm_t = st.v + 1
            @blk.tensor
            def _(tt, rd_t=rd_t, psu=ps_u[0]):
                tt.wait_ge(sv.h, rd_t)
                tt.wait_ge(fs.h, psu)
                tt.matmul(out=PS[:, 0:16], lhsT=ONESRT[:], rhs=RDENB[:],
                          start=True, stop=True).then_inc(st.h, 1)
            st.v = rdm_t
            v0_t = sv.v + 1
            @blk.vector
            def _(v, rdm_t=rdm_t):
                v.wait_ge(st.h, rdm_t)
                i = v.tensor_scalar(out=RD128[:], in0=PS[:, 0:16], scalar1=1.0,
                                    scalar2=None, op0=ALU.mult)
                fs.v += 1
                i.then_inc(fs.h, 1)
                v.drain()
                v.tensor_scalar(out=RDE[:], in0=RD128[:].rearrange(
                    "p (pr two) -> p pr two", two=2)[:, :, 0],
                    scalar1=1.0, scalar2=None, op0=ALU.mult)
                v.tensor_scalar(out=RDO[:], in0=RD128[:].rearrange(
                    "p (pr two) -> p pr two", two=2)[:, :, 1],
                    scalar1=1.0, scalar2=None, op0=ALU.mult)
                v.drain()
                v.tensor_tensor(
                    out=K2ND[0:64, :].rearrange("p (pr c) -> p pr c", c=64),
                    in0=K2N[0:64, :].rearrange("p (pr c) -> p pr c", c=64),
                    in1=RDE[0:64, :].unsqueeze(2).broadcast_to([64, 8, 64]),
                    op=ALU.mult)
                v.tensor_tensor(
                    out=K2ND[64:128, :].rearrange("p (pr c) -> p pr c", c=64),
                    in0=K2N[64:128, :].rearrange("p (pr c) -> p pr c", c=64),
                    in1=RDO[64:128, :].unsqueeze(2).broadcast_to([64, 8, 64]),
                    op=ALU.mult).then_inc(sv.h, 1)
            ps_u[0] = fs.v
            sv.v = v0_t
            if upto == "p4":
                @blk.vector
                def _(v, v0_t=v0_t):
                    v.wait_ge(sv.h, v0_t)
                    v.tensor_scalar(out=MRKV[:], in0=MRKV[:], scalar1=1.0, scalar2=None, op0=ALU.mult).then_inc(sv.h, 1)
                sv.v += 1
                return sv.v
            for srcT, dstT in ((K2N, KTP), (K2ND, VSB)):
                for g0 in (0, 4):
                    t_t = st.v + 1
                    ftw = ft.v
                    @blk.tensor
                    def _(tt, srcT=srcT, g0=g0, v0_t=v0_t, ftw=ftw):
                        tt.wait_ge(sv.h, v0_t)
                        tt.wait_ge(ft.h, ftw)
                        i = None
                        for j in range(4):
                            i = tt.transpose(out=PT[0:64, 128 * j:128 * (j + 1)],
                                             in_=srcT[:, 64 * (g0 + j):64 * (g0 + j + 1)],
                                             identity=EYET[:])
                        i.then_inc(st.h, 1)
                    st.v = t_t
                    e_t = ft.v + 1
                    @blk.scalar
                    def _(s, dstT=dstT, g0=g0, t_t=t_t):
                        s.wait_ge(st.h, t_t)
                        s.copy(out=dstT[0:64, 128 * g0:128 * (g0 + 4)],
                               in_=PT[0:64, 0:512]).then_inc(ft.h, 1)
                    ft.v = e_t
            prep_done_ft = ft.v
            if upto == "prep":
                return sv.v

            p_first = [True]
            rd_sv = [0, 0]
            qk_psu1 = [None]
            for g in range(4):
                for tt_i in range(4):
                    t = 4 * g + tt_i
                    ln_t = sv.v + 1
                    prev_tr = st.v
                    @blk.vector
                    def _(v, t=t, prev_tr=prev_tr):
                        v.wait_ge(st.h, prev_tr)
                        v.tensor_scalar(out=LNT[:], in0=STREAM[:, 1024 * t:1024 * (t + 1)],
                                        scalar1=MNEG[:, t:t + 1], scalar2=RSTD[:, t:t + 1],
                                        op0=ALU.add, op1=ALU.mult).then_inc(sv.h, 1)
                    sv.v = ln_t
                    for gg in range(2):
                        t_t = st.v + 1
                        ftw = ft.v
                        @blk.tensor
                        def _(tb, gg=gg, ln_t=ln_t, ftw=ftw):
                            tb.wait_ge(sv.h, ln_t)
                            tb.wait_ge(ft.h, ftw)
                            i = None
                            for j in range(4):
                                c = 4 * gg + j
                                i = tb.transpose(out=PT[:, 128 * j:128 * (j + 1)],
                                                 in_=LNT[:, 128 * c:128 * (c + 1)],
                                                 identity=EYET[:])
                            i.then_inc(st.h, 1)
                        st.v = t_t
                        e_t = ft.v + 1
                        @blk.vector
                        def _(v, gg=gg, tt_i=tt_i, t_t=t_t):
                            v.wait_ge(st.h, t_t)
                            dst = LNF[:].rearrange("p (k r) -> p k r", r=512) \
                                [:, 4 * gg:4 * gg + 4, 128 * tt_i:128 * (tt_i + 1)]
                            v.tensor_copy(dst, PT[:].rearrange(
                                "p (k r) -> p k r", r=128)).then_inc(ft.h, 1)
                        ft.v = e_t
                lnf_ft = ft.v
                qki = [0]
                for dst_kind in ("q", "k"):
                    cof = 0 if dst_kind == "q" else 1024
                    for och in range(8):
                        buf = qki[0] % 2
                        qki[0] += 1
                        OPS = PS[:, 0:512] if buf == 0 else PGB[:, 512:1024]
                        if buf == 0:
                            psu, fuw = ps_u[0], None
                        elif qk_psu1[0] is None:
                            psu, fuw = 0, fu.v
                        else:
                            psu, fuw = qk_psu1[0], None
                        mm_t = st.v + 1
                        @blk.tensor
                        def _(tb, och=och, cof=cof, lnf_ft=lnf_ft, psu=psu,
                              fuw=fuw, OPS=OPS):
                            tb.wait_ge(ft.h, lnf_ft)
                            tb.wait_ge(fs.h, psu)
                            if fuw is not None:
                                tb.wait_ge(fu.h, fuw)
                            i = None
                            for k in range(8):
                                i = tb.matmul(
                                    out=OPS,
                                    lhsT=WA[:, 3072 * k + cof + 128 * och:
                                            3072 * k + cof + 128 * (och + 1)],
                                    rhs=LNF[:, 512 * k:512 * (k + 1)],
                                    start=(k == 0), stop=(k == 7))
                            i.then_inc(st.h, 1)
                        st.v = mm_t
                        e_t = sv.v + 1
                        @blk.vector
                        def _(v, och=och, g=g, dst_kind=dst_kind, mm_t=mm_t,
                              OPS=OPS):
                            v.wait_ge(st.h, mm_t)
                            if dst_kind == "q":
                                dst = FMA[:, 2048 * och + 512 * g:2048 * och + 512 * (g + 1)]
                            else:
                                dst = KFM[:, 512 * och:512 * (och + 1)]
                            i = v.tensor_copy(dst, OPS)
                            fs.v += 1
                            i.then_inc(fs.h, 1)
                            v.tensor_scalar(out=MRKV[:], in0=MRKV[:], scalar1=1.0,
                                            scalar2=None, op0=ALU.mult).then_inc(sv.h, 1)
                        sv.v = e_t
                        ps_u[0] = fs.v
                        if buf == 1:
                            qk_psu1[0] = fs.v
                qk_sv = sv.v
                for tt_i in range(4):
                    mm_t = st.v + 1
                    fgw = fg.v
                    @blk.tensor
                    def _(tb, tt_i=tt_i, lnf_ft=lnf_ft, fgw=fgw):
                        tb.wait_ge(ft.h, lnf_ft)
                        tb.wait_ge(fg.h, fgw)
                        i = None
                        for k in range(8):
                            lhs = LNF[:, 512 * k + 128 * tt_i:512 * k + 128 * (tt_i + 1)]
                            for (a, bnd) in ((0, 512), (512, 1024)):
                                i = tb.matmul(out=PGA[:, a:bnd], lhsT=lhs,
                                              rhs=WA[:, 3072 * k + 2048 + a:3072 * k + 2048 + bnd],
                                              start=(k == 0), stop=(k == 7))
                        i.then_inc(st.h, 1)
                    st.v = mm_t
                    ve_t = sv.v + 1
                    @blk.vector
                    def _(v, tt_i=tt_i, mm_t=mm_t):
                        v.wait_ge(st.h, mm_t)
                        i = v.tensor_scalar(out=VTMS[:, 1024 * tt_i:1024 * (tt_i + 1)],
                                            in0=PGA[:, 0:1024], scalar1=1.0,
                                            scalar2=None, op0=ALU.mult)
                        fg.v += 1
                        i.then_inc(fg.h, 1)
                        v.tensor_scalar(out=MRKV[:], in0=MRKV[:], scalar1=1.0, scalar2=None, op0=ALU.mult).then_inc(sv.h, 1)
                    sv.v = ve_t
                vtm_sv = sv.v
                E3PB = [E3P, QH[0]]

                def b1_z3(p):
                    z_t = st.v + 1
                    @blk.tensor
                    def _(tb, p=p, qk_sv=qk_sv, psu=ps_u[0]):
                        tb.wait_ge(sv.h, qk_sv)
                        tb.wait_ge(fs.h, psu)
                        i = None
                        for hh in (0, 64):
                            i = tb.matmul(
                                out=PS[hh:hh + 64, 0:512],
                                lhsT=QLMF[hh:hh + 64, 64 * p:64 * (p + 1)],
                                rhs=KFM[hh:hh + 64, 512 * p:512 * (p + 1)],
                                start=True, stop=True)
                        i.then_inc(st.h, 1)
                    st.v = z_t
                    return z_t

                def b1_exp(p, z_t):
                    par = p % 2
                    e_t = ss.v + 1
                    rdw = rd_sv[par]
                    @blk.scalar
                    def _(s, par=par, z_t=z_t, rdw=rdw):
                        s.wait_ge(st.h, z_t)
                        s.wait_ge(sv.h, rdw)
                        i = s.activation(out=E3PB[par][:], in_=PS[:, 0:512],
                                         func=AF.Exp, scale=0.125)
                        fs.v += 1
                        i.then_inc(fs.h, 1)
                        s.activation(out=MRKS[:, 0:1], in_=MRKS[:, 1:2],
                                     func=AF.Exp).then_inc(ss.h, 1)
                    ss.v = e_t
                    ps_u[0] = fs.v
                    return e_t

                def b1_reduce(p, e_t):
                    par = p % 2
                    sl_t = sv.v + 1
                    @blk.vector
                    def _(v, p=p, g=g, par=par, e_t=e_t):
                        v.wait_ge(ss.h, e_t)
                        v.tensor_reduce(out=SLOC4[:, 4 * p + g:4 * p + g + 1],
                                        in_=E3PB[par][:], axis=AX.X,
                                        op=ALU.add).then_inc(sv.h, 1)
                    sv.v = sl_t
                    rd_sv[par] = sl_t

                def b1_tail(p, e_t):
                    par = p % 2
                    t_t = st.v + 1
                    ftw = ft.v
                    @blk.tensor
                    def _(tb, par=par, e_t=e_t, ftw=ftw):
                        tb.wait_ge(ss.h, e_t)
                        tb.wait_ge(ft.h, ftw)
                        i = None
                        for j in range(4):
                            i = tb.transpose(out=PT[:, 128 * j:128 * (j + 1)],
                                             in_=E3PB[par][:, 128 * j:128 * (j + 1)],
                                             identity=EYET[:])
                        i.then_inc(st.h, 1)
                    st.v = t_t
                    e2t_ = ft.v + 1
                    @blk.vector
                    def _(v, t_t=t_t):
                        v.wait_ge(st.h, t_t)
                        v.tensor_copy(E3TS[:], PT[:, 0:512]).then_inc(ft.h, 1)
                    ft.v = e2t_
                    pm_t = st.v + 1
                    fuw = fu.v
                    @blk.tensor
                    def _(tb, p=p, e2t_=e2t_, vtm_sv=vtm_sv, fuw=fuw):
                        tb.wait_ge(ft.h, e2t_)
                        tb.wait_ge(sv.h, vtm_sv)
                        tb.wait_ge(fu.h, fuw)
                        i = None
                        for tc in range(4):
                            i = tb.matmul(
                                out=PGB[:, 0:128],
                                lhsT=E3TS[:, 128 * tc:128 * (tc + 1)],
                                rhs=VTMS[:, 1024 * tc + 128 * p:1024 * tc + 128 * (p + 1)],
                                start=(tc == 0), stop=(tc == 3))
                        i.then_inc(st.h, 1)
                    st.v = pm_t
                    pe_t = sv.v + 1
                    @blk.vector
                    def _(v, p=p, g=g, pm_t=pm_t):
                        v.wait_ge(st.h, pm_t)
                        if g == 0:
                            i = v.tensor_scalar(out=P_ACC[:, 128 * p:128 * (p + 1)],
                                                in0=PGB[:, 0:128], scalar1=1.0,
                                                scalar2=None, op0=ALU.mult)
                        else:
                            i = v.tensor_tensor(out=P_ACC[:, 128 * p:128 * (p + 1)],
                                                in0=PGB[:, 0:128],
                                                in1=P_ACC[:, 128 * p:128 * (p + 1)],
                                                op=ALU.add)
                        fu.v += 1
                        i.then_inc(fu.h, 1)
                        v.tensor_scalar(out=MRKV[:], in0=MRKV[:], scalar1=1.0,
                                        scalar2=None, op0=ALU.mult).then_inc(sv.h, 1)
                    sv.v = pe_t

                prevp = None
                for p in range(9):
                    if p < 8:
                        z_t = b1_z3(p)
                    if prevp is not None:
                        b1_tail(prevp[0], prevp[1])
                    if p < 8:
                        e_t = b1_exp(p, z_t)
                        b1_reduce(p, e_t)
                        prevp = (p, e_t)
                    else:
                        prevp = None
                p_first[0] = False
            b1_st = st.v
            if upto == "b1":
                @blk.vector
                def _(v, b1_st=b1_st):
                    v.wait_ge(st.h, b1_st)
                    v.tensor_scalar(out=MRKV[:], in0=MRKV[:], scalar1=1.0, scalar2=None, op0=ALU.mult).then_inc(sv.h, 1)
                sv.v += 1
                return sv.v

            slc_t = sv.v + 1
            @blk.vector
            def _(v, b1_st=b1_st):
                v.wait_ge(st.h, b1_st)
                v.tensor_reduce(out=SLOC[:],
                                in_=SLOC4[:].rearrange("p (pr c) -> p pr c", c=4),
                                axis=AX.X, op=ALU.add)
                v.tensor_scalar(
                    out=PCMP[0:64, 0:512].rearrange("p (pr c) -> p pr c", c=64),
                    in0=P_ACC[0:64, :].rearrange("p (pr c) -> p pr c", c=128)[:, :, 0:64],
                    scalar1=1.0, scalar2=None, op0=ALU.mult)
                v.tensor_scalar(
                    out=PCMP[64:128, 0:512].rearrange("p (pr c) -> p pr c", c=64),
                    in0=P_ACC[64:128, :].rearrange("p (pr c) -> p pr c", c=128)[:, :, 64:128],
                    scalar1=1.0, scalar2=None, op0=ALU.mult)
                v.tensor_scalar(out=PCMP[:, 512:520], in0=SLOC[:], scalar1=1.0,
                                scalar2=None, op0=ALU.mult).then_inc(sv.h, 1)
            sv.v = slc_t
            agp_t = sg.v + 17
            @blk.gpsimd
            def _(gp, slc_t=slc_t):
                gp.wait_ge(sv.h, slc_t)
                gp.dma_start(out=CIN_P.rearrange("(p c) -> p c", p=128),
                             in_=PCMP[:]).then_inc(sg.h, 16)
                gp.wait_ge(sg.h, sg.v + 16)
                gp.collective_compute(
                    "AllGather", ALU.bypass, ins=[CIN_P], outs=[COUT_P],
                    replica_groups=GROUPS4).then_inc(sg.h, 1)
            sg.v = agp_t
            rb_marks = []
            for r in range(4):
                prb_ld = sd.v + 16
                prev_acc = sv.v
                @blk.sync
                def _(sync, r=r, agp_t=agp_t, prev_acc=prev_acc):
                    if r == 0:
                        sync.wait_ge(sg.h, agp_t)
                    else:
                        sync.wait_ge(sv.h, prev_acc)
                    sync.dma_start(
                        out=PRB[:],
                        in_=COUT_P[r, :].rearrange("(p c) -> p c", p=128)
                    ).then_inc(sd.h, 16)
                sd.v = prb_ld
                acc_t = sv.v + 1
                @blk.vector
                def _(v, r=r, prb_ld=prb_ld):
                    v.wait_ge(sd.h, prb_ld)
                    if r == 0:
                        v.tensor_scalar(out=PCMP[:], in0=PRB[:], scalar1=1.0,
                                        scalar2=None, op0=ALU.mult).then_inc(sv.h, 1)
                    else:
                        v.tensor_tensor(out=PCMP[:], in0=PCMP[:], in1=PRB[:],
                                        op=ALU.add).then_inc(sv.h, 1)
                sv.v = acc_t
            prb_ld = sv.v

            vts0_t = sv.v + 1
            @blk.vector
            def _(v, pdt=prep_done_ft):
                v.wait_ge(ft.h, pdt)
                v.tensor_scalar(out=VTS[:], in0=K2ND[:], scalar1=1.0,
                                scalar2=None, op0=ALU.mult).then_inc(sv.h, 1)
            sv.v = vts0_t
            for it in range(NS_ITERS):
                kv_t = st.v + 1
                fdw = fd.v
                @blk.tensor
                def _(tb, psu=ps_u[0], fdw=fdw, first=(it == 0),
                      vts0_t=vts0_t):
                    if first:
                        tb.wait_ge(sv.h, vts0_t)
                    tb.wait_ge(fs.h, psu)
                    tb.wait_ge(fd.h, fdw)
                    i = None
                    for h in range(16):
                        p, hh = h // 2, 64 * (h % 2)
                        i = tb.matmul(
                            out=PS[hh:hh + 64, 64 * p:64 * (p + 1)],
                            lhsT=KTP[0:64, 128 * p + hh:128 * p + hh + 64],
                            rhs=VSB[0:64, 128 * p + hh:128 * p + hh + 64],
                            start=True, stop=True)
                    for h in range(16):
                        p, hh = h // 2, 64 * (h % 2)
                        i = tb.matmul(
                            out=PDN[hh:hh + 64, 64 * p:64 * (p + 1)],
                            lhsT=VSB[0:64, 128 * p + hh:128 * p + hh + 64],
                            rhs=KTP[0:64, 128 * p + hh:128 * p + hh + 64],
                            start=True, stop=True)
                    i.then_inc(st.h, 1)
                st.v = kv_t
                r1_t = sv.v + 1
                @blk.vector
                def _(v, kv_t=kv_t):
                    v.wait_ge(st.h, kv_t)
                    i = v.scalar_tensor_tensor(out=R1SB[:], in0=EYBT[:],
                                               scalar=7.0, in1=PS[:, 0:512],
                                               op0=ALU.mult, op1=ALU.subtract)
                    fs.v += 1
                    i.then_inc(fs.h, 1)
                    i = v.scalar_tensor_tensor(out=R1TP[:], in0=EYBT[:],
                                               scalar=7.0, in1=PDN[:, 0:512],
                                               op0=ALU.mult, op1=ALU.subtract)
                    fd.v += 1
                    i.then_inc(fd.h, 1)
                    v.tensor_scalar(out=MRKV[:], in0=MRKV[:], scalar1=1.0, scalar2=None, op0=ALU.mult).then_inc(sv.h, 1)
                ps_u[0] = fs.v
                sv.v = r1_t
                t2_t = st.v + 1
                @blk.tensor
                def _(tb, r1_t=r1_t):
                    tb.wait_ge(sv.h, r1_t)
                    i = None
                    for h in range(16):
                        p, hh = h // 2, 64 * (h % 2)
                        i = tb.matmul(
                            out=PDN[hh:hh + 64, 512 + 64 * p:512 + 64 * (p + 1)],
                            lhsT=R1TP[hh:hh + 64, 64 * p:64 * (p + 1)],
                            rhs=R1SB[hh:hh + 64, 64 * p:64 * (p + 1)],
                            start=True, stop=True)
                    i.then_inc(st.h, 1)
                st.v = t2_t
                r2_t = sv.v + 1
                @blk.vector
                def _(v, t2_t=t2_t):
                    v.wait_ge(st.h, t2_t)
                    v.scalar_tensor_tensor(out=WSB[:], in0=EYBT[:], scalar=15.0,
                                           in1=PDN[:, 512:1024], op0=ALU.mult,
                                           op1=ALU.add)
                    v.scalar_tensor_tensor(out=R2SB[:], in0=R1SB[:], scalar=-7.0,
                                           in1=WSB[:], op0=ALU.mult, op1=ALU.add)
                    v.tensor_scalar(out=MRKV[:], in0=MRKV[:], scalar1=1.0, scalar2=None, op0=ALU.mult).then_inc(sv.h, 1)
                sv.v = r2_t
                t3_t = st.v + 1
                @blk.tensor
                def _(tb, r2_t=r2_t, psu=ps_u[0]):
                    tb.wait_ge(sv.h, r2_t)
                    tb.wait_ge(fs.h, psu)
                    i = None
                    for h in range(16):
                        p, hh = h // 2, 64 * (h % 2)
                        i = tb.matmul(
                            out=PS[hh:hh + 64, 64 * p:64 * (p + 1)],
                            lhsT=R1TP[hh:hh + 64, 64 * p:64 * (p + 1)],
                            rhs=R2SB[hh:hh + 64, 64 * p:64 * (p + 1)],
                            start=True, stop=True)
                    i.then_inc(st.h, 1)
                st.v = t3_t
                r3_t = sv.v + 1
                @blk.vector
                def _(v, t3_t=t3_t):
                    v.wait_ge(st.h, t3_t)
                    v.scalar_tensor_tensor(out=WSB[:], in0=EYBT[:], scalar=13.0,
                                           in1=PS[:, 0:512], op0=ALU.mult,
                                           op1=ALU.add)
                    i = v.scalar_tensor_tensor(out=R3SB[:], in0=R2SB[:],
                                               scalar=-7.0, in1=WSB[:],
                                               op0=ALU.mult, op1=ALU.add)
                    fs.v += 1
                    i.then_inc(fs.h, 1)
                    v.tensor_scalar(out=MRKV[:], in0=MRKV[:], scalar1=1.0, scalar2=None, op0=ALU.mult).then_inc(sv.h, 1)
                ps_u[0] = fs.v
                sv.v = r3_t
                vp_t = st.v + 1
                @blk.tensor
                def _(tb, r3_t=r3_t, psu=ps_u[0]):
                    tb.wait_ge(sv.h, r3_t)
                    tb.wait_ge(fs.h, psu)
                    i = None
                    for h in range(16):
                        p, hh = h // 2, 64 * (h % 2)
                        i = tb.matmul(
                            out=PS[hh:hh + 64, 64 * p:64 * (p + 1)],
                            lhsT=VTS[hh:hh + 64, 64 * p:64 * (p + 1)],
                            rhs=R3SB[hh:hh + 64, 64 * p:64 * (p + 1)],
                            start=True, stop=True)
                    for h in range(16):
                        p, hh = h // 2, 64 * (h % 2)
                        i = tb.matmul(
                            out=PDN[hh:hh + 64, 64 * p:64 * (p + 1)],
                            lhsT=R3SB[hh:hh + 64, 64 * p:64 * (p + 1)],
                            rhs=VTS[hh:hh + 64, 64 * p:64 * (p + 1)],
                            start=True, stop=True)
                    i.then_inc(st.h, 1)
                st.v = vp_t
                ve_t = sv.v + 1
                @blk.vector
                def _(v, vp_t=vp_t):
                    v.wait_ge(st.h, vp_t)
                    v.tensor_scalar(
                        out=VSB[0:64, :].rearrange("p (pr c) -> p pr c", c=128)[:, :, 0:64],
                        in0=PS[0:64, :].rearrange("p (pr c) -> p pr c", c=64),
                        scalar1=0.25, scalar2=None, op0=ALU.mult)
                    i = v.tensor_scalar(
                        out=VSB[0:64, :].rearrange("p (pr c) -> p pr c", c=128)[:, :, 64:128],
                        in0=PS[64:128, :].rearrange("p (pr c) -> p pr c", c=64),
                        scalar1=0.25, scalar2=None, op0=ALU.mult)
                    fs.v += 1
                    i.then_inc(fs.h, 1)
                    i = v.tensor_scalar(out=VTS[:], in0=PDN[:, 0:512], scalar1=0.25,
                                        scalar2=None, op0=ALU.mult)
                    fd.v += 1
                    i.then_inc(fd.h, 1)
                    v.tensor_scalar(out=MRKV[:], in0=MRKV[:], scalar1=1.0, scalar2=None, op0=ALU.mult).then_inc(sv.h, 1)
                ps_u[0] = fs.v
                sv.v = ve_t
            vts_ft = ft.v
            k3_t = sv.v + 1
            @blk.vector
            def _(v):
                v.drain()
                v.reciprocal(out=RCPS[:], in_=PCMP[:, 512:520])
                v.drain()
                v.tensor_tensor(out=K3V[:].rearrange("p (pr c) -> p pr c", c=64),
                                in0=PCMP[:, 0:512].rearrange("p (pr c) -> p pr c", c=64),
                                in1=RCPS[:].unsqueeze(2).broadcast_to([128, 8, 64]),
                                op=ALU.mult).then_inc(sv.h, 1)
            sv.v = k3_t
            tm_t = st.v + 1
            @blk.tensor
            def _(tb, k3_t=k3_t, vts_ft=vts_ft, psu=ps_u[0]):
                tb.wait_ge(sv.h, k3_t)
                tb.wait_ge(ft.h, vts_ft)
                tb.wait_ge(fs.h, psu)
                i = None
                for h in range(16):
                    p, hh = h // 2, 64 * (h % 2)
                    i = tb.matmul(
                        out=PS[hh:hh + 64, 64 * p:64 * (p + 1)],
                        lhsT=VTS[hh:hh + 64, 64 * p:64 * (p + 1)],
                        rhs=K3V[hh:hh + 64, 64 * p:64 * (p + 1)],
                        start=True, stop=True)
                i.then_inc(st.h, 1)
            st.v = tm_t
            te_t = sv.v + 1
            @blk.vector
            def _(v, tm_t=tm_t):
                v.wait_ge(st.h, tm_t)
                i = v.tensor_copy(TSB[:], PS[:, 0:512])
                fs.v += 1
                i.then_inc(fs.h, 1)
                v.tensor_scalar(out=MRKV[:], in0=MRKV[:], scalar1=1.0,
                                scalar2=None, op0=ALU.mult).then_inc(sv.h, 1)
            sv.v = te_t
            ps_u[0] = fs.v
            if upto == "ns":
                @blk.vector
                def _(v, te_t=te_t):
                    v.wait_ge(sv.h, te_t)
                    v.tensor_scalar(out=MRKV[:], in0=MRKV[:], scalar1=1.0, scalar2=None, op0=ALU.mult).then_inc(sv.h, 1)
                sv.v += 1
                return sv.v
            wproj_ld = sg.v + 16
            @blk.gpsimd
            def _(gp, b1_st=b1_st):
                gp.wait_ge(st.h, b1_st)
                gp.dma_start(
                    out=WA[:, 0:8192].rearrange("p (k c) -> p k c", c=D),
                    in_=WPROJ.rearrange("(k p) c -> p k c", p=128)).then_inc(sg.h, 16)
            sg.v = wproj_ld

            last_add = sv.v
            pt_u = [ft.v]
            for g in range(4):
                def b2_k1(p):
                    k1_t = st.v + 1
                    @blk.tensor
                    def _(tb, p=p, g=g, te_t=te_t, psu=ps_u[0]):
                        tb.wait_ge(sv.h, te_t)
                        tb.wait_ge(fs.h, psu)
                        i = None
                        for hh in (0, 64):
                            i = tb.matmul(
                                out=PS[hh:hh + 64, 0:512],
                                lhsT=KLMF[hh:hh + 64, 64 * p:64 * (p + 1)],
                                rhs=FMA[hh:hh + 64, 2048 * p + 512 * g:2048 * p + 512 * (g + 1)],
                                start=True, stop=True)
                        i.then_inc(st.h, 1)
                    st.v = k1_t
                    return k1_t

                def b2_e1_sv(p, rdmt):
                    e_t = ss.v + 1
                    @blk.scalar
                    def _(s, rdmt=rdmt):
                        s.wait_ge(st.h, rdmt)
                        i = s.activation(out=E1P[:], in_=PS[:, 0:512],
                                         func=AF.Exp, scale=0.125)
                        fs.v += 1
                        i.then_inc(fs.h, 1)
                        s.activation(out=MRKS[:, 0:1], in_=MRKS[:, 1:2],
                                     func=AF.Exp).then_inc(ss.h, 1)
                    ss.v = e_t
                    ps_u[0] = fs.v
                    sv_t = st.v + 1
                    ptw = pt_u[0]
                    @blk.tensor
                    def _(tb, p=p, e_t=e_t, ptw=ptw):
                        tb.wait_ge(ss.h, e_t)
                        tb.wait_ge(ft.h, ptw)
                        i = None
                        for hh in (0, 64):
                            i = tb.matmul(
                                out=PDN[hh:hh + 64, 512:1024],
                                lhsT=TSB[hh:hh + 64, 64 * p:64 * (p + 1)],
                                rhs=E1P[hh:hh + 64, :],
                                start=True, stop=True)
                        i.then_inc(st.h, 1)
                    st.v = sv_t
                    return e_t, sv_t

                def b2_tail(p, e_t, sv_t):
                    dm_t = st.v + 1
                    fdw = fd.v
                    @blk.tensor
                    def _(tb, e_t=e_t, fdw=fdw):
                        tb.wait_ge(ss.h, e_t)
                        tb.wait_ge(fd.h, fdw)
                        tb.matmul(out=PDN[0:2, 0:512],
                                  lhsT=O2T[:], rhs=E1P[:],
                                  start=True, stop=True).then_inc(st.h, 1)
                    st.v = dm_t
                    rc_t = ss.v + 1
                    @blk.scalar
                    def _(s, dm_t=dm_t):
                        s.wait_ge(st.h, dm_t)
                        i = s.activation(out=DSB[:], in_=PDN[0:2, 0:512],
                                         func=AF.Ln)
                        fd.v += 1
                        i.then_inc(fd.h, 1)
                        s.activation(out=RCDB[:], in_=DSB[:], func=AF.Exp,
                                     scale=-1.0).then_inc(ss.h, 1)
                    ss.v = rc_t
                    rm_t = st.v + 1
                    fdw = fd.v
                    @blk.tensor
                    def _(tb, rc_t=rc_t, fdw=fdw):
                        tb.wait_ge(ss.h, rc_t)
                        tb.wait_ge(fd.h, fdw)
                        tb.matmul(out=PDN[:, 0:512], lhsT=E2TTT[:],
                                  rhs=RCDB[:],
                                  start=True, stop=True).then_inc(st.h, 1)
                    st.v = rm_t
                    re_t = sv.v + 1
                    @blk.vector
                    def _(v, rm_t=rm_t):
                        v.wait_ge(st.h, rm_t)
                        i = v.tensor_copy(RSB[:], PDN[:, 0:512])
                        fd.v += 1
                        i.then_inc(fd.h, 1)
                        v.tensor_scalar(out=MRKV[:], in0=MRKV[:], scalar1=1.0,
                                        scalar2=None, op0=ALU.mult).then_inc(sv.h, 1)
                    sv.v = re_t
                    se_t = sv.v + 1
                    @blk.vector
                    def _(v, p=p, sv_t=sv_t):
                        v.wait_ge(st.h, sv_t)
                        i = v.tensor_tensor(out=SVF[:, 512 * p:512 * (p + 1)],
                                            in0=PDN[:, 512:1024], in1=RSB[:],
                                            op=ALU.mult)
                        pt_u[0] = ft.v + 1
                        ft.v += 1
                        i.then_inc(ft.h, 1)
                        v.tensor_scalar(out=MRKV[:], in0=MRKV[:], scalar1=1.0,
                                        scalar2=None, op0=ALU.mult).then_inc(sv.h, 1)
                    sv.v = se_t
                    return dm_t

                prevq = None
                for p in range(9):
                    if p < 8:
                        k1_t = b2_k1(p)
                        rdmt = k1_t
                    if prevq is not None:
                        rdmt = b2_tail(prevq[0], prevq[1][0], prevq[1][1])
                    if p < 8:
                        es = b2_e1_sv(p, rdmt)
                        prevq = (p, es)
                    else:
                        prevq = None
                svf_sv = sv.v
                for tt_i in range(4):
                    t = 4 * g + tt_i
                    pj_t = st.v + 1
                    fgw = fg.v
                    @blk.tensor
                    def _(tb, tt_i=tt_i, svf_sv=svf_sv, fgw=fgw,
                          wproj_ld=wproj_ld):
                        tb.wait_ge(sv.h, svf_sv)
                        tb.wait_ge(sg.h, wproj_ld)
                        tb.wait_ge(fg.h, fgw)
                        i = None
                        for c in range(8):
                            lhs = SVF[:, 512 * c + 128 * tt_i:512 * c + 128 * (tt_i + 1)]
                            for (a, bnd) in ((0, 512), (512, 1024)):
                                i = tb.matmul(out=PGA[:, a:bnd], lhsT=lhs,
                                              rhs=WA[:, 1024 * c + a:1024 * c + bnd],
                                              start=(c == 0), stop=(c == 7))
                        i.then_inc(st.h, 1)
                    st.v = pj_t
                    ad_t = sv.v + 1
                    @blk.vector
                    def _(v, t=t, pj_t=pj_t):
                        v.wait_ge(st.h, pj_t)
                        strm = STREAM[:, 1024 * t:1024 * (t + 1)]
                        i = v.tensor_tensor(out=strm, in0=PGA[:, 0:1024], in1=strm,
                                            op=ALU.add)
                        fg.v += 1
                        i.then_inc(fg.h, 1)
                        v.tensor_scalar(out=MRKV[:], in0=MRKV[:], scalar1=1.0, scalar2=None, op0=ALU.mult).then_inc(sv.h, 1)
                    sv.v = ad_t
                    last_add = sv.v
            return last_add

        xq1_ft = emit_ln_xq([(sx, x_ld), (sd, 128)])
        ffn1_st = emit_ffn(xq1_ft, 0, None)

        def emit_dump():
            for t in range(NT):
                prev_o = so.v
                c_t = sv.v + 1
                @blk.vector
                def _(v, t=t, prev_o=prev_o, base=sv.v):
                    v.wait_ge(sv.h, base)
                    v.wait_ge(so.h, prev_o)
                    v.tensor_copy(XOUT[:], STREAM[:, 1024 * t:1024 * (t + 1)]) \
                        .then_inc(sv.h, 1)
                sv.v = c_t
                o_t = so.v + 16
                @blk.sync
                def _(sync, t=t, c_t=c_t):
                    sync.wait_ge(sv.h, c_t)
                    sync.dma_start(out=OUT[128 * t:128 * (t + 1), :],
                                   in_=XOUT[:]).then_inc(so.h, 16)
                so.v = o_t
            fin_o = so.v
            @blk.sync
            def _(sync, fin_o=fin_o):
                sync.wait_ge(so.h, fin_o)

        if stage == "ffn1":
            emit_dump()
            return nc
        if stage.startswith("attn"):
            upto = {"attn": "all", "attn_dbg": "all", "attn_prep": "prep",
                    "attn_b1": "b1",
                    "attn_ns": "ns", "attn_p1": "p1", "attn_p2": "p2",
                    "attn_p3": "p3", "attn_p4": "p4", "attn_p3a": "p3a",
                    "attn_p3b": "p3b"}[stage]
            fin_sv = emit_attn(sv.v, upto)
            if stage == "attn_dbg":
                slots = [(VTMS, 1024), (QLMF, 512), (KLMF, 512), (K2N, 512),
                         (K2ND, 512), (VSB, 1024), (K3V, 512), (TSB, 512),
                         (SLOC, 8), (PCMP, 520), (RD128, 16), (R1TP, 512),
                         (KTP, 1024), (R1SB, 512),
                         (R2SB, 512), (R3SB, 512)]
                row = 0
                for src_ap, cols in slots:
                    pc = src_ap.shape[0]
                    c_t = sv.v + 1
                    prev_o = so.v
                    @blk.vector
                    def _(v, src_ap=src_ap, cols=cols, pc=pc, prev_o=prev_o,
                          base=sv.v):
                        v.wait_ge(sv.h, base)
                        v.wait_ge(so.h, prev_o)
                        nc_ = min(cols, 1024)
                        v.tensor_scalar(out=XOUT[0:pc, 0:nc_],
                                        in0=src_ap[:, 0:nc_], scalar1=1.0,
                                        scalar2=None, op0=ALU.mult)
                        if cols > 1024:
                            v.tensor_scalar(out=XOUT[0:pc, 0:1024],
                                            in0=src_ap[:, 0:1024], scalar1=1.0,
                                            scalar2=None, op0=ALU.mult)
                        v.tensor_scalar(out=MRKV[:], in0=MRKV[:], scalar1=1.0, scalar2=None, op0=ALU.mult).then_inc(sv.h, 1)
                    sv.v = c_t
                    o_t = so.v + 16
                    @blk.sync
                    def _(sync, row=row, c_t=c_t):
                        sync.wait_ge(sv.h, c_t)
                        sync.dma_start(out=OUT[128 * row:128 * (row + 1), :],
                                       in_=XOUT[:]).then_inc(so.h, 16)
                    so.v = o_t
                    row += 1
                fin_o = so.v
                @blk.sync
                def _(sync, fin_o=fin_o):
                    sync.wait_ge(so.h, fin_o)
                return nc
            emit_dump()
            return nc
        attn_sv = emit_attn(sv.v)
        xq2_ft = emit_ln_xq([(sv, attn_sv)])
        emit_ffn(xq2_ft, st.v, OUT)
        fin_o = so.v
        @blk.sync
        def _(sync, fin_o=fin_o):
            sync.wait_ge(so.h, fin_o)
    return nc


def make_consts():
    eye = np.eye(128).astype(bfl)
    eyeb = np.tile(np.vstack([np.eye(64), np.eye(64)]).astype(np.float32), (1, 8))
    ones1 = np.ones((128, 1), bfl)
    ones2 = np.zeros((128, 2), np.float32)
    ones2[:64, 0] = 1.0
    ones2[64:, 1] = 1.0
    e2t = np.zeros((2, 128), np.float32)
    e2t[0, :64] = 1.0
    e2t[1, 64:] = 1.0
    c80 = np.full((128, 1), 0x8000, np.uint16)
    onesr = np.ones((1, 128), bfl)
    etbl = np.zeros((128, 256), np.float32)
    for t in range(16):
        etbl[:, 16 * t + t] = 1.0
    return dict(eye=eye, eyeb=eyeb, ones1=ones1, ones2=ones2.astype(bfl),
                e2t=e2t.astype(bfl), c80=c80, onesr=onesr,
                etbl=etbl.astype(bfl))


def make_inmaps(L0_attn_x, L0_attn_w_qkv, L0_attn_w_proj, L0_ffn_w_gate,
                L0_ffn_w_up, L0_ffn_w_down):
    x = np.asarray(L0_attn_x, np.float32).reshape(2, 8192, 1024)
    consts = make_consts()
    wqkv = np.ascontiguousarray(np.asarray(L0_attn_w_qkv, np.float32))
    wproj = np.ascontiguousarray(np.asarray(L0_attn_w_proj, np.float32))
    wg = np.asarray(L0_ffn_w_gate, np.float32)
    wu = np.asarray(L0_ffn_w_up, np.float32)
    wd = np.asarray(L0_ffn_w_down, np.float32)
    in_maps = []
    for c in range(NCORES):
        b, t0 = c // 4, 2048 * (c % 4)
        m = dict(consts)
        m["x"] = np.ascontiguousarray(x[b, t0:t0 + 2048])
        m["wqkv"] = wqkv
        m["wproj"] = wproj
        m["wg_sl"] = np.ascontiguousarray(wg[:, 128 * c:128 * (c + 1)])
        m["wu_sl"] = np.ascontiguousarray(wu[:, 128 * c:128 * (c + 1)])
        m["wd_sl"] = np.ascontiguousarray(wd[:, 352 * c:352 * (c + 1)])
        in_maps.append(m)
    return in_maps


def kernel(L0_attn_x, L0_attn_w_qkv, L0_attn_w_proj, L0_ffn_w_gate,
           L0_ffn_w_up, L0_ffn_w_down):
    in_maps = make_inmaps(L0_attn_x, L0_attn_w_qkv, L0_attn_w_proj,
                          L0_ffn_w_gate, L0_ffn_w_up, L0_ffn_w_down)
    nc = build("full")
    res = run_bass_kernel_spmd(nc, in_maps, core_ids=list(range(NCORES)))
    outs = [np.asarray(res.results[c]["out"]) for c in range(NCORES)]
    full = np.stack([np.concatenate(outs[0:4], axis=0),
                     np.concatenate(outs[4:8], axis=0)]).astype(np.float32)
    return full
